# version 05
# speedup vs baseline: 1.3181x; 1.0061x over previous
import math
from collections import defaultdict
from itertools import product
from contextlib import ExitStack

import numpy as np

import concourse.bass as bass
import concourse.mybir as mybir
from concourse.bass_utils import run_bass_kernel_spmd

F32 = mybir.dt.float32
BF16 = mybir.dt.bfloat16
F8 = mybir.dt.float8e4
F16 = mybir.dt.float16
I8 = mybir.dt.int8
I16 = mybir.dt.int16
ACT = mybir.ActivationFunctionType
ALU = mybir.AluOpType
AX = mybir.AxisListType

PI = math.pi
CLP = 3.1415925

B, S, D = 16, 2048, 1024
H, HD = 8, 128
NCORE = 8
BPC = B // NCORE
T = BPC * S
NF = D // 128
TT = 512
NTT = T // TT
NQB = S // TT
CH = 128
NCH = S // CH
RINV = 128.0 ** (-0.5)

ENGINES = ("sync", "gpsimd", "tensor", "vector", "scalar")
P12_NOROT = False
P6_DBG = False
P6_NOROT = False
P6_ROTMODE = 0
P7_MODE = 0
P7_DBG = False



def _host_tables():
    import jax
    import jax.numpy as jnp
    cpu = jax.devices("cpu")[0]
    out = {}
    with jax.default_device(cpu):
        d = 128
        scale_vec = (jnp.arange(0, d, 2, dtype=jnp.float32) + 0.4 * d) / (1.4 * d)
        pos = jnp.arange(S, dtype=jnp.float32)
        scale = scale_vec[None, :] ** (pos[:, None] / 512.0)
        half = d // 2
        inv_freq = 1.0 / 10000.0 ** (jnp.arange(half, dtype=jnp.float32) / half)
        ang = pos[:, None] * inv_freq[None, :]
        sin, cos = jnp.sin(ang), jnp.cos(ang)
        cq = np.asarray((cos * scale).T)
        sq = np.asarray((sin * scale).T)
        inv_scale = 1.0 / scale
        ck = np.asarray((cos * inv_scale).T)
        sk = np.asarray((sin * inv_scale).T)
        xp = np.empty((4, 128, S), np.float32)
        for i, (c, s_) in enumerate(((cq, sq), (ck, sk))):
            xp[2 * i, 0:64] = c
            xp[2 * i, 64:128] = c
            xp[2 * i + 1, 0:64] = -s_
            xp[2 * i + 1, 64:128] = s_
        out["tbl_xpos"] = xp

        gam = 1.0 - jnp.exp(
            jnp.linspace(math.log(1.0 / 32), math.log(1.0 / 512), H)
        ).astype(jnp.float32)
        kl = jnp.arange(CH)[:, None]
        x = jnp.arange(-384, S)[None, :]
        diff = x - kl
        mask = diff >= 0
        dmat = []
        for h in range(H):
            dh = jnp.where(
                mask, gam[h] ** jnp.where(mask, diff, 0).astype(jnp.float32), 0.0
            )
            dmat.append(np.asarray(dh, np.float32))
        out["tbl_dm"] = np.stack(dmat)

    i = np.arange(128)[:, None]
    j = np.arange(TT)[None, :]
    cm = np.stack(
        [(128 * o + i > j).astype(np.float32) * np.float32(1e30) for o in range(4)]
    )
    out["tbl_cmsk"] = cm

    dims = np.arange(32)
    invf = np.power(10000.0, -dims / 32.0)
    rad = np.arange(S)[None, :] * invf[:, None]
    ks = np.rint(rad / (2 * np.pi))
    kc = np.rint((rad + np.pi / 2) / (2 * np.pi))
    for name, kk, base in (("ks", ks, rad), ("kc", kc, rad + np.pi / 2)):
        r = base - kk * (2 * np.pi)
        assert np.abs(r).max() < np.pi + 1e-9, name
    out["tbl_ropek"] = np.stack([ks, kc]).astype(np.float32)
    out["tbl_pos32"] = np.broadcast_to(
        np.arange(S, dtype=np.float32)[None, :], (32, S)
    ).copy()
    out["tbl_iota32"] = np.arange(32, dtype=np.float32)[:, None].copy()
    out["tbl_ident"] = np.eye(128, dtype=np.float32)
    return out


def _prep_core_inputs(ins, core, tables):
    m = {}
    x = np.asarray(ins["L0_attn_X"][2 * core:2 * core + 2])
    xt = np.transpose(x.reshape(T, D).reshape(T, NF, 128), (1, 2, 0))
    m["x_in"] = np.ascontiguousarray(xt)

    perm = np.concatenate([np.arange(0, 128, 2), np.arange(1, 128, 2)])

    def fm_weight(w):
        return np.ascontiguousarray(w.reshape(NF, 128, -1))

    wq = np.asarray(ins["L0_attn_W_Q"])[:, :, perm]
    wq = np.transpose(wq, (1, 0, 2)).reshape(D, D)
    m["wq0"] = fm_weight(wq)
    wk = np.asarray(ins["L0_attn_W_K"])[:, :, perm]
    wk = np.transpose(wk, (1, 0, 2)).reshape(D, D)
    m["wk0"] = fm_weight(wk)
    wv = np.transpose(np.asarray(ins["L0_attn_W_V"]), (1, 0, 2)).reshape(D, D)
    m["wv0"] = fm_weight(wv)
    m["wg0"] = fm_weight(np.asarray(ins["L0_attn_W_G"]))
    m["wo0"] = fm_weight(np.asarray(ins["L0_attn_W_O"]))
    m["wr0"] = fm_weight(np.ascontiguousarray(np.asarray(ins["L0_ffn_W_r"]).T))
    m["wk0f"] = fm_weight(np.ascontiguousarray(np.asarray(ins["L0_ffn_W_k"]).T))
    m["wv0f"] = fm_weight(np.ascontiguousarray(np.asarray(ins["L0_ffn_W_v"]).T))
    m["gnw_row"] = np.asarray(ins["L0_attn_gn_weight"]).reshape(1, D)
    m["gnb_row"] = np.asarray(ins["L0_attn_gn_bias"]).reshape(1, D)
    m["mur_col"] = np.asarray(ins["L0_ffn_mu_r"]).reshape(NF, 128, 1)
    m["muk_col"] = np.asarray(ins["L0_ffn_mu_k"]).reshape(NF, 128, 1)

    m["wq1"] = fm_weight(np.asarray(ins["L1_attn_wq"]).reshape(D, D))
    m["wk1"] = fm_weight(np.asarray(ins["L1_attn_wk"]).reshape(D, D))
    m["wv1"] = fm_weight(np.asarray(ins["L1_attn_wv"]).reshape(D, D))
    m["wo1"] = fm_weight(np.asarray(ins["L1_attn_wo"]).reshape(D, D))
    m["wf1"] = fm_weight(np.ascontiguousarray(np.asarray(ins["L1_ffn_weight"]).T))
    m["bias1_col"] = np.asarray(ins["L1_ffn_bias"]).reshape(NF, 128, 1)
    m.update(tables)
    return {k: np.ascontiguousarray(np.asarray(v, np.float32)) for k, v in m.items()}



class Sem:
    def __init__(self, handle):
        self.h = handle
        self.v = 0

    def inc(self, n=1):
        self.v += n
        return self.v


class EP:

    def __init__(self, cx):
        self.cx = cx
        self.L = defaultdict(list)
        self._lastw = {}

    def w(self, eng, sem, val=None):
        if val is None:
            val = sem.v
        key = (eng, id(sem))
        if self._lastw.get(key, -1) >= val:
            return
        self._lastw[key] = val
        self.L[eng].append(("w", sem.h, val))

    def op(self, eng, fn, inc=None, n=1):
        if inc is not None:
            inc.inc(n)
        self.L[eng].append(("o", fn, None if inc is None else (inc.h, n)))

    def run(self, name):
        cx = self.cx

        def mk(items):
            def body(eng):
                for it in items:
                    if it[0] == "w":
                        eng.wait_ge(it[1], it[2])
                    else:
                        r = it[1](eng)
                        if it[2] is not None:
                            r.then_inc(it[2][0], it[2][1])
            return body

        with cx.nc.Block(name) as blk:
            for e in ENGINES:
                if self.L[e]:
                    getattr(blk, e)(mk(self.L[e]))


class Ctx:
    def __init__(self, nc):
        self.nc = nc
        self.d = {}
        self.sems = {}
        self._stack = None

    def sem(self, name):
        return self.sems[name]


def _dma(dst, src):
    return lambda e: e.dma_start(out=dst, in_=src)


def _emit_weight_pair(ep, cx, dram, stage, hi, lo, nchunk=NF, width=D):
    LD, SE, VE = cx.sem("LD"), cx.sem("SE"), cx.sem("VE")
    ep.w("sync", VE)
    for fc in range(nchunk):
        ep.op("sync", _dma(stage[fc][:, 0:width], dram[fc, :, 0:width]), LD, 16)
    ep.w("scalar", LD)
    for fc in range(nchunk):
        ep.op("scalar",
              (lambda e, a=hi[fc], b=stage[fc]: e.copy(out=a[:, 0:width], in_=b[:, 0:width])),
              SE)
    ep.w("vector", SE)
    for fc in range(nchunk):
        ep.op("vector",
              (lambda e, a=lo[fc], b=stage[fc], c=hi[fc]: e.tensor_tensor(
                  out=a[:, 0:width], in0=b[:, 0:width], in1=c[:, 0:width],
                  op=ALU.subtract)),
              VE)


def _mm3(ep, cx, ps, wh, wl, ah, al, start, stop, inc=None):
    TE = cx.sem("TE")
    ep.op("tensor", (lambda e, o=ps, l=wh, r=ah, s=start: e.matmul(
        out=o, lhsT=l, rhs=r, start=s, stop=False)))
    ep.op("tensor", (lambda e, o=ps, l=wl, r=ah: e.matmul(
        out=o, lhsT=l, rhs=r, start=False, stop=False)))
    ep.op("tensor", (lambda e, o=ps, l=wh, r=al, s=stop: e.matmul(
        out=o, lhsT=l, rhs=r, start=False, stop=s)), inc if stop else None)



def _p0_norm0(cx):
    nc = cx.nc
    d = cx.d
    LD, ST, TE, VE, SE, FR = (cx.sem(n) for n in
                              ("LD", "ST", "TE", "VE", "SE", "FR0"))
    with (
        nc.sbuf_tensor("p0_xin", [128, NF * TT * 2], F32) as xin_all,
        nc.sbuf_tensor("p0_xsq", [128, TT * 2], F32) as xsq,
        nc.sbuf_tensor("p0_ones", [128, 1], F32) as ones_col,
        nc.sbuf_tensor("p0_onesr", [1, 128], F32) as ones_row,
        nc.sbuf_tensor("p0_rows", [1, TT * 4], F32) as rows,
        nc.sbuf_tensor("p0_tmp", [128, TT * 2], F32) as tmp,
        nc.sbuf_tensor("p0_tf", [128, TT * 2], F32) as tfull,
        nc.sbuf_tensor("p0_oh", [128, TT * 2], BF16) as ohi,
        nc.sbuf_tensor("p0_ol", [128, TT * 2], BF16) as olo,
        nc.psum_tensor("p0_psA", [128, TT], F32) as psA,
        nc.psum_tensor("p0_psB", [128, TT], F32) as psB,
        nc.psum_tensor("p0_psM", [128, TT], F32) as psM,
        nc.psum_tensor("p0_psS", [128, TT], F32) as psS,
    ):
        ep = EP(cx)
        ep.op("vector", lambda e: e.memset(ones_col[:], 1.0), VE)
        ep.op("vector", lambda e: e.memset(ones_row[:], 1.0), VE)
        xin = [
            [xin_all[:, (s * NF + fc) * TT:(s * NF + fc + 1) * TT]
             for fc in range(NF)] for s in range(2)
        ]
        mrow = rows[0:1, 0:TT]
        m2row = rows[0:1, TT:2 * TT]
        erow = rows[0:1, 2 * TT:3 * TT]
        srow = rows[0:1, 3 * TT:4 * TT]
        for tt in range(NTT):
            sl = tt % 2
            tok = slice(tt * TT, (tt + 1) * TT)
            if tt >= 2:
                ep.w("sync", FR, tt - 1)
            for fc in range(NF):
                ep.op("sync", _dma(xin[sl][fc], d["x_in"][fc, :, tok]), LD, 16)
            ep.w("tensor", LD)
            ep.w("tensor", VE)
            ep.w("tensor", SE)
            ep.w("scalar", LD)
            for fc in range(NF):
                ep.op("tensor", (lambda e, o=psA[0:1, :], l=ones_col[:], r=xin[sl][fc],
                                 s=(fc == 0), p=(fc == NF - 1): e.matmul(
                    out=o, lhsT=l, rhs=r, start=s, stop=p)))
                xq = xsq[:, (fc % 2) * TT:(fc % 2 + 1) * TT]
                if fc >= 2:
                    ep.w("scalar", TE, TE.v)
                ep.op("scalar", (lambda e, o=xq, i=xin[sl][fc]: e.activation(
                    out=o, in_=i, func=ACT.Square)), SE)
                ep.w("tensor", SE)
                ep.op("tensor", (lambda e, o=psB[0:1, :], l=ones_col[:], r=xq,
                                 s=(fc == 0), p=(fc == NF - 1): e.matmul(
                    out=o, lhsT=l, rhs=r, start=s, stop=p)), TE)
            ep.w("scalar", TE)
            ep.op("scalar", (lambda e: e.activation(out=mrow, in_=psA[0:1, :],
                                                    func=ACT.Copy, scale=1.0 / 1024)))
            ep.op("scalar", (lambda e: e.activation(out=m2row, in_=mrow,
                                                    func=ACT.Square)))
            ep.op("scalar", (lambda e: e.activation(out=erow, in_=psB[0:1, :],
                                                    func=ACT.Copy, scale=1.0 / 1024)), SE)
            ep.w("vector", SE)
            ep.op("vector", (lambda e: e.tensor_tensor(out=erow, in0=erow,
                                                       in1=m2row, op=ALU.subtract)))
            ep.op("vector", (lambda e: e.tensor_scalar(out=erow, in0=erow,
                                                       scalar1=1e-5, scalar2=None,
                                                       op0=ALU.add)), VE)
            ep.w("scalar", VE)
            ep.op("scalar", (lambda e: e.activation(out=m2row, in_=erow,
                                                    func=ACT.Sqrt)), SE)
            ep.w("vector", SE)
            ep.op("vector", (lambda e: e.reciprocal(out=srow, in_=m2row)), VE)
            ep.w("tensor", VE)
            ep.w("tensor", SE)
            ep.op("tensor", (lambda e: e.matmul(out=psM[:], lhsT=ones_row[:],
                                                rhs=mrow, start=True, stop=True)))
            ep.op("tensor", (lambda e: e.matmul(out=psS[:], lhsT=ones_row[:],
                                                rhs=srow, start=True, stop=True)), TE)
            ep.w("vector", TE)
            for fc in range(NF):
                tp = tmp[:, (fc % 2) * TT:(fc % 2 + 1) * TT]
                tf = tfull[:, (fc % 2) * TT:(fc % 2 + 1) * TT]
                oh = ohi[:, (fc % 2) * TT:(fc % 2 + 1) * TT]
                ol = olo[:, (fc % 2) * TT:(fc % 2 + 1) * TT]
                if fc >= 2:
                    ep.w("vector", ST, ST.v)
                ep.op("vector", (lambda e, o=tp, i=xin[sl][fc]: e.tensor_tensor(
                    out=o, in0=i, in1=psM[:], op=ALU.subtract)),
                    FR if fc == NF - 1 else None)
                ep.op("vector", (lambda e, o=tf, i=tp: e.tensor_tensor(
                    out=o, in0=i, in1=psS[:], op=ALU.mult)), VE)
                ep.w("scalar", VE)
                ep.op("scalar", (lambda e, o=oh, i=tf: e.copy(out=o, in_=i)), SE)
                ep.w("vector", SE)
                ep.op("vector", (lambda e, o=ol, a=tf, b=oh: e.tensor_tensor(
                    out=o, in0=a, in1=b, op=ALU.subtract)), VE)
                ep.w("gpsimd", VE)
                ep.op("gpsimd", _dma(d["xn0h"][fc, :, tok], oh), ST, 16)
                ep.op("gpsimd", _dma(d["xn0l"][fc, :, tok], ol), ST, 16)
        ep.w("gpsimd", ST)
        ep.run("p0")


def _p12_qk0(cx):
    nc = cx.nc
    d = cx.d
    LD, LD2, ST, TE, VE, SE = (cx.sem(n) for n in
                               ("LD", "LD2", "ST", "TE", "VE", "SE"))
    HG = 4
    W = HG * TT
    with (
        nc.sbuf_tensor("p12_wqh", [128, NF * D], BF16) as _wqh,
        nc.sbuf_tensor("p12_wql", [128, NF * D], BF16) as _wql,
        nc.sbuf_tensor("p12_wkh", [128, NF * D], BF16) as _wkh,
        nc.sbuf_tensor("p12_wkl", [128, NF * D], BF16) as _wkl,
    ):
        wqh = [_wqh[:, fc * D:(fc + 1) * D] for fc in range(NF)]
        wql = [_wql[:, fc * D:(fc + 1) * D] for fc in range(NF)]
        wkh = [_wkh[:, fc * D:(fc + 1) * D] for fc in range(NF)]
        wkl = [_wkl[:, fc * D:(fc + 1) * D] for fc in range(NF)]
        with nc.sbuf_tensor("p12_stage", [128, NF * D], F32) as _stage:
            stage = [_stage[:, fc * D:(fc + 1) * D] for fc in range(NF)]
            ep = EP(cx)
            _emit_weight_pair(ep, cx, d["wq0"], stage, wqh, wql)
            _emit_weight_pair(ep, cx, d["wk0"], stage, wkh, wkl)
            ep.run("p12w")
        with (
            nc.sbuf_tensor("p12_tbl", [128, 4 * S], F32) as _tbl,
            nc.sbuf_tensor("p12_xn", [128, NF * TT * 4], BF16) as _xn,
            nc.sbuf_tensor("p12_qsb", [128, W], F32) as qsb,
            nc.sbuf_tensor("p12_qsw", [128, W], F32) as qsw,
            nc.sbuf_tensor("p12_t2", [128, TT], F32) as t2s,
            nc.sbuf_tensor("p12_t1", [128, TT], F32) as t1s,
            nc.sbuf_tensor("p12_qf", [128, TT], F32) as qf,
            nc.sbuf_tensor("p12_oh", [128, W], BF16) as ohi,
            nc.sbuf_tensor("p12_ol", [128, W], BF16) as olo,
            nc.psum_tensor("p12_ps0", [128, TT], F32) as ps0,
            nc.psum_tensor("p12_ps1", [128, TT], F32) as ps1,
        ):
            tbl = [_tbl[:, i * S:(i + 1) * S] for i in range(4)]
            ep = EP(cx)
            for i in range(4):
                ep.op("sync", _dma(tbl[i], d["tbl_xpos"][i]), LD, 16)
            xnh = [[_xn[:, ((sl * 2 + 0) * NF + fc) * TT:((sl * 2 + 0) * NF + fc + 1) * TT]
                    for fc in range(NF)] for sl in range(2)]
            xnl = [[_xn[:, ((sl * 2 + 1) * NF + fc) * TT:((sl * 2 + 1) * NF + fc + 1) * TT]
                    for fc in range(NF)] for sl in range(2)]
            psums = [ps0, ps1]
            bank_free = [0, 0]
            rel = []
            st_hist = []
            for s in range(BPC):
                for tt in range(NQB):
                    ti = s * NQB + tt
                    sl = ti % 2
                    tok = slice(s * S + tt * TT, s * S + (tt + 1) * TT)
                    ttok = slice(tt * TT, (tt + 1) * TT)
                    if ti >= 2:
                        ep.w("sync", SE, rel[ti - 2])
                    for fc in range(NF):
                        ep.op("sync", _dma(xnh[sl][fc], d["xn0h"][fc, :, tok]), LD, 16)
                        ep.op("sync", _dma(xnl[sl][fc], d["xn0l"][fc, :, tok]), LD, 16)
                    ld_done = LD.v
                    for p, (wh, wl, qdh, qdl, ctb, stb) in enumerate((
                            (wqh, wql, d["q0h"], d["q0l"], tbl[0], tbl[1]),
                            (wkh, wkl, d["k0h"], d["k0l"], tbl[2], tbl[3]))):
                        for hg in range(2):
                            ep.w("tensor", LD, ld_done)
                            for hh in range(HG):
                                h = hg * HG + hh
                                bk = hh % 2
                                ps = psums[bk]
                                ocol = slice(h * 128, (h + 1) * 128)
                                if bank_free[bk] > 0:
                                    ep.w("tensor", SE, bank_free[bk])
                                for fc in range(NF):
                                    _mm3(ep, cx, ps[:], wh[fc][:, ocol],
                                         wl[fc][:, ocol], xnh[sl][fc], xnl[sl][fc],
                                         start=(fc == 0), stop=(fc == NF - 1),
                                         inc=TE)
                                ep.w("scalar", TE)
                                if hh == 0:
                                    ep.w("scalar", VE)
                                ep.op("scalar", (lambda e, o=qsb[:, hh * TT:(hh + 1) * TT],
                                                 i=ps[:]: e.copy(out=o, in_=i)), SE)
                                bank_free[bk] = SE.v
                            ep.op("scalar", _dma(qsw[0:64, :], qsb[64:128, :]), LD2, 16)
                            ep.op("scalar", _dma(qsw[64:128, :], qsb[0:64, :]), LD2, 16)
                            ep.w("vector", SE)
                            ep.w("vector", LD2)
                            if st_hist:
                                ep.w("vector", ST, st_hist[-1])
                            if P12_NOROT:
                                for hh in range(HG):
                                    hc = slice(hh * TT, (hh + 1) * TT)
                                    ep.op("vector", (lambda e, o=ohi[:, hc], i=qsb[:, hc]:
                                                     e.tensor_copy(out=o, in_=i)))
                                    ep.op("vector", (lambda e, o=olo[:, hc], a=qsb[:, hc],
                                                     b=ohi[:, hc]: e.tensor_tensor(
                                        out=o, in0=a, in1=b, op=ALU.subtract)), VE)
                            else:
                                for hh in range(HG):
                                  h = hg * HG + hh
                                  hc = slice(hh * TT, (hh + 1) * TT)
                                  ep.op("vector", (lambda e, o=t1s[:], i=qsb[:, hc],
                                                   t=ctb[:, ttok]: e.tensor_tensor(
                                      out=o, in0=i, in1=t, op=ALU.mult)))
                                  ep.op("vector", (lambda e, o=t2s[:], i=qsw[:, hc],
                                                   t=stb[:, ttok]: e.tensor_tensor(
                                      out=o, in0=i, in1=t, op=ALU.mult)))
                                  ep.op("vector", (lambda e, o=ohi[:, hc]: e.tensor_tensor(
                                      out=o, in0=t1s[:], in1=t2s[:], op=ALU.add)))
                                  ep.op("vector", (lambda e, o=qf[:]: e.tensor_tensor(
                                      out=o, in0=t1s[:], in1=t2s[:], op=ALU.add)))
                                  ep.op("vector", (lambda e, o=olo[:, hc], a=qf[:],
                                                   b=ohi[:, hc]: e.tensor_tensor(
                                      out=o, in0=a, in1=b, op=ALU.subtract)), VE)
                            ep.w("gpsimd", VE)
                            for hh in range(HG):
                                h = hg * HG + hh
                                hc = slice(hh * TT, (hh + 1) * TT)
                                ep.op("gpsimd", _dma(qdh[s, h, :, ttok], ohi[:, hc]), ST, 16)
                                ep.op("gpsimd", _dma(qdl[s, h, :, ttok], olo[:, hc]), ST, 16)
                            st_hist.append(ST.v)
                    rel.append(SE.v)
            ep.w("gpsimd", ST)
            ep.run("p12")


def _p34_vg0(cx):
    nc = cx.nc
    d = cx.d
    LD, ST, TE, VE, SE = (cx.sem(n) for n in ("LD", "ST", "TE", "VE", "SE"))
    with (
        nc.sbuf_tensor("p34_wvh", [128, NF * D], BF16) as _wvh,
        nc.sbuf_tensor("p34_wvl", [128, NF * D], BF16) as _wvl,
        nc.sbuf_tensor("p34_wgh", [128, NF * D], BF16) as _wgh,
        nc.sbuf_tensor("p34_wgl", [128, NF * D], BF16) as _wgl,
    ):
        wvh = [_wvh[:, fc * D:(fc + 1) * D] for fc in range(NF)]
        wvl = [_wvl[:, fc * D:(fc + 1) * D] for fc in range(NF)]
        wgh = [_wgh[:, fc * D:(fc + 1) * D] for fc in range(NF)]
        wgl = [_wgl[:, fc * D:(fc + 1) * D] for fc in range(NF)]
        with nc.sbuf_tensor("p34_stage", [128, NF * D], F32) as _stage:
            stage = [_stage[:, fc * D:(fc + 1) * D] for fc in range(NF)]
            ep = EP(cx)
            _emit_weight_pair(ep, cx, d["wv0"], stage, wvh, wvl)
            _emit_weight_pair(ep, cx, d["wg0"], stage, wgh, wgl)
            ep.run("p34w")
        with (
            nc.sbuf_tensor("p34_xn", [128, NF * TT * 4], BF16) as _xn,
            nc.sbuf_tensor("p34_vh", [128, D], BF16) as vh,
            nc.sbuf_tensor("p34_vl", [128, D], BF16) as vl,
            nc.sbuf_tensor("p34_g", [128, D], F32) as gst,
            nc.psum_tensor("p34_ps0", [128, TT], F32) as ps0,
            nc.psum_tensor("p34_ps1", [128, TT], F32) as ps1,
            nc.psum_tensor("p34_ps2", [128, TT], F32) as ps2,
            nc.psum_tensor("p34_ps3", [128, TT], F32) as ps3,
        ):
            xnh = [[_xn[:, ((sl * 2 + 0) * NF + fc) * TT:((sl * 2 + 0) * NF + fc + 1) * TT]
                    for fc in range(NF)] for sl in range(2)]
            xnl = [[_xn[:, ((sl * 2 + 1) * NF + fc) * TT:((sl * 2 + 1) * NF + fc + 1) * TT]
                    for fc in range(NF)] for sl in range(2)]
            ps = [ps0, ps1, ps2, ps3]
            rel = []
            ep = EP(cx)
            for s, tt in product(range(BPC), range(NQB)):
                ti = s * NQB + tt
                sl = ti % 2
                tok = slice(s * S + tt * TT, s * S + (tt + 1) * TT)
                if ti >= 2:
                    ep.w("sync", TE, rel[ti - 2])
                for fc in range(NF):
                    ep.op("sync", _dma(xnh[sl][fc], d["xn0h"][fc, :, tok]), LD, 16)
                    ep.op("sync", _dma(xnl[sl][fc], d["xn0l"][fc, :, tok]), LD, 16)
                ld_done = LD.v
                for cc in range(4):
                    c = tt * 4 + cc
                    cs = slice(cc * CH, (cc + 1) * CH)
                    ep.w("tensor", LD, ld_done)
                    ep.w("tensor", SE)
                    ep.w("tensor", VE)
                    for ih, tgt in product(range(2), range(2)):
                        half = (slice(0, TT), slice(TT, D))[ih]
                        wh, wl = ((wvh, wvl), (wgh, wgl))[tgt]
                        pp = ps[tgt * 2 + ih]
                        for fc in range(NF):
                            _mm3(ep, cx, pp[:], xnh[sl][fc][:, cs],
                                 xnl[sl][fc][:, cs],
                                 wh[fc][:, half], wl[fc][:, half],
                                 start=(fc == 0), stop=(fc == NF - 1),
                                 inc=TE)
                    te_done = TE.v
                    ep.w("scalar", TE, te_done)
                    ep.w("scalar", ST)
                    for ih in range(2):
                        hw = slice(ih * TT, (ih + 1) * TT)
                        ep.op("scalar", (lambda e, o=vh[:, hw], i=ps[ih][:]:
                                         e.copy(out=o, in_=i)), SE)
                        ep.op("scalar", (lambda e, o=gst[:, hw], i=ps[2 + ih][:]:
                                         e.activation(out=o, in_=i,
                                                      func=ACT.Silu)), SE)
                    ep.w("vector", SE)
                    ep.w("vector", ST)
                    for ih in range(2):
                        hw = slice(ih * TT, (ih + 1) * TT)
                        ep.op("vector", (lambda e, o=vl[:, hw], a=ps[ih][:],
                                         b=vh[:, hw]: e.tensor_tensor(
                            out=o, in0=a, in1=b, op=ALU.subtract)), VE)
                    ep.w("gpsimd", SE)
                    ep.w("gpsimd", VE)
                    for h in range(H):
                        hc2 = slice(h * HD, (h + 1) * HD)
                        cc2 = slice(c * CH, (c + 1) * CH)
                        ep.op("gpsimd", _dma(d["v0h"][s, h, :, cc2], vh[:, hc2]),
                              ST, 16)
                        ep.op("gpsimd", _dma(d["v0l"][s, h, :, cc2], vl[:, hc2]),
                              ST, 16)
                    ep.op("gpsimd", _dma(d["g0"][s, c], gst[:]), ST, 16)
                rel.append(TE.v)
            ep.w("gpsimd", ST)
            ep.run("p34")


def _pf12_mix(cx):
    nc = cx.nc
    d = cx.d
    LD, ST, TE, VE, SE = (cx.sem(n) for n in ("LD", "ST", "TE", "VE", "SE"))
    TP = TT + 1
    with (
        nc.sbuf_tensor("pf_wrh", [128, NF * D], BF16) as _wrh,
        nc.sbuf_tensor("pf_wrl", [128, NF * D], BF16) as _wrl,
        nc.sbuf_tensor("pf_wkh", [128, NF * D], BF16) as _wkh,
        nc.sbuf_tensor("pf_wkl", [128, NF * D], BF16) as _wkl,
        nc.sbuf_tensor("pf_mu", [128, 16], F32) as mu,
    ):
        wrh = [_wrh[:, fc * D:(fc + 1) * D] for fc in range(NF)]
        wrl = [_wrl[:, fc * D:(fc + 1) * D] for fc in range(NF)]
        wkh = [_wkh[:, fc * D:(fc + 1) * D] for fc in range(NF)]
        wkl = [_wkl[:, fc * D:(fc + 1) * D] for fc in range(NF)]
        with nc.sbuf_tensor("pf_stage", [128, NF * D], F32) as _stage:
            stage = [_stage[:, fc * D:(fc + 1) * D] for fc in range(NF)]
            ep = EP(cx)
            for fc in range(NF):
                ep.op("sync", _dma(mu[:, fc:fc + 1], d["mur_col"][fc]), LD, 16)
                ep.op("sync", _dma(mu[:, 8 + fc:9 + fc], d["muk_col"][fc]), LD, 16)
            _emit_weight_pair(ep, cx, d["wr0"], stage, wrh, wrl)
            _emit_weight_pair(ep, cx, d["wk0f"], stage, wkh, wkl)
            ep.run("pfw")
        with (
            nc.sbuf_tensor("pf_xn", [128, NF * TP * 4], BF16) as _xn,
            nc.sbuf_tensor("pf_xfz", [128, TP], F32) as xfz,
            nc.sbuf_tensor("pf_xf", [128, TP], F32) as xf,
            nc.sbuf_tensor("pf_dd", [128, TT], F32) as dd,
            nc.sbuf_tensor("pf_m32", [128, TT], F32) as m32,
            nc.sbuf_tensor("pf_xr", [128, NF * TT * 2], BF16) as _xr,
            nc.sbuf_tensor("pf_xk", [128, NF * TT * 2], BF16) as _xk,
            nc.sbuf_tensor("pf_t", [128, TT], F32) as tfu,
            nc.sbuf_tensor("pf_oh", [128, TT], BF16) as koh,
            nc.sbuf_tensor("pf_ol", [128, TT], BF16) as kol,
            nc.sbuf_tensor("pf_r", [128, TT], F32) as rst,
            nc.psum_tensor("pf_ps0", [128, TT], F32) as ps0,
            nc.psum_tensor("pf_ps1", [128, TT], F32) as ps1,
        ):
            xnh = [[_xn[:, ((sl * 2 + 0) * NF + fc) * TP:((sl * 2 + 0) * NF + fc + 1) * TP]
                    for fc in range(NF)] for sl in range(2)]
            xnl = [[_xn[:, ((sl * 2 + 1) * NF + fc) * TP:((sl * 2 + 1) * NF + fc + 1) * TP]
                    for fc in range(NF)] for sl in range(2)]
            xrh = [_xr[:, (2 * fc) * TT:(2 * fc + 1) * TT] for fc in range(NF)]
            xrl = [_xr[:, (2 * fc + 1) * TT:(2 * fc + 2) * TT] for fc in range(NF)]
            xkh = [_xk[:, (2 * fc) * TT:(2 * fc + 1) * TT] for fc in range(NF)]
            xkl = [_xk[:, (2 * fc + 1) * TT:(2 * fc + 2) * TT] for fc in range(NF)]
            psums = [ps0, ps1]
            rel = []
            bank_se = [0, 0]
            ep = EP(cx)
            ep.op("vector", lambda e: e.memset(xfz[:], 0.0), VE)
            for s, tt in product(range(BPC), range(NQB)):
                    ti = s * NQB + tt
                    sl = ti % 2
                    t0 = s * S + tt * TT
                    tok = slice(t0, t0 + TT)
                    if ti >= 2:
                        ep.w("sync", TE, rel[ti - 2])
                    for fc, (dr, buf) in product(range(NF),
                                                 ((d["xn0h"], xnh), (d["xn0l"], xnl))):
                        if tt == 0:
                            ep.op("sync", _dma(buf[sl][fc][:, 1:TP],
                                               dr[fc, :, tok]), LD, 16)
                        else:
                            ep.op("sync", _dma(buf[sl][fc],
                                               dr[fc, :, t0 - 1:t0 + TT]), LD, 16)
                    ld_done = LD.v
                    ep.w("vector", LD, ld_done)
                    if ti >= 1:
                        ep.w("vector", TE, rel[ti - 1])
                    for fc in range(NF):
                        xfb = xfz if tt == 0 else xf
                        if tt == 0:
                            ep.op("vector", (lambda e, a=xnh[sl][fc], b=xnl[sl][fc]:
                                             e.tensor_tensor(out=xfz[:, 1:TP],
                                                             in0=a[:, 1:TP],
                                                             in1=b[:, 1:TP],
                                                             op=ALU.add)))
                        else:
                            ep.op("vector", (lambda e, a=xnh[sl][fc], b=xnl[sl][fc]:
                                             e.tensor_tensor(out=xf[:], in0=a, in1=b,
                                                             op=ALU.add)))
                        ep.op("vector", (lambda e, xx=xfb: e.tensor_tensor(
                            out=dd[:], in0=xx[:, 1:TP], in1=xx[:, 0:TT],
                            op=ALU.subtract)))
                        for (mc, hh, ll) in ((mu[:, fc:fc + 1], xrh, xrl),
                                             (mu[:, 8 + fc:9 + fc], xkh, xkl)):
                            ep.op("vector", (lambda e, m=mc, xx=xfb: e.scalar_tensor_tensor(
                                out=m32[:], in0=dd[:], scalar=m, in1=xx[:, 0:TT],
                                op0=ALU.mult, op1=ALU.add)), VE)
                            ep.w("scalar", VE)
                            ep.op("scalar", (lambda e, o=hh[fc]: e.copy(
                                out=o, in_=m32[:])), SE)
                            ep.w("vector", SE)
                            ep.op("vector", (lambda e, o=ll[fc], b=hh[fc]:
                                             e.tensor_tensor(out=o, in0=m32[:],
                                                             in1=b,
                                                             op=ALU.subtract)), VE)
                    mix_done = VE.v
                    for tgt, oc in product(range(2), range(NF)):
                        pp = psums[oc % 2]
                        ocol = slice(oc * 128, (oc + 1) * 128)
                        ep.w("tensor", VE, mix_done)
                        if bank_se[oc % 2]:
                            ep.w("tensor", SE, bank_se[oc % 2])
                        ww = ((wrh, wrl), (wkh, wkl))[tgt]
                        aa = ((xrh, xrl), (xkh, xkl))[tgt]
                        for fc in range(NF):
                            _mm3(ep, cx, pp[:], ww[0][fc][:, ocol],
                                 ww[1][fc][:, ocol], aa[0][fc], aa[1][fc],
                                 start=(fc == 0), stop=(fc == NF - 1),
                                 inc=TE)
                        ep.w("scalar", TE)
                        ep.w("scalar", ST)
                        if tgt == 0:
                            ep.op("scalar", (lambda e, i=pp[:]: e.activation(
                                out=rst[:], in_=i, func=ACT.Sigmoid)), SE)
                            bank_se[oc % 2] = SE.v
                            ep.w("gpsimd", SE)
                            ep.op("gpsimd", _dma(d["r0"][oc, :, tok], rst[:]),
                                  ST, 16)
                        else:
                            ep.op("scalar", (lambda e, i=pp[:]: e.activation(
                                out=tfu[:], in_=i, func=ACT.Relu)), SE)
                            bank_se[oc % 2] = SE.v
                            ep.w("vector", SE)
                            ep.op("vector", (lambda e: e.tensor_tensor(
                                out=m32[:], in0=tfu[:], in1=tfu[:],
                                op=ALU.mult)), VE)
                            ep.w("scalar", VE)
                            ep.op("scalar", (lambda e: e.copy(
                                out=koh[:], in_=m32[:])), SE)
                            ep.w("vector", SE)
                            ep.op("vector", (lambda e: e.tensor_tensor(
                                out=kol[:], in0=m32[:], in1=koh[:],
                                op=ALU.subtract)), VE)
                            ep.w("gpsimd", VE)
                            ep.op("gpsimd", _dma(d["kah"][oc, :, tok], koh[:]),
                                  ST, 16)
                            ep.op("gpsimd", _dma(d["kal"][oc, :, tok], kol[:]),
                                  ST, 16)
                    rel.append(TE.v)
            ep.w("gpsimd", ST)
            ep.run("pf12")


def _pf3_ffn0(cx):
    nc = cx.nc
    d = cx.d
    LD, ST, TE, VE, SE = (cx.sem(n) for n in ("LD", "ST", "TE", "VE", "SE"))
    with (
        nc.sbuf_tensor("p3_wh", [128, NF * D], BF16) as _wh,
        nc.sbuf_tensor("p3_wl", [128, NF * D], BF16) as _wl,
    ):
        wh = [_wh[:, fc * D:(fc + 1) * D] for fc in range(NF)]
        wl = [_wl[:, fc * D:(fc + 1) * D] for fc in range(NF)]
        with nc.sbuf_tensor("p3_stage", [128, NF * D], F32) as _stage:
            stage = [_stage[:, fc * D:(fc + 1) * D] for fc in range(NF)]
            ep = EP(cx)
            _emit_weight_pair(ep, cx, d["wv0f"], stage, wh, wl)
            ep.run("p3w")
        with (
            nc.sbuf_tensor("p3_ka", [128, NF * TT * 4], BF16) as _ka,
            nc.sbuf_tensor("p3_r", [128, NF * TT * 2], F32) as _r,
            nc.sbuf_tensor("p3_o", [128, TT * 2], F32) as ost,
            nc.psum_tensor("p3_ps0", [128, TT], F32) as ps0,
            nc.psum_tensor("p3_ps1", [128, TT], F32) as ps1,
        ):
            kah = [[_ka[:, ((sl * 2 + 0) * NF + fc) * TT:((sl * 2 + 0) * NF + fc + 1) * TT]
                    for fc in range(NF)] for sl in range(2)]
            kal = [[_ka[:, ((sl * 2 + 1) * NF + fc) * TT:((sl * 2 + 1) * NF + fc + 1) * TT]
                    for fc in range(NF)] for sl in range(2)]
            rti = [[_r[:, (sl * NF + oc) * TT:(sl * NF + oc + 1) * TT]
                    for oc in range(NF)] for sl in range(2)]
            psums = [ps0, ps1]
            rel = []
            bank_ve = [0, 0]
            ep = EP(cx)
            for s in range(BPC):
                for tt in range(NQB):
                    ti = s * NQB + tt
                    sl = ti % 2
                    tok = slice(s * S + tt * TT, s * S + (tt + 1) * TT)
                    if ti >= 2:
                        ep.w("sync", VE, rel[ti - 2])
                    for fc in range(NF):
                        ep.op("sync", _dma(kah[sl][fc], d["kah"][fc, :, tok]), LD, 16)
                        ep.op("sync", _dma(kal[sl][fc], d["kal"][fc, :, tok]), LD, 16)
                        ep.op("sync", _dma(rti[sl][fc], d["r0"][fc, :, tok]), LD, 16)
                    ld_done = LD.v
                    for oc in range(NF):
                        pp = psums[oc % 2]
                        ocol = slice(oc * 128, (oc + 1) * 128)
                        ep.w("tensor", LD, ld_done)
                        if bank_ve[oc % 2]:
                            ep.w("tensor", VE, bank_ve[oc % 2])
                        for fc in range(NF):
                            _mm3(ep, cx, pp[:], wh[fc][:, ocol], wl[fc][:, ocol],
                                 kah[sl][fc], kal[sl][fc],
                                 start=(fc == 0), stop=(fc == NF - 1), inc=TE)
                        ep.w("vector", TE)
                        ep.w("vector", ST)
                        ob = ost[:, (oc % 2) * TT:(oc % 2 + 1) * TT]
                        ep.op("vector", (lambda e, o=ob, a=pp[:], r=rti[sl][oc]:
                                         e.tensor_tensor(out=o, in0=a, in1=r,
                                                         op=ALU.mult)), VE)
                        bank_ve[oc % 2] = VE.v
                        ep.w("gpsimd", VE)
                        ep.op("gpsimd", _dma(d["f0"][oc, :, tok], ob), ST, 16)
                    rel.append(VE.v)
            ep.w("gpsimd", ST)
            ep.run("pf3")


def _p5_ret0(cx):
    nc = cx.nc
    d = cx.d
    LD, LD2, ST, TE, VE, SE, GP = (cx.sem(n) for n in
                                   ("LD", "LD2", "ST", "TE", "VE", "SE", "GP"))
    DMW = S + 384
    with (
        nc.sbuf_tensor("p5_woh", [128, NF * D], BF16) as _woh,
        nc.sbuf_tensor("p5_wol", [128, NF * D], BF16) as _wol,
        nc.sbuf_tensor("p5_gnw", [128, D], F32) as gnwB,
        nc.sbuf_tensor("p5_gnb", [128, D], F32) as gnbB,
        nc.sbuf_tensor("p5_ident", [128, 128], F32) as ident,
        nc.sbuf_tensor("p5_ones", [128, 1], F32) as ones_col,
        nc.sbuf_tensor("p5_onesr", [1, 128], F32) as ones_row,
        nc.sbuf_tensor("p5_rows", [1, 6 * TT], F32) as rows,
    ):
        woh = [_woh[:, fc * D:(fc + 1) * D] for fc in range(NF)]
        wol = [_wol[:, fc * D:(fc + 1) * D] for fc in range(NF)]
        with (nc.sbuf_tensor("p5_stage", [128, NF * D], F32) as _stage,
              nc.psum_tensor("p5_bc", [128, TT], F32) as psbc):
            stage = [_stage[:, fc * D:(fc + 1) * D] for fc in range(NF)]
            ep = EP(cx)
            _emit_weight_pair(ep, cx, d["wo0"], stage, woh, wol)
            ep.op("sync", _dma(ident[:], d["tbl_ident"]), LD, 16)
            ep.op("sync", _dma(rows[0:1, 0:D], d["gnw_row"]), LD, 16)
            ep.op("sync", _dma(rows[0:1, D:2 * D], d["gnb_row"]), LD, 16)
            ep.op("vector", lambda e: e.memset(ones_col[:], 1.0), VE)
            ep.op("vector", lambda e: e.memset(ones_row[:], 1.0), VE)
            ep.w("tensor", LD)
            ep.w("tensor", VE)
            for i, dst in enumerate((gnwB, gnbB)):
                for hw in range(2):
                    ep.op("tensor", (lambda e, r=rows[0:1, i * D + hw * TT:
                                                      i * D + (hw + 1) * TT]:
                                     e.matmul(out=psbc[:], lhsT=ones_row[:],
                                              rhs=r, start=True, stop=True)), TE)
                    ep.w("vector", TE)
                    ep.op("vector", (lambda e, o=dst[:, hw * TT:(hw + 1) * TT]:
                                     e.tensor_copy(out=o, in_=psbc[:])), VE)
                    ep.w("tensor", VE)
            ep.run("p5w")
        with ExitStack() as _es:
            _qb = _es.enter_context(nc.sbuf_tensor("p5_q", [128, TT * 4], BF16))
            _kb = _es.enter_context(nc.sbuf_tensor("p5_k", [128, S * 4], BF16))
            _vb = _es.enter_context(nc.sbuf_tensor("p5_v", [128, S * 4], BF16))
            _dmb = _es.enter_context(nc.sbuf_tensor("p5_dm", [128, DMW * 2], F32))
            gbuf = _es.enter_context(nc.sbuf_tensor("p5_g", [128, 4 * D], F32))
            fbuf = _es.enter_context(nc.sbuf_tensor("p5_f0", [128, 2 * TT], F32))
            xbuf = _es.enter_context(nc.sbuf_tensor("p5_x", [128, 2 * TT], F32))
            _ph = _es.enter_context(nc.sbuf_tensor("p5_ph", [128, TT * 2], BF16))
            _pl = _es.enter_context(nc.sbuf_tensor("p5_pl", [128, TT * 2], BF16))
            _p32 = _es.enter_context(nc.sbuf_tensor("p5_p32", [128, TT * 2], F32))
            sqt = _es.enter_context(nc.sbuf_tensor("p5_sq", [128, TT], F32))
            cols = _es.enter_context(nc.sbuf_tensor("p5_cols", [128, 32], F32))
            ynt = _es.enter_context(nc.sbuf_tensor("p5_yn", [128, 128], F32))
            gyt = _es.enter_context(nc.sbuf_tensor("p5_gy", [128, 128], F32))
            _gyh = _es.enter_context(nc.sbuf_tensor("p5_gyh", [128, NF * TT], BF16))
            _gyl = _es.enter_context(nc.sbuf_tensor("p5_gyl", [128, NF * TT], BF16))
            x1s = _es.enter_context(nc.sbuf_tensor("p5_x1", [128, NF * TT], F32))
            xn1h_s = _es.enter_context(nc.sbuf_tensor("p5_xnh", [128, NF * TT], BF16))
            xn1l_s = _es.enter_context(nc.sbuf_tensor("p5_xnl", [128, NF * TT], BF16))
            tmp = _es.enter_context(nc.sbuf_tensor("p5_tmp", [128, TT], F32))
            pS0 = _es.enter_context(nc.psum_tensor("p5_pS0", [128, TT], F32))
            pS1 = _es.enter_context(nc.psum_tensor("p5_pS1", [128, TT], F32))
            pY0 = _es.enter_context(nc.psum_tensor("p5_pY0", [128, TT], F32))
            pY1 = _es.enter_context(nc.psum_tensor("p5_pY1", [128, TT], F32))
            pT = _es.enter_context(nc.psum_tensor("p5_pT", [128, 128], F32))
            pW = _es.enter_context(nc.psum_tensor("p5_pW", [128, TT], F32))
            psA = _es.enter_context(nc.psum_tensor("p5_pA", [128, TT], F32))
            psB = _es.enter_context(nc.psum_tensor("p5_pB", [128, TT], F32))
            qh = [_qb[:, i * TT:(i + 1) * TT] for i in range(4)]
            kbs = [_kb[:, i * S:(i + 1) * S] for i in range(4)]
            vbs = [_vb[:, i * S:(i + 1) * S] for i in range(4)]
            dmb = [_dmb[:, i * DMW:(i + 1) * DMW] for i in range(2)]
            phb = [_ph[:, i * TT:(i + 1) * TT] for i in range(2)]
            plb = [_pl[:, i * TT:(i + 1) * TT] for i in range(2)]
            p32b = [_p32[:, i * TT:(i + 1) * TT] for i in range(2)]
            gyh = [_gyh[:, fc * TT:(fc + 1) * TT] for fc in range(NF)]
            gyl = [_gyl[:, fc * TT:(fc + 1) * TT] for fc in range(NF)]
            mrow = rows[0:1, 2 * D:2 * D + TT]
            m2row = rows[0:1, 2 * D + TT:2 * D + 2 * TT]
            pSb = [pS0, pS1]
            pYb = [pY0, pY1]
            ep = EP(cx)
            yfree = [0, 0]
            gyfree = [0]
            hfree = []
            hseq = [0]
            for s, qb in product(range(BPC), range(NQB)):
                nk = 4 * (qb + 1)
                tok = slice(s * S + qb * TT, s * S + (qb + 1) * TT)
                stok = slice(qb * TT, (qb + 1) * TT)
                ep.w("sync", VE)
                for cc in range(4):
                    ep.op("sync", _dma(gbuf[:, cc * D:(cc + 1) * D],
                                       d["g0"][s, qb * 4 + cc]), LD, 16)
                def _head(h):
                    hs = hseq[0] % 2
                    if len(hfree) >= 2:
                        ep.w("sync", TE, hfree[-2][0])
                        ep.w("sync", VE, hfree[-2][1])
                    qhh, qhl = qh[2 * hs], qh[2 * hs + 1]
                    kbh, kbl = kbs[2 * hs], kbs[2 * hs + 1]
                    vbh, vbl = vbs[2 * hs], vbs[2 * hs + 1]
                    dmt = dmb[hs]
                    ep.op("sync", _dma(qhh, d["q0h"][s, h, :, stok]), LD, 16)
                    ep.op("sync", _dma(qhl, d["q0l"][s, h, :, stok]), LD, 16)
                    ep.op("sync", _dma(kbh[:, 0:nk * CH],
                                       d["k0h"][s, h, :, 0:nk * CH]), LD, 16)
                    ep.op("sync", _dma(kbl[:, 0:nk * CH],
                                       d["k0l"][s, h, :, 0:nk * CH]), LD, 16)
                    ep.op("sync", _dma(vbh[:, 0:nk * CH],
                                       d["v0h"][s, h, :, 0:nk * CH]), LD, 16)
                    ep.op("sync", _dma(vbl[:, 0:nk * CH],
                                       d["v0l"][s, h, :, 0:nk * CH]), LD, 16)
                    ep.op("sync", _dma(dmt[:, 0:384 + (qb + 1) * TT],
                                       d["tbl_dm"][h, :, 0:384 + (qb + 1) * TT]),
                          LD, 16)
                    ld_done = LD.v
                    pY = pYb[h % 2]
                    ep.w("tensor", LD, ld_done)
                    ep.w("tensor", VE, yfree[h % 2])
                    first_y = [True]
                    pvals = {}
                    tvals = {}

                    def _st(kt):
                        pS = pSb[kt % 2]
                        if kt >= 2:
                            ep.w("tensor", VE, pvals[kt - 2])
                        kc = slice(kt * CH, (kt + 1) * CH)
                        _mm3(ep, cx, pS[:], kbh[:, kc], kbl[:, kc],
                             qhh, qhl, start=True, stop=True, inc=TE)

                    def _pb(kt):
                        pS = pSb[kt % 2]
                        sl = kt % 2
                        x0 = 384 + qb * TT - kt * CH
                        if kt >= 2:
                            ep.w("vector", TE, tvals[kt - 2])
                        ep.w("vector", TE)
                        ep.op("vector", (lambda e, o=phb[sl], a=pS[:],
                                         m=dmt[:, x0:x0 + TT]: e.tensor_tensor(
                            out=o, in0=a, in1=m, op=ALU.mult)))
                        ep.op("vector", (lambda e, o=p32b[sl], a=pS[:],
                                         m=dmt[:, x0:x0 + TT]: e.tensor_tensor(
                            out=o, in0=a, in1=m, op=ALU.mult)))
                        ep.op("vector", (lambda e, o=plb[sl], a=p32b[sl],
                                         b=phb[sl]: e.tensor_tensor(
                            out=o, in0=a, in1=b, op=ALU.subtract)), VE)
                        pvals[kt] = VE.v

                    def _pv(kt, last_k=False):
                        sl = kt % 2
                        kc = slice(kt * CH, (kt + 1) * CH)
                        ep.w("tensor", VE, pvals[kt])
                        for qs in range(4):
                            qc = slice(qs * 128, (qs + 1) * 128)
                            st = first_y[0]
                            first_y[0] = False
                            ep.op("tensor", (lambda e, o=pY[:, qc], l=phb[sl][:, qc],
                                             r=vbh[:, kc], s0=st: e.matmul(
                                out=o, lhsT=l, rhs=r, start=s0, stop=False)))
                            ep.op("tensor", (lambda e, o=pY[:, qc], l=plb[sl][:, qc],
                                             r=vbh[:, kc]: e.matmul(
                                out=o, lhsT=l, rhs=r, start=False, stop=False)))
                            ep.op("tensor", (lambda e, o=pY[:, qc], l=phb[sl][:, qc],
                                             r=vbl[:, kc],
                                             s1=(last_k and qs == 3): e.matmul(
                                out=o, lhsT=l, rhs=r, start=False, stop=s1)),
                                TE if qs == 3 else None)
                        tvals[kt] = TE.v

                    _st(0)
                    _pb(0)
                    for kt in range(1, nk):
                        _st(kt)
                        _pb(kt)
                        _pv(kt - 1)
                    _pv(nk - 1, last_k=True)
                    hfree.append((TE.v, VE.v))
                    hseq[0] += 1
                    ep.w("scalar", TE)
                    ep.w("vector", TE)
                    for qs in range(4):
                        qc = slice(qs * 128, (qs + 1) * 128)
                        c0, c1, c2, c3, c4 = (cols[:, i:i + 1] for i in range(5))
                        ep.op("scalar", (lambda e, i=pY[:, qc]: e.activation(
                            out=sqt[:, 0:128], in_=i, func=ACT.Square)), SE)
                        ep.w("vector", SE)
                        ep.op("vector", (lambda e, i=pY[:, qc]: e.reduce_sum(
                            out=c0, in_=i, axis=AX.X)))
                        ep.op("vector", (lambda e: e.reduce_sum(
                            out=c1, in_=sqt[:, 0:128], axis=AX.X)), VE)
                        ep.w("scalar", VE)
                        ep.op("scalar", (lambda e: e.activation(
                            out=c2, in_=c0, func=ACT.Copy, scale=1.0 / 128)))
                        ep.op("scalar", (lambda e: e.activation(
                            out=c3, in_=c2, func=ACT.Square)))
                        ep.op("scalar", (lambda e: e.activation(
                            out=c4, in_=c1, func=ACT.Copy, scale=1.0 / 128)), SE)
                        ep.w("vector", SE)
                        ep.op("vector", (lambda e: e.tensor_tensor(
                            out=c4, in0=c4, in1=c3, op=ALU.subtract)))
                        ep.op("vector", (lambda e: e.tensor_scalar(
                            out=c4, in0=c4, scalar1=1e-5, scalar2=None,
                            op0=ALU.add)), VE)
                        ep.w("scalar", VE)
                        ep.op("scalar", (lambda e: e.activation(
                            out=c3, in_=c4, func=ACT.Sqrt)), SE)
                        ep.w("vector", SE)
                        ep.op("vector", (lambda e: e.reciprocal(out=c4, in_=c3)))
                        ep.op("vector", (lambda e, i=pY[:, qc]: e.tensor_scalar(
                            out=ynt[:], in0=i, scalar1=c2, scalar2=c4,
                            op0=ALU.subtract, op1=ALU.mult)))
                        ep.op("vector", (lambda e, w=gnwB[:, h * HD:(h + 1) * HD]:
                                         e.tensor_tensor(out=ynt[:], in0=ynt[:],
                                                         in1=w, op=ALU.mult)))
                        ep.op("vector", (lambda e, w=gnbB[:, h * HD:(h + 1) * HD]:
                                         e.tensor_tensor(out=ynt[:], in0=ynt[:],
                                                         in1=w, op=ALU.add)))
                        ep.op("vector", (lambda e, g=gbuf[:, qs * D + h * HD:
                                                          qs * D + (h + 1) * HD]:
                                         e.tensor_tensor(out=gyt[:], in0=ynt[:],
                                                         in1=g, op=ALU.mult)), VE)
                        ep.w("tensor", VE)
                        ep.w("tensor", SE)
                        ep.op("tensor", (lambda e: e.transpose(
                            out=pT[:], in_=gyt[:], identity=ident[:])), TE)
                        ep.w("scalar", TE)
                        if h == 0 and qs == 0:
                            ep.w("scalar", TE, gyfree[0])
                        ep.op("scalar", (lambda e, o=gyh[h][:, qc]: e.copy(
                            out=o, in_=pT[:])), SE)
                        ep.w("vector", SE)
                        ep.op("vector", (lambda e, o=gyl[h][:, qc], b=gyh[h][:, qc]:
                                         e.tensor_tensor(out=o, in0=pT[:], in1=b,
                                                         op=ALU.subtract)), VE)
                    yfree[h % 2] = VE.v

                for h in range(H):
                    _head(h)
                gy_done = VE.v
                fx_ve = []
                for mc in range(NF):
                    mcol = slice(mc * 128, (mc + 1) * 128)
                    if mc >= 2:
                        ep.w("sync", VE, fx_ve[mc - 2])
                    mslot = slice((mc % 2) * TT, (mc % 2 + 1) * TT)
                    ep.op("sync", _dma(fbuf[:, mslot], d["f0"][mc, :, tok]),
                          LD, 16)
                    ep.op("sync", _dma(xbuf[:, mslot], d["x_in"][mc, :, tok]),
                          LD, 16)
                    ld_fx = LD.v
                    ep.w("tensor", VE, gy_done)
                    ep.w("tensor", SE)
                    if mc == 0:
                        ep.w("vector", ST)
                    for fc in range(NF):
                        _mm3(ep, cx, pW[:], woh[fc][:, mcol], wol[fc][:, mcol],
                             gyh[fc], gyl[fc], start=(fc == 0), stop=(fc == NF - 1),
                             inc=TE)
                    ep.w("vector", TE)
                    ep.w("vector", LD, ld_fx)
                    ep.op("vector", (lambda e, f=fbuf[:, mslot]:
                                     e.tensor_tensor(out=tmp[:], in0=pW[:], in1=f,
                                                     op=ALU.add)))
                    ep.op("vector", (lambda e, x=xbuf[:, mslot],
                                     o=x1s[:, mc * TT:(mc + 1) * TT]:
                                     e.tensor_tensor(out=o, in0=tmp[:], in1=x,
                                                     op=ALU.add)), VE)
                    fx_ve.append(VE.v)
                    ep.w("scalar", VE)
                    ep.op("scalar", (lambda e, i=x1s[:, mc * TT:(mc + 1) * TT]:
                                     e.activation(out=sqt[:], in_=i,
                                                  func=ACT.Square)), SE)
                    ep.w("tensor", VE)
                    ep.w("tensor", SE)
                    ep.op("tensor", (lambda e, r=x1s[:, mc * TT:(mc + 1) * TT],
                                     s0=(mc == 0), p0=(mc == NF - 1): e.matmul(
                        out=psA[0:1, :], lhsT=ones_col[:], rhs=r, start=s0,
                        stop=p0)))
                    ep.op("tensor", (lambda e, s0=(mc == 0), p0=(mc == NF - 1):
                                     e.matmul(
                        out=psB[0:1, :], lhsT=ones_col[:], rhs=sqt[:], start=s0,
                        stop=p0)), TE)
                ep.w("scalar", TE)
                ep.op("scalar", (lambda e: e.activation(out=mrow, in_=psA[0:1, :],
                                                        func=ACT.Copy,
                                                        scale=1.0 / 1024)))
                ep.op("scalar", (lambda e: e.activation(out=m2row, in_=mrow,
                                                        func=ACT.Square)), SE)
                ep.w("vector", SE)
                ep.op("vector", (lambda e: e.tensor_scalar(
                    out=m2row, in0=m2row, scalar1=-1.0, scalar2=None,
                    op0=ALU.mult)))
                ep.op("vector", (lambda e: e.scalar_tensor_tensor(
                    out=m2row, in0=psB[0:1, :], scalar=1.0 / 1024, in1=m2row,
                    op0=ALU.mult, op1=ALU.add)))
                ep.op("vector", (lambda e: e.tensor_scalar(
                    out=m2row, in0=m2row, scalar1=1e-5, scalar2=None,
                    op0=ALU.add)), VE)
                ep.w("scalar", VE)
                ep.op("scalar", (lambda e: e.activation(out=m2row, in_=m2row,
                                                        func=ACT.Sqrt)), SE)
                ep.w("vector", SE)
                ep.op("vector", (lambda e: e.reciprocal(out=m2row, in_=m2row)), VE)
                ep.w("tensor", VE)
                ep.w("tensor", SE)
                ep.op("tensor", (lambda e: e.matmul(out=psA[:], lhsT=ones_row[:],
                                                    rhs=mrow, start=True,
                                                    stop=True)))
                ep.op("tensor", (lambda e: e.matmul(out=psB[:], lhsT=ones_row[:],
                                                    rhs=m2row, start=True,
                                                    stop=True)), TE)
                ep.w("vector", TE)
                ep.w("vector", ST)
                for mc in range(NF):
                    mm = slice(mc * TT, (mc + 1) * TT)
                    ep.op("vector", (lambda e, x=x1s[:, mm]: e.tensor_tensor(
                        out=tmp[:], in0=x, in1=psA[:], op=ALU.subtract)))
                    ep.op("vector", (lambda e, o=sqt[:]: e.tensor_tensor(
                        out=o, in0=tmp[:], in1=psB[:], op=ALU.mult)), VE)
                    ep.w("scalar", VE)
                    ep.op("scalar", (lambda e, o=xn1h_s[:, mm]: e.copy(
                        out=o, in_=sqt[:])), SE)
                    ep.w("vector", SE)
                    ep.op("vector", (lambda e, o=xn1l_s[:, mm], b=xn1h_s[:, mm]:
                                     e.tensor_tensor(out=o, in0=sqt[:], in1=b,
                                                     op=ALU.subtract)), VE)
                gyfree[0] = TE.v
                ep.w("gpsimd", VE)
                for fc in range(NF):
                    mm2 = slice(fc * TT, (fc + 1) * TT)
                    ep.op("gpsimd", _dma(d["x1"][fc, :, tok], x1s[:, mm2]), ST, 16)
                    ep.op("gpsimd", _dma(d["xn1h"][fc, :, tok], xn1h_s[:, mm2]),
                          ST, 16)
                    ep.op("gpsimd", _dma(d["xn1l"][fc, :, tok], xn1l_s[:, mm2]),
                          ST, 16)
            ep.w("gpsimd", ST)
            ep.run("p5")


def _p6_kv1(cx):
    nc = cx.nc
    d = cx.d
    LD, LD2, ST, TE, VE, SE = (cx.sem(n) for n in
                               ("LD", "LD2", "ST", "TE", "VE", "SE"))
    p = cx.p67
    RC, RS, w8, ident, bcols, cmsk = (p["RC"], p["RS"], p["w8"], p["ident"],
                                      p["bcols"], p["cmsk"])
    with ExitStack() as es6:
        wkh_ = es6.enter_context(nc.sbuf_tensor("p6_wkh", [128, NF * D], BF16))
        wkl_ = es6.enter_context(nc.sbuf_tensor("p6_wkl", [128, NF * D], BF16))
        wvh_ = es6.enter_context(nc.sbuf_tensor("p6_wvh", [128, NF * D], BF16))
        wvl_ = es6.enter_context(nc.sbuf_tensor("p6_wvl", [128, NF * D], BF16))
        wkh = [wkh_[:, fc * D:(fc + 1) * D] for fc in range(NF)]
        wkl = [wkl_[:, fc * D:(fc + 1) * D] for fc in range(NF)]
        wvh = [wvh_[:, fc * D:(fc + 1) * D] for fc in range(NF)]
        wvl = [wvl_[:, fc * D:(fc + 1) * D] for fc in range(NF)]
        with ExitStack() as es6b:
            stage_ = es6b.enter_context(
                nc.sbuf_tensor("p6_stage", [128, NF * D], F32))
            stage = [stage_[:, fc * D:(fc + 1) * D] for fc in range(NF)]
            rk = es6b.enter_context(nc.sbuf_tensor("p6_rk", [32, 2 * S], F32))
            rr = es6b.enter_context(nc.sbuf_tensor("p6_rr", [32, 4 * S], F32))
            io32 = es6b.enter_context(nc.sbuf_tensor("p6_io", [32, 2], F32))
            ep = EP(cx)
            ep.op("sync", _dma(rk[:, 0:S], d["tbl_ropek"][0]), LD, 16)
            ep.op("sync", _dma(rk[:, S:2 * S], d["tbl_ropek"][1]), LD, 16)
            ep.op("sync", _dma(rr[:, 0:S], d["tbl_pos32"]), LD, 16)
            ep.op("sync", _dma(io32[:, 0:1], d["tbl_iota32"]), LD, 16)
            ep.op("sync", _dma(ident[:], d["tbl_ident"]), LD, 16)
            for o in range(4):
                ep.op("sync", _dma(cmsk[:, o * TT:(o + 1) * TT],
                                   d["tbl_cmsk"][o]), LD, 16)
            for fc in range(NF):
                ep.op("sync", _dma(bcols[:, fc:fc + 1], d["bias1_col"][fc]),
                      LD, 16)
            ep.w("scalar", LD)
            ep.op("scalar", (lambda e: e.activation(
                out=io32[:, 1:2], in_=io32[:, 0:1], func=ACT.Exp,
                scale=-math.log(10000.0) / 32.0)), SE)
            ep.w("vector", SE)
            ep.w("vector", LD)
            ep.op("vector", (lambda e: e.tensor_scalar(
                out=rr[:, S:2 * S], in0=rr[:, 0:S], scalar1=io32[:, 1:2],
                scalar2=0.0, op0=ALU.mult, op1=ALU.add)), VE)
            for i, (koff, extra) in enumerate(((0, None), (S, PI / 2))):
                ep.w("vector", SE)
                ep.op("vector", (lambda e, ko=koff: e.tensor_scalar(
                    out=rr[:, 2 * S:3 * S], in0=rk[:, ko:ko + S],
                    scalar1=-2 * PI, scalar2=None, op0=ALU.mult)))
                ep.op("vector", (lambda e: e.tensor_tensor(
                    out=rr[:, 2 * S:3 * S], in0=rr[:, 2 * S:3 * S],
                    in1=rr[:, S:2 * S], op=ALU.add)))
                if extra is None:
                    ep.op("vector", (lambda e: e.tensor_scalar(
                        out=rr[:, 2 * S:3 * S], in0=rr[:, 2 * S:3 * S],
                        scalar1=CLP, scalar2=-CLP, op0=ALU.min, op1=ALU.max)), VE)
                else:
                    ep.op("vector", (lambda e, x=extra: e.tensor_scalar(
                        out=rr[:, 2 * S:3 * S], in0=rr[:, 2 * S:3 * S],
                        scalar1=x, scalar2=CLP, op0=ALU.add, op1=ALU.min)))
                    ep.op("vector", (lambda e: e.tensor_scalar(
                        out=rr[:, 2 * S:3 * S], in0=rr[:, 2 * S:3 * S],
                        scalar1=-CLP, scalar2=None, op0=ALU.max)), VE)
                ep.w("scalar", VE)
                ep.op("scalar", (lambda e: e.activation(
                    out=rr[:, 3 * S:4 * S], in_=rr[:, 2 * S:3 * S],
                    func=ACT.Sin)))
                if i == 0:
                    ep.op("scalar", (lambda e: e.activation(
                        out=RS[0:32, :], in_=rr[:, 3 * S:4 * S], func=ACT.Copy,
                        scale=-RINV)))
                    ep.op("scalar", (lambda e: e.activation(
                        out=RS[32:64, :], in_=rr[:, 3 * S:4 * S], func=ACT.Copy,
                        scale=RINV)), SE)
                else:
                    ep.op("scalar", (lambda e: e.activation(
                        out=RC[0:32, :], in_=rr[:, 3 * S:4 * S], func=ACT.Copy,
                        scale=RINV)))
                    ep.op("scalar", (lambda e: e.activation(
                        out=RC[32:64, :], in_=rr[:, 3 * S:4 * S], func=ACT.Copy,
                        scale=RINV)), SE)
            ep.op("vector", (lambda e: e.memset(RC[64:128, :], RINV)), VE)
            if P6_DBG:
                ep.w("gpsimd", SE)
                ep.w("gpsimd", VE)
                ep.op("gpsimd", _dma(d["dbg"][:, 0:S], RC[:]), ST, 16)
                ep.op("gpsimd", _dma(d["dbg"][0:64, S:2 * S], RS[:]), ST, 16)
                ep.w("gpsimd", cx.sem("ST"))
            _emit_weight_pair(ep, cx, d["wk1"], stage, wkh, wkl)
            _emit_weight_pair(ep, cx, d["wv1"], stage, wvh, wvl)
            ep.w("sync", VE)
            for fc in range(NF):
                ep.op("sync", _dma(stage[fc], d["wf1"][fc]), LD, 16)
            ep.w("scalar", LD)
            for fc in range(NF):
                ep.op("scalar", (lambda e, o=w8[fc], i=stage[fc]: e.copy(
                    out=o, in_=i)), SE)
            ep.run("p6w")

        with ExitStack() as es6c:
            xn_ = es6c.enter_context(
                nc.sbuf_tensor("p6_xn", [128, NF * TT * 4], BF16))
            qsb = es6c.enter_context(nc.sbuf_tensor("p6_qsb", [128, 2048], F32))
            qsw = es6c.enter_context(nc.sbuf_tensor("p6_qsw", [64, 2048], F32))
            t1s = es6c.enter_context(nc.sbuf_tensor("p6_t1", [128, TT], F32))
            t2s = es6c.enter_context(nc.sbuf_tensor("p6_t2", [64, TT], F32))
            qf = es6c.enter_context(nc.sbuf_tensor("p6_qf", [128, TT], F32))
            ohi = es6c.enter_context(nc.sbuf_tensor("p6_oh", [128, 2048], BF16))
            olo = es6c.enter_context(nc.sbuf_tensor("p6_ol", [128, 2048], BF16))
            stA = es6c.enter_context(nc.sbuf_tensor("p6_sa", [128, H * 132], BF16))
            stB = es6c.enter_context(nc.sbuf_tensor("p6_sb", [128, H * 128], BF16))
            ps0 = es6c.enter_context(nc.psum_tensor("p6_ps0", [128, TT], F32))
            ps1 = es6c.enter_context(nc.psum_tensor("p6_ps1", [128, TT], F32))
            ps2 = es6c.enter_context(nc.psum_tensor("p6_ps2", [128, TT], F32))
            ps3 = es6c.enter_context(nc.psum_tensor("p6_ps3", [128, TT], F32))
            xnh = [[xn_[:, ((sl * 2 + 0) * NF + fc) * TT:
                        ((sl * 2 + 0) * NF + fc + 1) * TT]
                    for fc in range(NF)] for sl in range(2)]
            xnl = [[xn_[:, ((sl * 2 + 1) * NF + fc) * TT:
                        ((sl * 2 + 1) * NF + fc + 1) * TT]
                    for fc in range(NF)] for sl in range(2)]
            psums = [ps0, ps1]
            ep = EP(cx)
            if P6_DBG:
                ep.op("gpsimd", _dma(d["dbg"][:, 0:S], RC[:]), ST, 16)
                ep.op("gpsimd", _dma(d["dbg"][0:64, S:2 * S], RS[:]), ST, 16)
                ep.w("gpsimd", ST)
            ep.op("vector", lambda e: e.memset(stA[:], 1.0), VE)
            bank_free = [0, 0]
            rel = []
            st_hist = []
            HG = 4
            for s, tt in product(range(BPC), range(NQB)):
                ti = s * NQB + tt
                sl = ti % 2
                tok = slice(s * S + tt * TT, s * S + (tt + 1) * TT)
                ttok = slice(tt * TT, (tt + 1) * TT)
                if ti >= 2:
                    ep.w("sync", SE, rel[ti - 2])
                for fc in range(NF):
                    ep.op("sync", _dma(xnh[sl][fc], d["xn1h"][fc, :, tok]),
                          LD, 16)
                    ep.op("sync", _dma(xnl[sl][fc], d["xn1l"][fc, :, tok]),
                          LD, 16)
                ld_done = LD.v

                def _khg(hg):
                    ep.w("tensor", LD, ld_done)
                    for hh in range(HG):
                        h = hg * HG + hh
                        bk = hh % 2
                        pp = psums[bk]
                        ocol = slice(h * 128, (h + 1) * 128)
                        if bank_free[bk] > 0:
                            ep.w("tensor", SE, bank_free[bk])
                        for fc in range(NF):
                            _mm3(ep, cx, pp[:], wkh[fc][:, ocol],
                                 wkl[fc][:, ocol], xnh[sl][fc], xnl[sl][fc],
                                 start=(fc == 0), stop=(fc == NF - 1), inc=TE)
                        ep.w("scalar", TE)
                        if hh == 0:
                            ep.w("scalar", VE)
                        ep.op("scalar", (lambda e, o=qsb[:, hh * TT:(hh + 1) * TT],
                                         i=pp[:]: e.copy(out=o, in_=i)), SE)
                        bank_free[bk] = SE.v
                    ep.op("scalar", _dma(qsw[0:32, :], qsb[32:64, :]), LD2, 16)
                    ep.op("scalar", _dma(qsw[32:64, :], qsb[0:32, :]), LD2, 16)
                    ep.w("vector", SE)
                    ep.w("vector", LD2)
                    if st_hist:
                        ep.w("vector", ST, st_hist[-1])
                    if P6_NOROT:
                        for hh in range(HG):
                            hc = slice(hh * TT, (hh + 1) * TT)
                            ep.op("vector", (lambda e, o=ohi[:, hc], i=qsb[:, hc]:
                                             e.tensor_copy(out=o, in_=i)))
                            ep.op("vector", (lambda e, o=olo[:, hc], a=qsb[:, hc],
                                             b=ohi[:, hc]: e.tensor_tensor(
                                out=o, in0=a, in1=b, op=ALU.subtract)), VE)
                    elif P6_ROTMODE == 1:
                        for hh in range(HG):
                            hc = slice(hh * TT, (hh + 1) * TT)
                            ep.op("vector", (lambda e, i=qsb[:, hc]:
                                             e.tensor_tensor(out=t1s[:], in0=i,
                                                             in1=RC[:, ttok],
                                                             op=ALU.mult)))
                            ep.op("vector", (lambda e, o=ohi[:, hc]:
                                             e.tensor_copy(out=o, in_=t1s[:])))
                            ep.op("vector", (lambda e, o=olo[:, hc], b=ohi[:, hc]:
                                             e.tensor_tensor(out=o, in0=t1s[:],
                                                             in1=b,
                                                             op=ALU.subtract)), VE)
                    elif P6_ROTMODE == 2:
                        for hh in range(HG):
                            hc = slice(hh * TT, (hh + 1) * TT)
                            ep.op("vector", (lambda e, i=qsw[:, hc]:
                                             e.tensor_tensor(out=t2s[:], in0=i,
                                                             in1=RS[:, ttok],
                                                             op=ALU.mult)))
                            ep.op("vector", (lambda e, o=ohi[:, hc]:
                                             e.memset(o, 0.0)))
                            ep.op("vector", (lambda e, o=ohi[0:64, hc]:
                                             e.tensor_copy(out=o, in_=t2s[:])))
                            ep.op("vector", (lambda e, o=olo[:, hc], b=ohi[:, hc]:
                                             e.tensor_tensor(out=o, in0=b, in1=b,
                                                             op=ALU.subtract)), VE)
                    else:
                      for hh in range(HG):
                        h = hg * HG + hh
                        hc = slice(hh * TT, (hh + 1) * TT)
                        ep.op("vector", (lambda e, i=qsb[:, hc], tb=RC[:, ttok]:
                                         e.tensor_tensor(
                            out=t1s[:], in0=i, in1=tb, op=ALU.mult)))
                        ep.op("vector", (lambda e, i=qsw[:, hc], tb=RS[:, ttok]:
                                         e.tensor_tensor(
                            out=t2s[:], in0=i, in1=tb, op=ALU.mult)))
                        ep.op("vector", (lambda e: e.tensor_tensor(
                            out=qf[0:64, :], in0=t1s[0:64, :], in1=t2s[:],
                            op=ALU.add)))
                        ep.op("vector", (lambda e: e.tensor_copy(
                            out=qf[64:128, :], in_=t1s[64:128, :])))
                        ep.op("vector", (lambda e, o=ohi[:, hc]:
                                         e.tensor_copy(out=o, in_=qf[:])))
                        ep.op("vector", (lambda e, o=olo[:, hc], b=ohi[:, hc]:
                                         e.tensor_tensor(out=o, in0=qf[:], in1=b,
                                                         op=ALU.subtract)), VE)
                    ep.w("gpsimd", VE)
                    for hh in range(HG):
                        h = hg * HG + hh
                        hc = slice(hh * TT, (hh + 1) * TT)
                        ep.op("gpsimd", _dma(d["k1h"][s, h, :, ttok], ohi[:, hc]),
                              ST, 16)
                        ep.op("gpsimd", _dma(d["k1l"][s, h, :, ttok], olo[:, hc]),
                              ST, 16)
                    st_hist.append(ST.v)

                for hg in range(2):
                    _khg(hg)

                def _vch(cc):
                    c = tt * 4 + cc
                    cs = slice(cc * CH, (cc + 1) * CH)
                    ep.w("tensor", LD, ld_done)
                    ep.w("tensor", SE)
                    ep.w("tensor", VE)
                    for ih in range(2):
                        half = (slice(0, TT), slice(TT, D))[ih]
                        pp = (ps2, ps3)[ih]
                        for fc in range(NF):
                            _mm3(ep, cx, pp[:], xnh[sl][fc][:, cs],
                                 xnl[sl][fc][:, cs],
                                 wvh[fc][:, half], wvl[fc][:, half],
                                 start=(fc == 0), stop=(fc == NF - 1), inc=TE)
                    ep.w("scalar", TE)
                    ep.w("scalar", ST)
                    for h in range(H):
                        pp = (ps2, ps3)[h // 4]
                        pcol = slice((h % 4) * 128, (h % 4 + 1) * 128)
                        ep.op("scalar", (lambda e, o=stA[:, h * 132:h * 132 + 128],
                                         i=pp[:, pcol]: e.copy(out=o, in_=i)), SE)
                    ep.w("vector", SE)
                    ep.w("vector", ST)
                    for h in range(H):
                        pp = (ps2, ps3)[h // 4]
                        pcol = slice((h % 4) * 128, (h % 4 + 1) * 128)
                        ep.op("vector", (lambda e, o=stB[:, h * 128:(h + 1) * 128],
                                         i=pp[:, pcol],
                                         b=stA[:, h * 132:h * 132 + 128]:
                                         e.tensor_tensor(out=o, in0=i, in1=b,
                                                         op=ALU.subtract)), VE)
                    ep.w("gpsimd", SE)
                    ep.w("gpsimd", VE)
                    ep.op("gpsimd", _dma(d["v1a"][s, :, :, c, :], stA[:]), ST, 16)
                    ep.op("gpsimd", _dma(d["v1b"][s, :, :, c, :], stB[:]), ST, 16)

                for cc in range(4):
                    _vch(cc)
                rel.append(SE.v)
            ep.w("gpsimd", ST)
            ep.run("p6")


def _p7_attn1(cx):
    nc = cx.nc
    d = cx.d
    LD, LD2, ST, TE, VE, SE = (cx.sem(n) for n in
                               ("LD", "LD2", "ST", "TE", "VE", "SE"))
    p = cx.p67
    RC, RS, w8, ident, bcols, cmsk = (p["RC"], p["RS"], p["w8"], p["ident"],
                                      p["bcols"], p["cmsk"])
    with ExitStack() as es7:
        wqh_ = es7.enter_context(nc.sbuf_tensor("p7_wqh", [128, NF * D], BF16))
        wql_ = es7.enter_context(nc.sbuf_tensor("p7_wql", [128, NF * D], BF16))
        woh_ = es7.enter_context(nc.sbuf_tensor("p7_woh", [128, NF * D], BF16))
        wol_ = es7.enter_context(nc.sbuf_tensor("p7_wol", [128, NF * D], BF16))
        wqh = [wqh_[:, fc * D:(fc + 1) * D] for fc in range(NF)]
        wql = [wql_[:, fc * D:(fc + 1) * D] for fc in range(NF)]
        woh = [woh_[:, fc * D:(fc + 1) * D] for fc in range(NF)]
        wol = [wol_[:, fc * D:(fc + 1) * D] for fc in range(NF)]
        with ExitStack() as es7b:
            stage_ = es7b.enter_context(
                nc.sbuf_tensor("p7_stage", [128, NF * D], F32))
            stage = [stage_[:, fc * D:(fc + 1) * D] for fc in range(NF)]
            ep = EP(cx)
            _emit_weight_pair(ep, cx, d["wq1"], stage, wqh, wql)
            _emit_weight_pair(ep, cx, d["wo1"], stage, woh, wol)
            ep.run("p7w")
        with ExitStack() as es:
            xn_ = es.enter_context(nc.sbuf_tensor("p7_xn", [128, NF * TT * 2], BF16))
            x1b = es.enter_context(nc.sbuf_tensor("p7_x1", [128, 2 * TT], F32))
            qsb = es.enter_context(nc.sbuf_tensor("p7_qsb", [128, 2048], F32))
            qsw = es.enter_context(nc.sbuf_tensor("p7_qsw", [64, 2048], F32))
            t1s = es.enter_context(nc.sbuf_tensor("p7_t1", [128, TT], F32))
            t2s = es.enter_context(nc.sbuf_tensor("p7_t2", [64, TT], F32))
            qf = es.enter_context(nc.sbuf_tensor("p7_qf", [128, TT], F32))
            qph = es.enter_context(nc.sbuf_tensor("p7_qph", [128, H * TT], BF16))
            qpl = es.enter_context(nc.sbuf_tensor("p7_qpl", [128, H * TT], BF16))
            x8_ = es.enter_context(nc.sbuf_tensor("p7_x8", [128, NF * TT], I8))
            xq32 = es.enter_context(nc.sbuf_tensor("p7_xq", [128, TT], F32))
            kp = es.enter_context(nc.sbuf_tensor("p7_kp", [128, S * 2], BF16))
            va = es.enter_context(nc.sbuf_tensor("p7_va", [128, NCH * 132], BF16))
            vb = es.enter_context(nc.sbuf_tensor("p7_vb", [128, NCH * 128], BF16))
            p32_ = es.enter_context(nc.sbuf_tensor("p7_p32", [128, TT * 2], F32))
            phb_ = es.enter_context(nc.sbuf_tensor("p7_ph", [128, TT * 3], BF16))
            plb_ = es.enter_context(nc.sbuf_tensor("p7_pl", [128, TT * 3], BF16))
            rcol = es.enter_context(nc.sbuf_tensor("p7_rc", [128, 8], F32))
            oht = es.enter_context(nc.sbuf_tensor("p7_oht", [128, 128], F32))
            osb = es.enter_context(nc.sbuf_tensor("p7_osb", [128, 132], F32))
            ofh = es.enter_context(nc.sbuf_tensor("p7_ofh", [128, H * TT], BF16))
            ofl = es.enter_context(nc.sbuf_tensor("p7_ofl", [128, H * TT], BF16))
            f16_ = es.enter_context(nc.sbuf_tensor("p7_f16", [128, TT], I16))
            outs = es.enter_context(nc.sbuf_tensor("p7_out", [128, TT], F32))
            pS0 = es.enter_context(nc.psum_tensor("p7_pS0", [128, TT], F32))
            pS1 = es.enter_context(nc.psum_tensor("p7_pS1", [128, TT], F32))
            pS2 = es.enter_context(nc.psum_tensor("p7_pS2", [128, TT], F32))
            pOa = es.enter_context(nc.psum_tensor("p7_pOa", [128, 512], F32))
            pOb = es.enter_context(nc.psum_tensor("p7_pOb", [128, 512], F32))
            pT = es.enter_context(nc.psum_tensor("p7_pT", [128, 128], F32))
            pW = es.enter_context(nc.psum_tensor("p7_pW", [128, TT], F32))
            pF = es.enter_context(nc.psum_tensor("p7_pF", [128, TT], F32))
            xnh = [xn_[:, (2 * fc) * TT:(2 * fc + 1) * TT] for fc in range(NF)]
            xnl = [xn_[:, (2 * fc + 1) * TT:(2 * fc + 2) * TT] for fc in range(NF)]
            x8 = [x8_.bitcast(F8)[:, fc * TT:(fc + 1) * TT] for fc in range(NF)]
            f16 = f16_.bitcast(F16)
            phb = [phb_[:, i * TT:(i + 1) * TT] for i in range(3)]
            plb = [plb_[:, i * TT:(i + 1) * TT] for i in range(3)]
            p32b = [p32_[:, i * TT:(i + 1) * TT] for i in range(2)]
            pSb = [pS0, pS1, pS2]
            ep = EP(cx)
            HG = 4
            te_fin = [0]
            qg_se = [0, 0]
            kvw = []
            for s, qb in product(range(BPC), range(NQB)):
                nk = 4 * (qb + 1)
                tok = slice(s * S + qb * TT, s * S + (qb + 1) * TT)
                ttok = slice(qb * TT, (qb + 1) * TT)
                ep.w("sync", VE)
                ep.w("sync", TE)
                for fc in range(NF):
                    ep.op("sync", _dma(xnh[fc], d["xn1h"][fc, :, tok]), LD, 16)
                    ep.op("sync", _dma(xnl[fc], d["xn1l"][fc, :, tok]), LD, 16)
                ld_done = LD.v

                def _qhg(hg):
                    ep.w("tensor", LD, ld_done)
                    for hh in range(HG):
                        h = hg * HG + hh
                        ocol = slice(h * 128, (h + 1) * 128)
                        pq = (pW, pF)[hh % 2]
                        if hg == 0 and hh < 2:
                            ep.w("tensor", SE)
                            ep.w("tensor", VE)
                        elif qg_se[hh % 2]:
                            ep.w("tensor", SE, qg_se[hh % 2])
                        for fc in range(NF):
                            _mm3(ep, cx, pq[:], wqh[fc][:, ocol],
                                 wql[fc][:, ocol], xnh[fc], xnl[fc],
                                 start=(fc == 0), stop=(fc == NF - 1), inc=TE)
                        ep.w("scalar", TE)
                        if hh == 0:
                            ep.w("scalar", VE)
                        ep.op("scalar", (lambda e, o=qsb[:, hh * TT:(hh + 1) * TT],
                                         i=pq[:]: e.copy(out=o, in_=i)), SE)
                        qg_se[hh % 2] = SE.v
                    ep.op("scalar", _dma(qsw[0:32, :], qsb[32:64, :]), LD2, 16)
                    ep.op("scalar", _dma(qsw[32:64, :], qsb[0:32, :]), LD2, 16)
                    ep.w("vector", SE)
                    ep.w("vector", LD2)
                    ep.w("vector", TE, te_fin[0])
                    for hh in range(HG):
                        h = hg * HG + hh
                        hc = slice(hh * TT, (hh + 1) * TT)
                        qc = slice(h * TT, (h + 1) * TT)
                        ep.op("vector", (lambda e, i=qsb[:, hc], tb=RC[:, ttok]:
                                         e.tensor_tensor(
                            out=t1s[:], in0=i, in1=tb, op=ALU.mult)))
                        ep.op("vector", (lambda e, i=qsw[:, hc], tb=RS[:, ttok]:
                                         e.tensor_tensor(
                            out=t2s[:], in0=i, in1=tb, op=ALU.mult)))
                        ep.op("vector", (lambda e: e.tensor_tensor(
                            out=qf[0:64, :], in0=t1s[0:64, :], in1=t2s[:],
                            op=ALU.add)))
                        ep.op("vector", (lambda e: e.tensor_copy(
                            out=qf[64:128, :], in_=t1s[64:128, :])))
                        ep.op("vector", (lambda e, o=qph[:, qc]:
                                         e.tensor_copy(out=o, in_=qf[:])))
                        ep.op("vector", (lambda e, o=qpl[:, qc], b=qph[:, qc]:
                                         e.tensor_tensor(out=o, in0=qf[:], in1=b,
                                                         op=ALU.subtract)), VE)

                for hg in range(2):
                    _qhg(hg)
                for fc in range(NF):
                    if fc > 0:
                        ep.w("vector", SE)
                    ep.op("vector", (lambda e, a=xnh[fc], b=xnl[fc]:
                                     e.tensor_tensor(out=xq32[:], in0=a, in1=b,
                                                     op=ALU.add)), VE)
                    ep.w("scalar", VE)
                    ep.op("scalar", (lambda e, o=x8[fc]: e.copy(
                        out=o, in_=xq32[:])), SE)
                q_done = VE.v
                x8_done = SE.v

                def _flash(h):
                    qc = slice(h * TT, (h + 1) * TT)
                    ep.w("vector", SE, x8_done)
                    if kvw:
                        ep.w("sync", TE, kvw[-1][0])
                        ep.w("sync", VE, kvw[-1][1])
                    else:
                        ep.w("sync", TE)
                        ep.w("sync", VE)
                    ep.op("sync", _dma(kp[:, 0:nk * CH],
                                       d["k1h"][s, h, :, 0:nk * CH]), LD, 16)
                    ep.op("sync", _dma(kp[:, S:S + nk * CH],
                                       d["k1l"][s, h, :, 0:nk * CH]), LD, 16)
                    ep.op("sync", _dma(va[:, 0:nk * 132],
                                       d["v1a"][s, :, h, 0:nk, :]), LD, 16)
                    ep.op("sync", _dma(vb[:, 0:nk * CH],
                                       d["v1b"][s, :, h, 0:nk, :]), LD, 16)
                    ldh = LD.v
                    ep.w("tensor", LD, ldh)
                    ep.w("tensor", VE)
                    first_a = [True]
                    first_b = [True]
                    pvals = {}
                    tvals = {}

                    def _st(kt):
                        pS = pSb[kt % 3]
                        if kt >= 3:
                            ep.w("tensor", VE, pvals[kt - 3])
                        kc = slice(kt * CH, (kt + 1) * CH)
                        _mm3(ep, cx, pS[:], kp[:, kc],
                             kp[:, S + kt * CH:S + (kt + 1) * CH],
                             qph[:, qc], qpl[:, qc], start=True, stop=True,
                             inc=TE)

                    def _pb(kt):
                        pS = pSb[kt % 3]
                        sl = kt % 3
                        s2 = kt % 2
                        if kt >= 3:
                            ep.w("vector", TE, tvals[kt - 3])
                        if kt >= 2:
                            ep.w("scalar", VE, pvals[kt - 2])
                        o = kt - 4 * qb
                        if o >= 0:
                            ep.w("vector", TE)
                            ep.op("vector", (lambda e, i=pS[:],
                                             m=cmsk[:, o * TT:(o + 1) * TT]:
                                             e.tensor_tensor(out=xq32[:], in0=i,
                                                             in1=m,
                                                             op=ALU.subtract)), VE)
                            ep.w("scalar", VE)
                            ep.op("scalar", (lambda e, o2=p32b[s2]: e.activation(
                                out=o2, in_=xq32[:], func=ACT.Exp)), SE)
                        else:
                            ep.w("scalar", TE)
                            ep.op("scalar", (lambda e, o2=p32b[s2], i=pS[:]:
                                             e.activation(out=o2, in_=i,
                                                          func=ACT.Exp)), SE)
                        ep.w("vector", SE)
                        ep.op("vector", (lambda e, o2=phb[sl], i=p32b[s2]:
                                         e.tensor_copy(out=o2, in_=i)))
                        ep.op("vector", (lambda e, o2=plb[sl], i=p32b[s2],
                                         b=phb[sl]: e.tensor_tensor(
                            out=o2, in0=i, in1=b, op=ALU.subtract)), VE)
                        pvals[kt] = VE.v

                    def _pv(kt, last_k=False):
                        sl = kt % 3
                        ep.w("tensor", VE, pvals[kt])
                        for qs in range(4):
                            qcc = slice(qs * 128, (qs + 1) * 128)
                            po = (pOa, pOb)[qs // 2]
                            base = (qs % 2) * 132
                            oc = slice(base, base + 129)
                            oc3 = slice(base, base + 128)
                            st = first_a[0] if qs < 2 else first_b[0]
                            if qs < 2:
                                first_a[0] = False
                            else:
                                first_b[0] = False
                            ep.op("tensor", (lambda e, o3=po[:, oc],
                                             l=phb[sl][:, qcc],
                                             r=va[:, kt * 132:kt * 132 + 129],
                                             s0=st: e.matmul(
                                out=o3, lhsT=l, rhs=r, start=s0, stop=False)))
                            ep.op("tensor", (lambda e, o3=po[:, oc],
                                             l=plb[sl][:, qcc],
                                             r=va[:, kt * 132:kt * 132 + 129]:
                                             e.matmul(out=o3, lhsT=l, rhs=r,
                                                      start=False, stop=False)))
                            ep.op("tensor", (lambda e, o3=po[:, oc3],
                                             l=phb[sl][:, qcc],
                                             r=vb[:, kt * CH:(kt + 1) * CH],
                                             s1=(last_k and qs == 3):
                                             e.matmul(out=o3, lhsT=l, rhs=r,
                                                      start=False, stop=s1)),
                                TE if qs == 3 else None)
                        tvals[kt] = TE.v

                    _st(0)
                    _pb(0)
                    _st(1)
                    _pb(1)
                    for kt in range(2, nk):
                        _st(kt)
                        _pb(kt)
                        _pv(kt - 2)
                    _pv(nk - 2)
                    _pv(nk - 1, last_k=True)
                    kvw.append((TE.v, VE.v))
                    if P7_DBG and s == 0 and qb == 0 and h == 0:
                        ep.w("vector", TE)
                        ep.op("vector", (lambda e: e.tensor_copy(
                            out=outs[:, 0:264], in_=pOa[:, 0:264])))
                        ep.op("vector", (lambda e: e.tensor_copy(
                            out=outs[:, 264:528], in_=pOb[:, 0:264])), VE)
                        ep.w("gpsimd", VE)
                        ep.op("gpsimd", _dma(d["dbg"][:, 0:528], outs[:, 0:528]),
                              ST, 16)
                        ep.w("gpsimd", ST)
                    for qs in range(4):
                        po = (pOa, pOb)[qs // 2]
                        base = (qs % 2) * 132
                        ep.w("vector", TE)
                        ep.op("vector", (lambda e, i=po[:, base:base + 132]:
                                         e.tensor_copy(out=osb[:], in_=i)), VE)
                        ep.w("scalar", VE)
                        ep.op("scalar", lambda e: e.nop(), SE)
                        ep.w("vector", SE)
                        ep.op("vector", (lambda e:
                                         e.reciprocal(out=rcol[:, 0:1],
                                                      in_=osb[:, 128:129])), VE)
                        ep.w("scalar", VE)
                        ep.op("scalar", (lambda e:
                                         e.activation(out=oht[:],
                                                      in_=osb[:, 0:128],
                                                      func=ACT.Copy,
                                                      scale=rcol[:, 0:1])), SE)
                        ep.w("vector", SE)
                        ep.op("vector", lambda e: e.nop(), VE)
                        if P7_DBG and s == 0 and qb == 0 and h == 0:
                            ep.w("gpsimd", VE)
                            ep.op("gpsimd", _dma(d["dbg"][:, 528 + qs * 128:
                                                          528 + (qs + 1) * 128],
                                                 oht[:]), ST, 16)
                            ep.op("gpsimd", _dma(d["dbg"][:, 1040 + qs * 8:
                                                          1048 + qs * 8],
                                                 rcol[:]), ST, 16)
                            ep.op("gpsimd", _dma(d["dbg"][:, 1072 + qs * 132:
                                                          1204 + qs * 132],
                                                 osb[:]), ST, 16)
                            ep.w("gpsimd", ST)
                            ep.w("vector", ST)
                        ep.w("tensor", VE)
                        ep.w("tensor", SE)
                        ep.op("tensor", (lambda e: e.transpose(
                            out=pT[:], in_=oht[:], identity=ident[:])), TE)
                        ep.w("scalar", TE)
                        if h == 0 and qs == 0:
                            ep.w("scalar", TE, te_fin[0])
                        ep.op("scalar", (lambda e,
                                         o3=ofh[:, h * TT + qs * 128:
                                                h * TT + (qs + 1) * 128]:
                                         e.copy(out=o3, in_=pT[:])), SE)
                        ep.w("vector", SE)
                        ep.op("vector", (lambda e,
                                         o3=ofl[:, h * TT + qs * 128:
                                                h * TT + (qs + 1) * 128],
                                         b=ofh[:, h * TT + qs * 128:
                                               h * TT + (qs + 1) * 128]:
                                         e.tensor_tensor(out=o3, in0=pT[:], in1=b,
                                                         op=ALU.subtract)), VE)

                for h in range(H):
                    _flash(h)
                of_done = VE.v

                fin_ve = []

                def _fin(mc):
                    mcol = slice(mc * 128, (mc + 1) * 128)
                    if mc >= 2:
                        ep.w("sync", VE, fin_ve[mc - 2])
                    ep.op("sync", _dma(x1b[:, (mc % 2) * TT:(mc % 2 + 1) * TT],
                                       d["x1"][mc, :, tok]), LD, 16)
                    ld_fin = LD.v
                    ep.w("tensor", VE, of_done)
                    ep.w("tensor", SE)
                    ep.w("tensor", VE)
                    for fc in range(NF):
                        _mm3(ep, cx, pW[:], woh[fc][:, mcol], wol[fc][:, mcol],
                             ofh[:, fc * TT:(fc + 1) * TT],
                             ofl[:, fc * TT:(fc + 1) * TT],
                             start=(fc == 0), stop=(fc == NF - 1), inc=TE)
                    for fc in range(NF):
                        ep.op("tensor", (lambda e, l=w8[fc][:, mcol], r=x8[fc],
                                         s0=(fc == 0), s1=(fc == NF - 1):
                                         e.matmul(out=pF[:], lhsT=l, rhs=r,
                                                  start=s0, stop=s1)),
                              TE if fc == NF - 1 else None)
                    ep.w("scalar", TE)
                    ep.op("scalar", (lambda e: e.copy(out=f16[:], in_=pF[:])), SE)
                    ep.w("vector", SE)
                    ep.w("vector", ST)
                    ep.w("vector", LD, ld_fin)
                    ob = outs[:, 0:TT]
                    if P7_MODE == 1:
                        ep.op("vector", (lambda e, o3=ob, b=bcols[:, mc:mc + 1]:
                                         e.tensor_scalar(out=o3, in0=f16[:],
                                                         scalar1=b, scalar2=None,
                                                         op0=ALU.add)))
                    elif P7_MODE == 2:
                        ep.op("vector", (lambda e, o3=ob: e.tensor_copy(
                            out=o3, in_=pW[:])))
                    else:
                        ep.op("vector", (lambda e, o3=ob, b=bcols[:, mc:mc + 1]:
                                         e.scalar_tensor_tensor(out=o3, in0=pW[:],
                                                                scalar=b, in1=f16[:],
                                                                op0=ALU.add,
                                                                op1=ALU.add)))
                    ep.op("vector", (lambda e, o3=ob,
                                     x=x1b[:, (mc % 2) * TT:(mc % 2 + 1) * TT]:
                                     e.tensor_tensor(out=o3, in0=o3, in1=x,
                                                     op=ALU.add)), VE)
                    fin_ve.append(VE.v)
                    ep.w("gpsimd", VE)
                    ep.op("gpsimd", _dma(d["y_out"][mc, :, tok], ob), ST, 16)

                for mc in range(NF):
                    _fin(mc)
                te_fin[0] = TE.v
            ep.w("gpsimd", ST)
            ep.run("p7")


def _build(cx):
    nc = cx.nc
    d = cx.d
    _p0_norm0(cx)
    _p12_qk0(cx)
    _p34_vg0(cx)
    _pf12_mix(cx)
    _pf3_ffn0(cx)
    _p5_ret0(cx)
    with ExitStack() as _es67:
        nc2 = cx.nc
        cx.p67 = {
            "RC": _es67.enter_context(nc2.sbuf_tensor("p67_rc", [128, S], F32)),
            "RS": _es67.enter_context(nc2.sbuf_tensor("p67_rs", [64, S], F32)),
            "ident": _es67.enter_context(nc2.sbuf_tensor("p67_id", [128, 128], F32)),
            "bcols": _es67.enter_context(nc2.sbuf_tensor("p67_bc", [128, NF], F32)),
            "cmsk": _es67.enter_context(nc2.sbuf_tensor("p67_cm", [128, 4 * TT], F32)),
        }
        _w8i = _es67.enter_context(nc2.sbuf_tensor("p67_w8", [128, NF * D], I8))
        cx.p67["w8"] = [_w8i.bitcast(F8)[:, fc * D:(fc + 1) * D]
                        for fc in range(NF)]
        _p6_kv1(cx)
        _p7_attn1(cx)


def build_nc(dbg_in=(), dbg_out=(), phases=None):
    nc = bass.Bass("TRN2", target_bir_lowering=False)
    cx = Ctx(nc)
    d = cx.d

    def dram(name, shape, dtype=F32):
        if name in dbg_in:
            kind = "ExternalInput"
        elif name in dbg_out:
            kind = "ExternalOutput"
        else:
            kind = "Internal"
        d[name] = nc.dram_tensor(name, list(shape), dtype, kind=kind).ap()

    for name, shape in [
        ("x_in", (NF, 128, T)),
        ("wq0", (NF, 128, D)), ("wk0", (NF, 128, D)), ("wv0", (NF, 128, D)),
        ("wg0", (NF, 128, D)), ("wo0", (NF, 128, D)),
        ("wr0", (NF, 128, D)), ("wk0f", (NF, 128, D)), ("wv0f", (NF, 128, D)),
        ("gnw_row", (1, D)), ("gnb_row", (1, D)),
        ("mur_col", (NF, 128, 1)), ("muk_col", (NF, 128, 1)),
        ("wq1", (NF, 128, D)), ("wk1", (NF, 128, D)), ("wv1", (NF, 128, D)),
        ("wo1", (NF, 128, D)), ("wf1", (NF, 128, D)),
        ("bias1_col", (NF, 128, 1)),
        ("tbl_xpos", (4, 128, S)), ("tbl_dm", (H, 128, S + 384)),
        ("tbl_cmsk", (4, 128, TT)), ("tbl_ropek", (2, 32, S)),
        ("tbl_pos32", (32, S)), ("tbl_iota32", (32, 1)),
        ("tbl_ident", (128, 128)),
    ]:
        d[name] = nc.dram_tensor(name, list(shape), F32, kind="ExternalInput").ap()

    dram("xn0h", (NF, 128, T), BF16)
    dram("xn0l", (NF, 128, T), BF16)
    dram("q0h", (BPC, H, 128, S), BF16)
    dram("q0l", (BPC, H, 128, S), BF16)
    dram("k0h", (BPC, H, 128, S), BF16)
    dram("k0l", (BPC, H, 128, S), BF16)
    dram("v0h", (BPC, H, 128, S), BF16)
    dram("v0l", (BPC, H, 128, S), BF16)
    dram("g0", (BPC, NCH, 128, D), F32)
    dram("r0", (NF, 128, T), F32)
    dram("kah", (NF, 128, T), BF16)
    dram("kal", (NF, 128, T), BF16)
    dram("f0", (NF, 128, T), F32)
    dram("x1", (NF, 128, T), F32)
    dram("xn1h", (NF, 128, T), BF16)
    dram("xn1l", (NF, 128, T), BF16)
    dram("k1h", (BPC, H, 128, S), BF16)
    dram("k1l", (BPC, H, 128, S), BF16)
    dram("v1a", (BPC, 128, H, NCH, 132), BF16)
    dram("v1b", (BPC, 128, H, NCH, 128), BF16)

    d["y_out"] = nc.dram_tensor("y_out", [NF, 128, T], F32,
                                kind="ExternalOutput").ap()
    d["dbg"] = nc.dram_tensor("dbg", [128, 2 * S + 2048], F32,
                              kind="ExternalOutput" if (P6_DBG or P7_DBG) else "Internal").ap()

    for name in ("LD", "LD2", "ST", "TE", "VE", "SE", "GP",
                 "FR0", "FR1", "FR2", "FR3", "FR4", "FR5"):
        h = nc.semaphore(name)
        cx.sems[name] = Sem(h.__enter__())

    _build(cx)
    return nc, cx


def kernel(**inputs):
    tables = _host_tables()
    nc, _ = build_nc()
    in_maps = [_prep_core_inputs(inputs, c, tables) for c in range(NCORE)]
    res = run_bass_kernel_spmd(nc, in_maps, core_ids=list(range(NCORE)))
    outs = []
    for c in range(NCORE):
        y = res.results[c]["y_out"]
        yt = np.transpose(y, (2, 0, 1)).reshape(BPC, S, D)
        outs.append(yt)
    return np.concatenate(outs, 0).astype(np.float32)


# version 06 (final)
# speedup vs baseline: 1.3724x; 1.0412x over previous
import math
from collections import defaultdict
from itertools import product
from contextlib import ExitStack

import numpy as np

import concourse.bass as bass
import concourse.mybir as mybir
from concourse.bass_utils import run_bass_kernel_spmd

F32 = mybir.dt.float32
BF16 = mybir.dt.bfloat16
F8 = mybir.dt.float8e4
F16 = mybir.dt.float16
I8 = mybir.dt.int8
I16 = mybir.dt.int16
ACT = mybir.ActivationFunctionType
ALU = mybir.AluOpType
AX = mybir.AxisListType

PI = math.pi
CLP = 3.1415925

B, S, D = 16, 2048, 1024
H, HD = 8, 128
NCORE = 8
BPC = B // NCORE
T = BPC * S
NF = D // 128
TT = 512
NTT = T // TT
NQB = S // TT
CH = 128
NCH = S // CH
RINV = 128.0 ** (-0.5)

ENGINES = ("sync", "gpsimd", "tensor", "vector", "scalar")
P12_NOROT = False
P6_DBG = False
P6_NOROT = False
P6_ROTMODE = 0
P7_MODE = 0
P7_DBG = False



def _host_tables():
    import jax
    import jax.numpy as jnp
    cpu = jax.devices("cpu")[0]
    out = {}
    with jax.default_device(cpu):
        d = 128
        scale_vec = (jnp.arange(0, d, 2, dtype=jnp.float32) + 0.4 * d) / (1.4 * d)
        pos = jnp.arange(S, dtype=jnp.float32)
        scale = scale_vec[None, :] ** (pos[:, None] / 512.0)
        half = d // 2
        inv_freq = 1.0 / 10000.0 ** (jnp.arange(half, dtype=jnp.float32) / half)
        ang = pos[:, None] * inv_freq[None, :]
        sin, cos = jnp.sin(ang), jnp.cos(ang)
        cq = np.asarray((cos * scale).T)
        sq = np.asarray((sin * scale).T)
        inv_scale = 1.0 / scale
        ck = np.asarray((cos * inv_scale).T)
        sk = np.asarray((sin * inv_scale).T)
        xp = np.empty((4, 128, S), np.float32)
        for i, (c, s_) in enumerate(((cq, sq), (ck, sk))):
            xp[2 * i, 0:64] = c
            xp[2 * i, 64:128] = c
            xp[2 * i + 1, 0:64] = -s_
            xp[2 * i + 1, 64:128] = s_
        out["tbl_xpos"] = xp

        gam = 1.0 - jnp.exp(
            jnp.linspace(math.log(1.0 / 32), math.log(1.0 / 512), H)
        ).astype(jnp.float32)
        kl = jnp.arange(CH)[:, None]
        x = jnp.arange(-384, S)[None, :]
        diff = x - kl
        mask = diff >= 0
        dmat = []
        for h in range(H):
            dh = jnp.where(
                mask, gam[h] ** jnp.where(mask, diff, 0).astype(jnp.float32), 0.0
            )
            dmat.append(np.asarray(dh, np.float32))
        out["tbl_dm"] = np.stack(dmat)

    i = np.arange(128)[:, None]
    j = np.arange(TT)[None, :]
    cm = np.stack(
        [(128 * o + i > j).astype(np.float32) * np.float32(1e30) for o in range(4)]
    )
    out["tbl_cmsk"] = cm

    dims = np.arange(32)
    invf = np.power(10000.0, -dims / 32.0)
    rad = np.arange(S)[None, :] * invf[:, None]
    ks = np.rint(rad / (2 * np.pi))
    kc = np.rint((rad + np.pi / 2) / (2 * np.pi))
    for name, kk, base in (("ks", ks, rad), ("kc", kc, rad + np.pi / 2)):
        r = base - kk * (2 * np.pi)
        assert np.abs(r).max() < np.pi + 1e-9, name
    out["tbl_ropek"] = np.stack([ks, kc]).astype(np.float32)
    out["tbl_pos32"] = np.broadcast_to(
        np.arange(S, dtype=np.float32)[None, :], (32, S)
    ).copy()
    out["tbl_iota32"] = np.arange(32, dtype=np.float32)[:, None].copy()
    out["tbl_ident"] = np.eye(128, dtype=np.float32)
    return out


def _prep_core_inputs(ins, core, tables):
    m = {}
    x = np.asarray(ins["L0_attn_X"][2 * core:2 * core + 2])
    xt = np.transpose(x.reshape(T, D).reshape(T, NF, 128), (1, 2, 0))
    m["x_in"] = np.ascontiguousarray(xt)

    perm = np.concatenate([np.arange(0, 128, 2), np.arange(1, 128, 2)])

    def fm_weight(w):
        return np.ascontiguousarray(w.reshape(NF, 128, -1))

    wq = np.asarray(ins["L0_attn_W_Q"])[:, :, perm]
    wq = np.transpose(wq, (1, 0, 2)).reshape(D, D)
    m["wq0"] = fm_weight(wq)
    wk = np.asarray(ins["L0_attn_W_K"])[:, :, perm]
    wk = np.transpose(wk, (1, 0, 2)).reshape(D, D)
    m["wk0"] = fm_weight(wk)
    wv = np.transpose(np.asarray(ins["L0_attn_W_V"]), (1, 0, 2)).reshape(D, D)
    m["wv0"] = fm_weight(wv)
    m["wg0"] = fm_weight(np.asarray(ins["L0_attn_W_G"]))
    m["wo0"] = fm_weight(np.asarray(ins["L0_attn_W_O"]))
    m["wr0"] = fm_weight(np.ascontiguousarray(np.asarray(ins["L0_ffn_W_r"]).T))
    m["wk0f"] = fm_weight(np.ascontiguousarray(np.asarray(ins["L0_ffn_W_k"]).T))
    m["wv0f"] = fm_weight(np.ascontiguousarray(np.asarray(ins["L0_ffn_W_v"]).T))
    m["gnw_row"] = np.asarray(ins["L0_attn_gn_weight"]).reshape(1, D)
    m["gnb_row"] = np.asarray(ins["L0_attn_gn_bias"]).reshape(1, D)
    m["mur_col"] = np.asarray(ins["L0_ffn_mu_r"]).reshape(NF, 128, 1)
    m["muk_col"] = np.asarray(ins["L0_ffn_mu_k"]).reshape(NF, 128, 1)

    m["wq1"] = fm_weight(np.asarray(ins["L1_attn_wq"]).reshape(D, D))
    m["wk1"] = fm_weight(np.asarray(ins["L1_attn_wk"]).reshape(D, D))
    m["wv1"] = fm_weight(np.asarray(ins["L1_attn_wv"]).reshape(D, D))
    m["wo1"] = fm_weight(np.asarray(ins["L1_attn_wo"]).reshape(D, D))
    m["wf1"] = fm_weight(np.ascontiguousarray(np.asarray(ins["L1_ffn_weight"]).T))
    m["bias1_col"] = np.asarray(ins["L1_ffn_bias"]).reshape(NF, 128, 1)
    m.update(tables)
    return {k: np.ascontiguousarray(np.asarray(v, np.float32)) for k, v in m.items()}



class Sem:
    def __init__(self, handle):
        self.h = handle
        self.v = 0

    def inc(self, n=1):
        self.v += n
        return self.v


class EP:

    def __init__(self, cx):
        self.cx = cx
        self.L = defaultdict(list)
        self._lastw = {}

    def w(self, eng, sem, val=None):
        if val is None:
            val = sem.v
        key = (eng, id(sem))
        if self._lastw.get(key, -1) >= val:
            return
        self._lastw[key] = val
        self.L[eng].append(("w", sem.h, val))

    def op(self, eng, fn, inc=None, n=1):
        if inc is not None:
            inc.inc(n)
        self.L[eng].append(("o", fn, None if inc is None else (inc.h, n)))

    def run(self, name):
        cx = self.cx

        def mk(items):
            def body(eng):
                for it in items:
                    if it[0] == "w":
                        eng.wait_ge(it[1], it[2])
                    else:
                        r = it[1](eng)
                        if it[2] is not None:
                            r.then_inc(it[2][0], it[2][1])
            return body

        with cx.nc.Block(name) as blk:
            for e in ENGINES:
                if self.L[e]:
                    getattr(blk, e)(mk(self.L[e]))


class Ctx:
    def __init__(self, nc):
        self.nc = nc
        self.d = {}
        self.sems = {}
        self._stack = None

    def sem(self, name):
        return self.sems[name]


def _dma(dst, src):
    return lambda e: e.dma_start(out=dst, in_=src)


def _emit_weight_pair(ep, cx, dram, stage, hi, lo, nchunk=NF, width=D):
    LD, SE, VE = cx.sem("LD"), cx.sem("SE"), cx.sem("VE")
    ep.w("sync", VE)
    for fc in range(nchunk):
        ep.op("sync", _dma(stage[fc][:, 0:width], dram[fc, :, 0:width]), LD, 16)
    ep.w("scalar", LD)
    for fc in range(nchunk):
        ep.op("scalar",
              (lambda e, a=hi[fc], b=stage[fc]: e.copy(out=a[:, 0:width], in_=b[:, 0:width])),
              SE)
    ep.w("vector", SE)
    for fc in range(nchunk):
        ep.op("vector",
              (lambda e, a=lo[fc], b=stage[fc], c=hi[fc]: e.tensor_tensor(
                  out=a[:, 0:width], in0=b[:, 0:width], in1=c[:, 0:width],
                  op=ALU.subtract)),
              VE)


def _mm3(ep, cx, ps, wh, wl, ah, al, start, stop, inc=None):
    TE = cx.sem("TE")
    ep.op("tensor", (lambda e, o=ps, l=wh, r=ah, s=start: e.matmul(
        out=o, lhsT=l, rhs=r, start=s, stop=False)))
    ep.op("tensor", (lambda e, o=ps, l=wl, r=ah: e.matmul(
        out=o, lhsT=l, rhs=r, start=False, stop=False)))
    ep.op("tensor", (lambda e, o=ps, l=wh, r=al, s=stop: e.matmul(
        out=o, lhsT=l, rhs=r, start=False, stop=s)), inc if stop else None)



def _p0_norm0(cx):
    nc = cx.nc
    d = cx.d
    LD, ST, TE, VE, SE, FR = (cx.sem(n) for n in
                              ("LD", "ST", "TE", "VE", "SE", "FR0"))
    with (
        nc.sbuf_tensor("p0_xin", [128, NF * TT * 2], F32) as xin_all,
        nc.sbuf_tensor("p0_xsq", [128, TT * 2], F32) as xsq,
        nc.sbuf_tensor("p0_ones", [128, 1], F32) as ones_col,
        nc.sbuf_tensor("p0_onesr", [1, 128], F32) as ones_row,
        nc.sbuf_tensor("p0_rows", [1, TT * 4], F32) as rows,
        nc.sbuf_tensor("p0_tmp", [128, TT * 2], F32) as tmp,
        nc.sbuf_tensor("p0_tf", [128, TT * 2], F32) as tfull,
        nc.sbuf_tensor("p0_oh", [128, TT * 2], BF16) as ohi,
        nc.sbuf_tensor("p0_ol", [128, TT * 2], BF16) as olo,
        nc.psum_tensor("p0_psA", [128, TT], F32) as psA,
        nc.psum_tensor("p0_psB", [128, TT], F32) as psB,
        nc.psum_tensor("p0_psM", [128, TT], F32) as psM,
        nc.psum_tensor("p0_psS", [128, TT], F32) as psS,
    ):
        ep = EP(cx)
        ep.op("vector", lambda e: e.memset(ones_col[:], 1.0), VE)
        ep.op("vector", lambda e: e.memset(ones_row[:], 1.0), VE)
        xin = [
            [xin_all[:, (s * NF + fc) * TT:(s * NF + fc + 1) * TT]
             for fc in range(NF)] for s in range(2)
        ]
        mrow = rows[0:1, 0:TT]
        m2row = rows[0:1, TT:2 * TT]
        erow = rows[0:1, 2 * TT:3 * TT]
        srow = rows[0:1, 3 * TT:4 * TT]
        for tt in range(NTT):
            sl = tt % 2
            tok = slice(tt * TT, (tt + 1) * TT)
            if tt >= 2:
                ep.w("sync", FR, tt - 1)
            for fc in range(NF):
                ep.op("sync", _dma(xin[sl][fc], d["x_in"][fc, :, tok]), LD, 16)
            ep.w("tensor", LD)
            ep.w("tensor", VE)
            ep.w("tensor", SE)
            ep.w("scalar", LD)
            for fc in range(NF):
                ep.op("tensor", (lambda e, o=psA[0:1, :], l=ones_col[:], r=xin[sl][fc],
                                 s=(fc == 0), p=(fc == NF - 1): e.matmul(
                    out=o, lhsT=l, rhs=r, start=s, stop=p)))
                xq = xsq[:, (fc % 2) * TT:(fc % 2 + 1) * TT]
                if fc >= 2:
                    ep.w("scalar", TE, TE.v)
                ep.op("scalar", (lambda e, o=xq, i=xin[sl][fc]: e.activation(
                    out=o, in_=i, func=ACT.Square)), SE)
                ep.w("tensor", SE)
                ep.op("tensor", (lambda e, o=psB[0:1, :], l=ones_col[:], r=xq,
                                 s=(fc == 0), p=(fc == NF - 1): e.matmul(
                    out=o, lhsT=l, rhs=r, start=s, stop=p)), TE)
            ep.w("scalar", TE)
            ep.op("scalar", (lambda e: e.activation(out=mrow, in_=psA[0:1, :],
                                                    func=ACT.Copy, scale=1.0 / 1024)))
            ep.op("scalar", (lambda e: e.activation(out=m2row, in_=mrow,
                                                    func=ACT.Square)))
            ep.op("scalar", (lambda e: e.activation(out=erow, in_=psB[0:1, :],
                                                    func=ACT.Copy, scale=1.0 / 1024)), SE)
            ep.w("vector", SE)
            ep.op("vector", (lambda e: e.tensor_tensor(out=erow, in0=erow,
                                                       in1=m2row, op=ALU.subtract)))
            ep.op("vector", (lambda e: e.tensor_scalar(out=erow, in0=erow,
                                                       scalar1=1e-5, scalar2=None,
                                                       op0=ALU.add)), VE)
            ep.w("scalar", VE)
            ep.op("scalar", (lambda e: e.activation(out=m2row, in_=erow,
                                                    func=ACT.Sqrt)), SE)
            ep.w("vector", SE)
            ep.op("vector", (lambda e: e.reciprocal(out=srow, in_=m2row)), VE)
            ep.w("tensor", VE)
            ep.w("tensor", SE)
            ep.op("tensor", (lambda e: e.matmul(out=psM[:], lhsT=ones_row[:],
                                                rhs=mrow, start=True, stop=True)))
            ep.op("tensor", (lambda e: e.matmul(out=psS[:], lhsT=ones_row[:],
                                                rhs=srow, start=True, stop=True)), TE)
            ep.w("vector", TE)
            for fc in range(NF):
                tp = tmp[:, (fc % 2) * TT:(fc % 2 + 1) * TT]
                tf = tfull[:, (fc % 2) * TT:(fc % 2 + 1) * TT]
                oh = ohi[:, (fc % 2) * TT:(fc % 2 + 1) * TT]
                ol = olo[:, (fc % 2) * TT:(fc % 2 + 1) * TT]
                if fc >= 2:
                    ep.w("vector", ST, ST.v)
                ep.op("vector", (lambda e, o=tp, i=xin[sl][fc]: e.tensor_tensor(
                    out=o, in0=i, in1=psM[:], op=ALU.subtract)),
                    FR if fc == NF - 1 else None)
                ep.op("vector", (lambda e, o=tf, i=tp: e.tensor_tensor(
                    out=o, in0=i, in1=psS[:], op=ALU.mult)), VE)
                ep.w("scalar", VE)
                ep.op("scalar", (lambda e, o=oh, i=tf: e.copy(out=o, in_=i)), SE)
                ep.w("vector", SE)
                ep.op("vector", (lambda e, o=ol, a=tf, b=oh: e.tensor_tensor(
                    out=o, in0=a, in1=b, op=ALU.subtract)), VE)
                ep.w("gpsimd", VE)
                ep.op("gpsimd", _dma(d["xn0h"][fc, :, tok], oh), ST, 16)
                ep.op("gpsimd", _dma(d["xn0l"][fc, :, tok], ol), ST, 16)
        ep.w("gpsimd", ST)
        ep.run("p0")


def _p12_qk0(cx):
    nc = cx.nc
    d = cx.d
    LD, LD2, ST, TE, VE, SE = (cx.sem(n) for n in
                               ("LD", "LD2", "ST", "TE", "VE", "SE"))
    HG = 4
    W = HG * TT
    with (
        nc.sbuf_tensor("p12_wqh", [128, NF * D], BF16) as _wqh,
        nc.sbuf_tensor("p12_wql", [128, NF * D], BF16) as _wql,
        nc.sbuf_tensor("p12_wkh", [128, NF * D], BF16) as _wkh,
        nc.sbuf_tensor("p12_wkl", [128, NF * D], BF16) as _wkl,
    ):
        wqh = [_wqh[:, fc * D:(fc + 1) * D] for fc in range(NF)]
        wql = [_wql[:, fc * D:(fc + 1) * D] for fc in range(NF)]
        wkh = [_wkh[:, fc * D:(fc + 1) * D] for fc in range(NF)]
        wkl = [_wkl[:, fc * D:(fc + 1) * D] for fc in range(NF)]
        with nc.sbuf_tensor("p12_stage", [128, NF * D], F32) as _stage:
            stage = [_stage[:, fc * D:(fc + 1) * D] for fc in range(NF)]
            ep = EP(cx)
            _emit_weight_pair(ep, cx, d["wq0"], stage, wqh, wql)
            _emit_weight_pair(ep, cx, d["wk0"], stage, wkh, wkl)
            ep.run("p12w")
        with (
            nc.sbuf_tensor("p12_tbl", [128, 4 * S], F32) as _tbl,
            nc.sbuf_tensor("p12_xn", [128, NF * TT * 4], BF16) as _xn,
            nc.sbuf_tensor("p12_qsb", [128, W], F32) as qsb,
            nc.sbuf_tensor("p12_qsw", [128, W], F32) as qsw,
            nc.sbuf_tensor("p12_t2", [128, TT], F32) as t2s,
            nc.sbuf_tensor("p12_t1", [128, TT], F32) as t1s,
            nc.sbuf_tensor("p12_qf", [128, TT], F32) as qf,
            nc.sbuf_tensor("p12_oh", [128, W], BF16) as ohi,
            nc.sbuf_tensor("p12_ol", [128, W], BF16) as olo,
            nc.psum_tensor("p12_ps0", [128, TT], F32) as ps0,
            nc.psum_tensor("p12_ps1", [128, TT], F32) as ps1,
        ):
            tbl = [_tbl[:, i * S:(i + 1) * S] for i in range(4)]
            ep = EP(cx)
            for i in range(4):
                ep.op("sync", _dma(tbl[i], d["tbl_xpos"][i]), LD, 16)
            xnh = [[_xn[:, ((sl * 2 + 0) * NF + fc) * TT:((sl * 2 + 0) * NF + fc + 1) * TT]
                    for fc in range(NF)] for sl in range(2)]
            xnl = [[_xn[:, ((sl * 2 + 1) * NF + fc) * TT:((sl * 2 + 1) * NF + fc + 1) * TT]
                    for fc in range(NF)] for sl in range(2)]
            psums = [ps0, ps1]
            bank_free = [0, 0]
            rel = []
            st_hist = []
            for s in range(BPC):
                for tt in range(NQB):
                    ti = s * NQB + tt
                    sl = ti % 2
                    tok = slice(s * S + tt * TT, s * S + (tt + 1) * TT)
                    ttok = slice(tt * TT, (tt + 1) * TT)
                    if ti >= 2:
                        ep.w("sync", SE, rel[ti - 2])
                    for fc in range(NF):
                        ep.op("sync", _dma(xnh[sl][fc], d["xn0h"][fc, :, tok]), LD, 16)
                        ep.op("sync", _dma(xnl[sl][fc], d["xn0l"][fc, :, tok]), LD, 16)
                    ld_done = LD.v
                    for p, (wh, wl, qdh, qdl, ctb, stb) in enumerate((
                            (wqh, wql, d["q0h"], d["q0l"], tbl[0], tbl[1]),
                            (wkh, wkl, d["k0h"], d["k0l"], tbl[2], tbl[3]))):
                        for hg in range(2):
                            ep.w("tensor", LD, ld_done)
                            for hh in range(HG):
                                h = hg * HG + hh
                                bk = hh % 2
                                ps = psums[bk]
                                ocol = slice(h * 128, (h + 1) * 128)
                                if bank_free[bk] > 0:
                                    ep.w("tensor", SE, bank_free[bk])
                                for fc in range(NF):
                                    _mm3(ep, cx, ps[:], wh[fc][:, ocol],
                                         wl[fc][:, ocol], xnh[sl][fc], xnl[sl][fc],
                                         start=(fc == 0), stop=(fc == NF - 1),
                                         inc=TE)
                                ep.w("scalar", TE)
                                if hh == 0:
                                    ep.w("scalar", VE)
                                ep.op("scalar", (lambda e, o=qsb[:, hh * TT:(hh + 1) * TT],
                                                 i=ps[:]: e.copy(out=o, in_=i)), SE)
                                bank_free[bk] = SE.v
                            ep.op("scalar", _dma(qsw[0:64, :], qsb[64:128, :]), LD2, 16)
                            ep.op("scalar", _dma(qsw[64:128, :], qsb[0:64, :]), LD2, 16)
                            ep.w("vector", SE)
                            ep.w("vector", LD2)
                            if st_hist:
                                ep.w("vector", ST, st_hist[-1])
                            if P12_NOROT:
                                for hh in range(HG):
                                    hc = slice(hh * TT, (hh + 1) * TT)
                                    ep.op("vector", (lambda e, o=ohi[:, hc], i=qsb[:, hc]:
                                                     e.tensor_copy(out=o, in_=i)))
                                    ep.op("vector", (lambda e, o=olo[:, hc], a=qsb[:, hc],
                                                     b=ohi[:, hc]: e.tensor_tensor(
                                        out=o, in0=a, in1=b, op=ALU.subtract)), VE)
                            else:
                                for hh in range(HG):
                                  h = hg * HG + hh
                                  hc = slice(hh * TT, (hh + 1) * TT)
                                  ep.op("vector", (lambda e, o=t1s[:], i=qsb[:, hc],
                                                   t=ctb[:, ttok]: e.tensor_tensor(
                                      out=o, in0=i, in1=t, op=ALU.mult)))
                                  ep.op("vector", (lambda e, o=t2s[:], i=qsw[:, hc],
                                                   t=stb[:, ttok]: e.tensor_tensor(
                                      out=o, in0=i, in1=t, op=ALU.mult)))
                                  ep.op("vector", (lambda e, o=ohi[:, hc]: e.tensor_tensor(
                                      out=o, in0=t1s[:], in1=t2s[:], op=ALU.add)))
                                  ep.op("vector", (lambda e, o=qf[:]: e.tensor_tensor(
                                      out=o, in0=t1s[:], in1=t2s[:], op=ALU.add)))
                                  ep.op("vector", (lambda e, o=olo[:, hc], a=qf[:],
                                                   b=ohi[:, hc]: e.tensor_tensor(
                                      out=o, in0=a, in1=b, op=ALU.subtract)), VE)
                            ep.w("gpsimd", VE)
                            for hh in range(HG):
                                h = hg * HG + hh
                                hc = slice(hh * TT, (hh + 1) * TT)
                                ep.op("gpsimd", _dma(qdh[s, h, :, ttok], ohi[:, hc]), ST, 16)
                                ep.op("gpsimd", _dma(qdl[s, h, :, ttok], olo[:, hc]), ST, 16)
                            st_hist.append(ST.v)
                    rel.append(SE.v)
            ep.w("gpsimd", ST)
            ep.run("p12")


def _p34_vg0(cx):
    nc = cx.nc
    d = cx.d
    LD, ST, TE, VE, SE = (cx.sem(n) for n in ("LD", "ST", "TE", "VE", "SE"))
    with (
        nc.sbuf_tensor("p34_wvh", [128, NF * D], BF16) as _wvh,
        nc.sbuf_tensor("p34_wvl", [128, NF * D], BF16) as _wvl,
        nc.sbuf_tensor("p34_wgh", [128, NF * D], BF16) as _wgh,
        nc.sbuf_tensor("p34_wgl", [128, NF * D], BF16) as _wgl,
    ):
        wvh = [_wvh[:, fc * D:(fc + 1) * D] for fc in range(NF)]
        wvl = [_wvl[:, fc * D:(fc + 1) * D] for fc in range(NF)]
        wgh = [_wgh[:, fc * D:(fc + 1) * D] for fc in range(NF)]
        wgl = [_wgl[:, fc * D:(fc + 1) * D] for fc in range(NF)]
        with nc.sbuf_tensor("p34_stage", [128, NF * D], F32) as _stage:
            stage = [_stage[:, fc * D:(fc + 1) * D] for fc in range(NF)]
            ep = EP(cx)
            _emit_weight_pair(ep, cx, d["wv0"], stage, wvh, wvl)
            _emit_weight_pair(ep, cx, d["wg0"], stage, wgh, wgl)
            ep.run("p34w")
        with (
            nc.sbuf_tensor("p34_xn", [128, NF * TT * 4], BF16) as _xn,
            nc.sbuf_tensor("p34_vh", [128, D], BF16) as vh,
            nc.sbuf_tensor("p34_vl", [128, D], BF16) as vl,
            nc.sbuf_tensor("p34_g", [128, D], F32) as gst,
            nc.psum_tensor("p34_ps0", [128, TT], F32) as ps0,
            nc.psum_tensor("p34_ps1", [128, TT], F32) as ps1,
            nc.psum_tensor("p34_ps2", [128, TT], F32) as ps2,
            nc.psum_tensor("p34_ps3", [128, TT], F32) as ps3,
        ):
            xnh = [[_xn[:, ((sl * 2 + 0) * NF + fc) * TT:((sl * 2 + 0) * NF + fc + 1) * TT]
                    for fc in range(NF)] for sl in range(2)]
            xnl = [[_xn[:, ((sl * 2 + 1) * NF + fc) * TT:((sl * 2 + 1) * NF + fc + 1) * TT]
                    for fc in range(NF)] for sl in range(2)]
            ps = [ps0, ps1, ps2, ps3]
            rel = []
            ep = EP(cx)
            for s, tt in product(range(BPC), range(NQB)):
                ti = s * NQB + tt
                sl = ti % 2
                tok = slice(s * S + tt * TT, s * S + (tt + 1) * TT)
                if ti >= 2:
                    ep.w("sync", TE, rel[ti - 2])
                for fc in range(NF):
                    ep.op("sync", _dma(xnh[sl][fc], d["xn0h"][fc, :, tok]), LD, 16)
                    ep.op("sync", _dma(xnl[sl][fc], d["xn0l"][fc, :, tok]), LD, 16)
                ld_done = LD.v
                for cc in range(4):
                    c = tt * 4 + cc
                    cs = slice(cc * CH, (cc + 1) * CH)
                    ep.w("tensor", LD, ld_done)
                    ep.w("tensor", SE)
                    ep.w("tensor", VE)
                    for ih, tgt in product(range(2), range(2)):
                        half = (slice(0, TT), slice(TT, D))[ih]
                        wh, wl = ((wvh, wvl), (wgh, wgl))[tgt]
                        pp = ps[tgt * 2 + ih]
                        for fc in range(NF):
                            _mm3(ep, cx, pp[:], xnh[sl][fc][:, cs],
                                 xnl[sl][fc][:, cs],
                                 wh[fc][:, half], wl[fc][:, half],
                                 start=(fc == 0), stop=(fc == NF - 1),
                                 inc=TE)
                    te_done = TE.v
                    ep.w("scalar", TE, te_done)
                    ep.w("scalar", ST)
                    for ih in range(2):
                        hw = slice(ih * TT, (ih + 1) * TT)
                        ep.op("scalar", (lambda e, o=vh[:, hw], i=ps[ih][:]:
                                         e.copy(out=o, in_=i)), SE)
                        ep.op("scalar", (lambda e, o=gst[:, hw], i=ps[2 + ih][:]:
                                         e.activation(out=o, in_=i,
                                                      func=ACT.Silu)), SE)
                    ep.w("vector", SE)
                    ep.w("vector", ST)
                    for ih in range(2):
                        hw = slice(ih * TT, (ih + 1) * TT)
                        ep.op("vector", (lambda e, o=vl[:, hw], a=ps[ih][:],
                                         b=vh[:, hw]: e.tensor_tensor(
                            out=o, in0=a, in1=b, op=ALU.subtract)), VE)
                    ep.w("gpsimd", SE)
                    ep.w("gpsimd", VE)
                    for h in range(H):
                        hc2 = slice(h * HD, (h + 1) * HD)
                        cc2 = slice(c * CH, (c + 1) * CH)
                        ep.op("gpsimd", _dma(d["v0h"][s, h, :, cc2], vh[:, hc2]),
                              ST, 16)
                        ep.op("gpsimd", _dma(d["v0l"][s, h, :, cc2], vl[:, hc2]),
                              ST, 16)
                    ep.op("gpsimd", _dma(d["g0"][s, c], gst[:]), ST, 16)
                rel.append(TE.v)
            ep.w("gpsimd", ST)
            ep.run("p34")


def _pf12_mix(cx):
    nc = cx.nc
    d = cx.d
    LD, ST, TE, VE, SE = (cx.sem(n) for n in ("LD", "ST", "TE", "VE", "SE"))
    TP = TT + 1
    with (
        nc.sbuf_tensor("pf_wrh", [128, NF * D], BF16) as _wrh,
        nc.sbuf_tensor("pf_wrl", [128, NF * D], BF16) as _wrl,
        nc.sbuf_tensor("pf_wkh", [128, NF * D], BF16) as _wkh,
        nc.sbuf_tensor("pf_wkl", [128, NF * D], BF16) as _wkl,
        nc.sbuf_tensor("pf_mu", [128, 16], F32) as mu,
    ):
        wrh = [_wrh[:, fc * D:(fc + 1) * D] for fc in range(NF)]
        wrl = [_wrl[:, fc * D:(fc + 1) * D] for fc in range(NF)]
        wkh = [_wkh[:, fc * D:(fc + 1) * D] for fc in range(NF)]
        wkl = [_wkl[:, fc * D:(fc + 1) * D] for fc in range(NF)]
        with nc.sbuf_tensor("pf_stage", [128, NF * D], F32) as _stage:
            stage = [_stage[:, fc * D:(fc + 1) * D] for fc in range(NF)]
            ep = EP(cx)
            for fc in range(NF):
                ep.op("sync", _dma(mu[:, fc:fc + 1], d["mur_col"][fc]), LD, 16)
                ep.op("sync", _dma(mu[:, 8 + fc:9 + fc], d["muk_col"][fc]), LD, 16)
            _emit_weight_pair(ep, cx, d["wr0"], stage, wrh, wrl)
            _emit_weight_pair(ep, cx, d["wk0f"], stage, wkh, wkl)
            ep.run("pfw")
        with (
            nc.sbuf_tensor("pf_xn", [128, NF * TP * 4], BF16) as _xn,
            nc.sbuf_tensor("pf_xfz", [128, TP], F32) as xfz,
            nc.sbuf_tensor("pf_xf", [128, TP], F32) as xf,
            nc.sbuf_tensor("pf_dd", [128, TT], F32) as dd,
            nc.sbuf_tensor("pf_m32", [128, TT], F32) as m32,
            nc.sbuf_tensor("pf_xr", [128, NF * TT * 2], BF16) as _xr,
            nc.sbuf_tensor("pf_xk", [128, NF * TT * 2], BF16) as _xk,
            nc.sbuf_tensor("pf_t", [128, TT], F32) as tfu,
            nc.sbuf_tensor("pf_oh", [128, TT], BF16) as koh,
            nc.sbuf_tensor("pf_ol", [128, TT], BF16) as kol,
            nc.sbuf_tensor("pf_r", [128, TT], F32) as rst,
            nc.psum_tensor("pf_ps0", [128, TT], F32) as ps0,
            nc.psum_tensor("pf_ps1", [128, TT], F32) as ps1,
        ):
            xnh = [[_xn[:, ((sl * 2 + 0) * NF + fc) * TP:((sl * 2 + 0) * NF + fc + 1) * TP]
                    for fc in range(NF)] for sl in range(2)]
            xnl = [[_xn[:, ((sl * 2 + 1) * NF + fc) * TP:((sl * 2 + 1) * NF + fc + 1) * TP]
                    for fc in range(NF)] for sl in range(2)]
            xrh = [_xr[:, (2 * fc) * TT:(2 * fc + 1) * TT] for fc in range(NF)]
            xrl = [_xr[:, (2 * fc + 1) * TT:(2 * fc + 2) * TT] for fc in range(NF)]
            xkh = [_xk[:, (2 * fc) * TT:(2 * fc + 1) * TT] for fc in range(NF)]
            xkl = [_xk[:, (2 * fc + 1) * TT:(2 * fc + 2) * TT] for fc in range(NF)]
            psums = [ps0, ps1]
            rel = []
            bank_se = [0, 0]
            ep = EP(cx)
            ep.op("vector", lambda e: e.memset(xfz[:], 0.0), VE)
            for s, tt in product(range(BPC), range(NQB)):
                    ti = s * NQB + tt
                    sl = ti % 2
                    t0 = s * S + tt * TT
                    tok = slice(t0, t0 + TT)
                    if ti >= 2:
                        ep.w("sync", TE, rel[ti - 2])
                    for fc, (dr, buf) in product(range(NF),
                                                 ((d["xn0h"], xnh), (d["xn0l"], xnl))):
                        if tt == 0:
                            ep.op("sync", _dma(buf[sl][fc][:, 1:TP],
                                               dr[fc, :, tok]), LD, 16)
                        else:
                            ep.op("sync", _dma(buf[sl][fc],
                                               dr[fc, :, t0 - 1:t0 + TT]), LD, 16)
                    ld_done = LD.v
                    ep.w("vector", LD, ld_done)
                    if ti >= 1:
                        ep.w("vector", TE, rel[ti - 1])
                    for fc in range(NF):
                        xfb = xfz if tt == 0 else xf
                        if tt == 0:
                            ep.op("vector", (lambda e, a=xnh[sl][fc], b=xnl[sl][fc]:
                                             e.tensor_tensor(out=xfz[:, 1:TP],
                                                             in0=a[:, 1:TP],
                                                             in1=b[:, 1:TP],
                                                             op=ALU.add)))
                        else:
                            ep.op("vector", (lambda e, a=xnh[sl][fc], b=xnl[sl][fc]:
                                             e.tensor_tensor(out=xf[:], in0=a, in1=b,
                                                             op=ALU.add)))
                        ep.op("vector", (lambda e, xx=xfb: e.tensor_tensor(
                            out=dd[:], in0=xx[:, 1:TP], in1=xx[:, 0:TT],
                            op=ALU.subtract)))
                        for (mc, hh, ll) in ((mu[:, fc:fc + 1], xrh, xrl),
                                             (mu[:, 8 + fc:9 + fc], xkh, xkl)):
                            ep.op("vector", (lambda e, m=mc, xx=xfb: e.scalar_tensor_tensor(
                                out=m32[:], in0=dd[:], scalar=m, in1=xx[:, 0:TT],
                                op0=ALU.mult, op1=ALU.add)), VE)
                            ep.w("scalar", VE)
                            ep.op("scalar", (lambda e, o=hh[fc]: e.copy(
                                out=o, in_=m32[:])), SE)
                            ep.w("vector", SE)
                            ep.op("vector", (lambda e, o=ll[fc], b=hh[fc]:
                                             e.tensor_tensor(out=o, in0=m32[:],
                                                             in1=b,
                                                             op=ALU.subtract)), VE)
                    mix_done = VE.v
                    for tgt, oc in product(range(2), range(NF)):
                        pp = psums[oc % 2]
                        ocol = slice(oc * 128, (oc + 1) * 128)
                        ep.w("tensor", VE, mix_done)
                        if bank_se[oc % 2]:
                            ep.w("tensor", SE, bank_se[oc % 2])
                        ww = ((wrh, wrl), (wkh, wkl))[tgt]
                        aa = ((xrh, xrl), (xkh, xkl))[tgt]
                        for fc in range(NF):
                            _mm3(ep, cx, pp[:], ww[0][fc][:, ocol],
                                 ww[1][fc][:, ocol], aa[0][fc], aa[1][fc],
                                 start=(fc == 0), stop=(fc == NF - 1),
                                 inc=TE)
                        ep.w("scalar", TE)
                        ep.w("scalar", ST)
                        if tgt == 0:
                            ep.op("scalar", (lambda e, i=pp[:]: e.activation(
                                out=rst[:], in_=i, func=ACT.Sigmoid)), SE)
                            bank_se[oc % 2] = SE.v
                            ep.w("gpsimd", SE)
                            ep.op("gpsimd", _dma(d["r0"][oc, :, tok], rst[:]),
                                  ST, 16)
                        else:
                            ep.op("scalar", (lambda e, i=pp[:]: e.activation(
                                out=tfu[:], in_=i, func=ACT.Relu)), SE)
                            bank_se[oc % 2] = SE.v
                            ep.w("vector", SE)
                            ep.op("vector", (lambda e: e.tensor_tensor(
                                out=m32[:], in0=tfu[:], in1=tfu[:],
                                op=ALU.mult)), VE)
                            ep.w("scalar", VE)
                            ep.op("scalar", (lambda e: e.copy(
                                out=koh[:], in_=m32[:])), SE)
                            ep.w("vector", SE)
                            ep.op("vector", (lambda e: e.tensor_tensor(
                                out=kol[:], in0=m32[:], in1=koh[:],
                                op=ALU.subtract)), VE)
                            ep.w("gpsimd", VE)
                            ep.op("gpsimd", _dma(d["kah"][oc, :, tok], koh[:]),
                                  ST, 16)
                            ep.op("gpsimd", _dma(d["kal"][oc, :, tok], kol[:]),
                                  ST, 16)
                    rel.append(TE.v)
            ep.w("gpsimd", ST)
            ep.run("pf12")


def _pf3_ffn0(cx):
    nc = cx.nc
    d = cx.d
    LD, ST, TE, VE, SE = (cx.sem(n) for n in ("LD", "ST", "TE", "VE", "SE"))
    with (
        nc.sbuf_tensor("p3_wh", [128, NF * D], BF16) as _wh,
        nc.sbuf_tensor("p3_wl", [128, NF * D], BF16) as _wl,
    ):
        wh = [_wh[:, fc * D:(fc + 1) * D] for fc in range(NF)]
        wl = [_wl[:, fc * D:(fc + 1) * D] for fc in range(NF)]
        with nc.sbuf_tensor("p3_stage", [128, NF * D], F32) as _stage:
            stage = [_stage[:, fc * D:(fc + 1) * D] for fc in range(NF)]
            ep = EP(cx)
            _emit_weight_pair(ep, cx, d["wv0f"], stage, wh, wl)
            ep.run("p3w")
        with (
            nc.sbuf_tensor("p3_ka", [128, NF * TT * 4], BF16) as _ka,
            nc.sbuf_tensor("p3_r", [128, NF * TT * 2], F32) as _r,
            nc.sbuf_tensor("p3_o", [128, TT * 2], F32) as ost,
            nc.psum_tensor("p3_ps0", [128, TT], F32) as ps0,
            nc.psum_tensor("p3_ps1", [128, TT], F32) as ps1,
        ):
            kah = [[_ka[:, ((sl * 2 + 0) * NF + fc) * TT:((sl * 2 + 0) * NF + fc + 1) * TT]
                    for fc in range(NF)] for sl in range(2)]
            kal = [[_ka[:, ((sl * 2 + 1) * NF + fc) * TT:((sl * 2 + 1) * NF + fc + 1) * TT]
                    for fc in range(NF)] for sl in range(2)]
            rti = [[_r[:, (sl * NF + oc) * TT:(sl * NF + oc + 1) * TT]
                    for oc in range(NF)] for sl in range(2)]
            psums = [ps0, ps1]
            rel = []
            bank_ve = [0, 0]
            ep = EP(cx)
            for s in range(BPC):
                for tt in range(NQB):
                    ti = s * NQB + tt
                    sl = ti % 2
                    tok = slice(s * S + tt * TT, s * S + (tt + 1) * TT)
                    if ti >= 2:
                        ep.w("sync", VE, rel[ti - 2])
                    for fc in range(NF):
                        ep.op("sync", _dma(kah[sl][fc], d["kah"][fc, :, tok]), LD, 16)
                        ep.op("sync", _dma(kal[sl][fc], d["kal"][fc, :, tok]), LD, 16)
                        ep.op("sync", _dma(rti[sl][fc], d["r0"][fc, :, tok]), LD, 16)
                    ld_done = LD.v
                    for oc in range(NF):
                        pp = psums[oc % 2]
                        ocol = slice(oc * 128, (oc + 1) * 128)
                        ep.w("tensor", LD, ld_done)
                        if bank_ve[oc % 2]:
                            ep.w("tensor", VE, bank_ve[oc % 2])
                        for fc in range(NF):
                            _mm3(ep, cx, pp[:], wh[fc][:, ocol], wl[fc][:, ocol],
                                 kah[sl][fc], kal[sl][fc],
                                 start=(fc == 0), stop=(fc == NF - 1), inc=TE)
                        ep.w("vector", TE)
                        ep.w("vector", ST)
                        ob = ost[:, (oc % 2) * TT:(oc % 2 + 1) * TT]
                        ep.op("vector", (lambda e, o=ob, a=pp[:], r=rti[sl][oc]:
                                         e.tensor_tensor(out=o, in0=a, in1=r,
                                                         op=ALU.mult)), VE)
                        bank_ve[oc % 2] = VE.v
                        ep.w("gpsimd", VE)
                        ep.op("gpsimd", _dma(d["f0"][oc, :, tok], ob), ST, 16)
                    rel.append(VE.v)
            ep.w("gpsimd", ST)
            ep.run("pf3")


def _p5_ret0(cx):
    nc = cx.nc
    d = cx.d
    LD, LD2, ST, TE, VE, SE, GP = (cx.sem(n) for n in
                                   ("LD", "LD2", "ST", "TE", "VE", "SE", "GP"))
    DMW = S + 384
    with (
        nc.sbuf_tensor("p5_woh", [128, NF * D], BF16) as _woh,
        nc.sbuf_tensor("p5_wol", [128, NF * D], BF16) as _wol,
        nc.sbuf_tensor("p5_gnw", [128, D], F32) as gnwB,
        nc.sbuf_tensor("p5_gnb", [128, D], F32) as gnbB,
        nc.sbuf_tensor("p5_ident", [128, 128], F32) as ident,
        nc.sbuf_tensor("p5_ones", [128, 1], F32) as ones_col,
        nc.sbuf_tensor("p5_onesr", [1, 128], F32) as ones_row,
        nc.sbuf_tensor("p5_rows", [1, 6 * TT], F32) as rows,
    ):
        woh = [_woh[:, fc * D:(fc + 1) * D] for fc in range(NF)]
        wol = [_wol[:, fc * D:(fc + 1) * D] for fc in range(NF)]
        with (nc.sbuf_tensor("p5_stage", [128, NF * D], F32) as _stage,
              nc.psum_tensor("p5_bc", [128, TT], F32) as psbc):
            stage = [_stage[:, fc * D:(fc + 1) * D] for fc in range(NF)]
            ep = EP(cx)
            _emit_weight_pair(ep, cx, d["wo0"], stage, woh, wol)
            ep.op("sync", _dma(ident[:], d["tbl_ident"]), LD, 16)
            ep.op("sync", _dma(rows[0:1, 0:D], d["gnw_row"]), LD, 16)
            ep.op("sync", _dma(rows[0:1, D:2 * D], d["gnb_row"]), LD, 16)
            ep.op("vector", lambda e: e.memset(ones_col[:], 1.0), VE)
            ep.op("vector", lambda e: e.memset(ones_row[:], 1.0), VE)
            ep.w("tensor", LD)
            ep.w("tensor", VE)
            for i, dst in enumerate((gnwB, gnbB)):
                for hw in range(2):
                    ep.op("tensor", (lambda e, r=rows[0:1, i * D + hw * TT:
                                                      i * D + (hw + 1) * TT]:
                                     e.matmul(out=psbc[:], lhsT=ones_row[:],
                                              rhs=r, start=True, stop=True)), TE)
                    ep.w("vector", TE)
                    ep.op("vector", (lambda e, o=dst[:, hw * TT:(hw + 1) * TT]:
                                     e.tensor_copy(out=o, in_=psbc[:])), VE)
                    ep.w("tensor", VE)
            ep.run("p5w")
        with ExitStack() as _es:
            _qb = _es.enter_context(nc.sbuf_tensor("p5_q", [128, TT * 4], BF16))
            _kb = _es.enter_context(nc.sbuf_tensor("p5_k", [128, S * 4], BF16))
            _vb = _es.enter_context(nc.sbuf_tensor("p5_v", [128, S * 4], BF16))
            _dmb = _es.enter_context(nc.sbuf_tensor("p5_dm", [128, DMW * 2], F32))
            gbuf = _es.enter_context(nc.sbuf_tensor("p5_g", [128, 4 * D], F32))
            fbuf = _es.enter_context(nc.sbuf_tensor("p5_f0", [128, 2 * TT], F32))
            xbuf = _es.enter_context(nc.sbuf_tensor("p5_x", [128, 2 * TT], F32))
            _ph = _es.enter_context(nc.sbuf_tensor("p5_ph", [128, TT * 2], BF16))
            _pl = _es.enter_context(nc.sbuf_tensor("p5_pl", [128, TT * 2], BF16))
            _p32 = _es.enter_context(nc.sbuf_tensor("p5_p32", [128, TT * 2], F32))
            sqt = _es.enter_context(nc.sbuf_tensor("p5_sq", [128, TT], F32))
            cols = _es.enter_context(nc.sbuf_tensor("p5_cols", [128, 32], F32))
            ynt = _es.enter_context(nc.sbuf_tensor("p5_yn", [128, 128], F32))
            gyt = _es.enter_context(nc.sbuf_tensor("p5_gy", [128, 128], F32))
            _gyh = _es.enter_context(nc.sbuf_tensor("p5_gyh", [128, NF * TT], BF16))
            _gyl = _es.enter_context(nc.sbuf_tensor("p5_gyl", [128, NF * TT], BF16))
            x1s = _es.enter_context(nc.sbuf_tensor("p5_x1", [128, NF * TT], F32))
            xn1h_s = _es.enter_context(nc.sbuf_tensor("p5_xnh", [128, NF * TT], BF16))
            xn1l_s = _es.enter_context(nc.sbuf_tensor("p5_xnl", [128, NF * TT], BF16))
            tmp = _es.enter_context(nc.sbuf_tensor("p5_tmp", [128, TT], F32))
            pS0 = _es.enter_context(nc.psum_tensor("p5_pS0", [128, TT], F32))
            pS1 = _es.enter_context(nc.psum_tensor("p5_pS1", [128, TT], F32))
            pY0 = _es.enter_context(nc.psum_tensor("p5_pY0", [128, TT], F32))
            pY1 = _es.enter_context(nc.psum_tensor("p5_pY1", [128, TT], F32))
            pT = _es.enter_context(nc.psum_tensor("p5_pT", [128, 128], F32))
            pW = _es.enter_context(nc.psum_tensor("p5_pW", [128, TT], F32))
            psA = _es.enter_context(nc.psum_tensor("p5_pA", [128, TT], F32))
            psB = _es.enter_context(nc.psum_tensor("p5_pB", [128, TT], F32))
            qh = [_qb[:, i * TT:(i + 1) * TT] for i in range(4)]
            kbs = [_kb[:, i * S:(i + 1) * S] for i in range(4)]
            vbs = [_vb[:, i * S:(i + 1) * S] for i in range(4)]
            dmb = [_dmb[:, i * DMW:(i + 1) * DMW] for i in range(2)]
            phb = [_ph[:, i * TT:(i + 1) * TT] for i in range(2)]
            plb = [_pl[:, i * TT:(i + 1) * TT] for i in range(2)]
            p32b = [_p32[:, i * TT:(i + 1) * TT] for i in range(2)]
            gyh = [_gyh[:, fc * TT:(fc + 1) * TT] for fc in range(NF)]
            gyl = [_gyl[:, fc * TT:(fc + 1) * TT] for fc in range(NF)]
            mrow = rows[0:1, 2 * D:2 * D + TT]
            m2row = rows[0:1, 2 * D + TT:2 * D + 2 * TT]
            pSb = [pS0, pS1]
            pYb = [pY0, pY1]
            ep = EP(cx)
            yfree = [0, 0]
            gyfree = [0]
            hfree = []
            hseq = [0]
            prev_gy = [0]
            for s, qb in product(range(BPC), range(NQB)):
                nk = 4 * (qb + 1)
                tok = slice(s * S + qb * TT, s * S + (qb + 1) * TT)
                stok = slice(qb * TT, (qb + 1) * TT)
                ep.w("sync", VE, prev_gy[0])
                for cc in range(4):
                    ep.op("sync", _dma(gbuf[:, cc * D:(cc + 1) * D],
                                       d["g0"][s, qb * 4 + cc]), LD, 16)
                def _head(h):
                    hs = hseq[0] % 2
                    if len(hfree) >= 2:
                        ep.w("sync", TE, hfree[-2][0])
                        ep.w("sync", VE, hfree[-2][1])
                    qhh, qhl = qh[2 * hs], qh[2 * hs + 1]
                    kbh, kbl = kbs[2 * hs], kbs[2 * hs + 1]
                    vbh, vbl = vbs[2 * hs], vbs[2 * hs + 1]
                    dmt = dmb[hs]
                    ep.op("sync", _dma(qhh, d["q0h"][s, h, :, stok]), LD, 16)
                    ep.op("sync", _dma(qhl, d["q0l"][s, h, :, stok]), LD, 16)
                    ep.op("sync", _dma(kbh[:, 0:nk * CH],
                                       d["k0h"][s, h, :, 0:nk * CH]), LD, 16)
                    ep.op("sync", _dma(kbl[:, 0:nk * CH],
                                       d["k0l"][s, h, :, 0:nk * CH]), LD, 16)
                    ep.op("sync", _dma(vbh[:, 0:nk * CH],
                                       d["v0h"][s, h, :, 0:nk * CH]), LD, 16)
                    ep.op("sync", _dma(vbl[:, 0:nk * CH],
                                       d["v0l"][s, h, :, 0:nk * CH]), LD, 16)
                    ep.op("sync", _dma(dmt[:, 0:384 + (qb + 1) * TT],
                                       d["tbl_dm"][h, :, 0:384 + (qb + 1) * TT]),
                          LD, 16)
                    ld_done = LD.v
                    pY = pYb[h % 2]
                    ep.w("tensor", LD, ld_done)
                    ep.w("tensor", VE, yfree[h % 2])
                    first_y = [True]
                    pvals = {}
                    tvals = {}

                    def _st(kt):
                        pS = pSb[kt % 2]
                        if kt >= 2:
                            ep.w("tensor", VE, pvals[kt - 2])
                        kc = slice(kt * CH, (kt + 1) * CH)
                        _mm3(ep, cx, pS[:], kbh[:, kc], kbl[:, kc],
                             qhh, qhl, start=True, stop=True, inc=TE)

                    def _pb(kt):
                        pS = pSb[kt % 2]
                        sl = kt % 2
                        x0 = 384 + qb * TT - kt * CH
                        if kt >= 2:
                            ep.w("vector", TE, tvals[kt - 2])
                        ep.w("vector", TE)
                        ep.op("vector", (lambda e, o=phb[sl], a=pS[:],
                                         m=dmt[:, x0:x0 + TT]: e.tensor_tensor(
                            out=o, in0=a, in1=m, op=ALU.mult)))
                        ep.op("vector", (lambda e, o=p32b[sl], a=pS[:],
                                         m=dmt[:, x0:x0 + TT]: e.tensor_tensor(
                            out=o, in0=a, in1=m, op=ALU.mult)))
                        ep.op("vector", (lambda e, o=plb[sl], a=p32b[sl],
                                         b=phb[sl]: e.tensor_tensor(
                            out=o, in0=a, in1=b, op=ALU.subtract)), VE)
                        pvals[kt] = VE.v

                    def _pv(kt, last_k=False):
                        sl = kt % 2
                        kc = slice(kt * CH, (kt + 1) * CH)
                        ep.w("tensor", VE, pvals[kt])
                        for qs in range(4):
                            qc = slice(qs * 128, (qs + 1) * 128)
                            st = first_y[0]
                            first_y[0] = False
                            ep.op("tensor", (lambda e, o=pY[:, qc], l=phb[sl][:, qc],
                                             r=vbh[:, kc], s0=st: e.matmul(
                                out=o, lhsT=l, rhs=r, start=s0, stop=False)))
                            ep.op("tensor", (lambda e, o=pY[:, qc], l=plb[sl][:, qc],
                                             r=vbh[:, kc]: e.matmul(
                                out=o, lhsT=l, rhs=r, start=False, stop=False)))
                            ep.op("tensor", (lambda e, o=pY[:, qc], l=phb[sl][:, qc],
                                             r=vbl[:, kc],
                                             s1=(last_k and qs == 3): e.matmul(
                                out=o, lhsT=l, rhs=r, start=False, stop=s1)),
                                TE if qs == 3 else None)
                        tvals[kt] = TE.v

                    _st(0)
                    _pb(0)
                    for kt in range(1, nk):
                        _st(kt)
                        _pb(kt)
                        _pv(kt - 1)
                    _pv(nk - 1, last_k=True)
                    hfree.append((TE.v, VE.v))
                    hseq[0] += 1
                    ep.w("scalar", TE)
                    ep.w("vector", TE)
                    for qs in range(4):
                        qc = slice(qs * 128, (qs + 1) * 128)
                        c0, c1, c2, c3, c4 = (cols[:, i:i + 1] for i in range(5))
                        ep.op("scalar", (lambda e, i=pY[:, qc]: e.activation(
                            out=sqt[:, 0:128], in_=i, func=ACT.Square)), SE)
                        ep.w("vector", SE)
                        ep.op("vector", (lambda e, i=pY[:, qc]: e.reduce_sum(
                            out=c0, in_=i, axis=AX.X)))
                        ep.op("vector", (lambda e: e.reduce_sum(
                            out=c1, in_=sqt[:, 0:128], axis=AX.X)), VE)
                        ep.w("scalar", VE)
                        ep.op("scalar", (lambda e: e.activation(
                            out=c2, in_=c0, func=ACT.Copy, scale=1.0 / 128)))
                        ep.op("scalar", (lambda e: e.activation(
                            out=c3, in_=c2, func=ACT.Square)))
                        ep.op("scalar", (lambda e: e.activation(
                            out=c4, in_=c1, func=ACT.Copy, scale=1.0 / 128)), SE)
                        ep.w("vector", SE)
                        ep.op("vector", (lambda e: e.tensor_tensor(
                            out=c4, in0=c4, in1=c3, op=ALU.subtract)))
                        ep.op("vector", (lambda e: e.tensor_scalar(
                            out=c4, in0=c4, scalar1=1e-5, scalar2=None,
                            op0=ALU.add)), VE)
                        ep.w("scalar", VE)
                        ep.op("scalar", (lambda e: e.activation(
                            out=c3, in_=c4, func=ACT.Sqrt)), SE)
                        ep.w("vector", SE)
                        ep.op("vector", (lambda e: e.reciprocal(out=c4, in_=c3)))
                        ep.op("vector", (lambda e, i=pY[:, qc]: e.tensor_scalar(
                            out=ynt[:], in0=i, scalar1=c2, scalar2=c4,
                            op0=ALU.subtract, op1=ALU.mult)))
                        ep.op("vector", (lambda e, w=gnwB[:, h * HD:(h + 1) * HD]:
                                         e.tensor_tensor(out=ynt[:], in0=ynt[:],
                                                         in1=w, op=ALU.mult)))
                        ep.op("vector", (lambda e, w=gnbB[:, h * HD:(h + 1) * HD]:
                                         e.tensor_tensor(out=ynt[:], in0=ynt[:],
                                                         in1=w, op=ALU.add)))
                        ep.op("vector", (lambda e, g=gbuf[:, qs * D + h * HD:
                                                          qs * D + (h + 1) * HD]:
                                         e.tensor_tensor(out=gyt[:], in0=ynt[:],
                                                         in1=g, op=ALU.mult)), VE)
                        ep.w("tensor", VE)
                        ep.w("tensor", SE)
                        ep.op("tensor", (lambda e: e.transpose(
                            out=pT[:], in_=gyt[:], identity=ident[:])), TE)
                        ep.w("scalar", TE)
                        if h == 0 and qs == 0:
                            ep.w("scalar", TE, gyfree[0])
                        ep.op("scalar", (lambda e, o=gyh[h][:, qc]: e.copy(
                            out=o, in_=pT[:])), SE)
                        ep.w("vector", SE)
                        ep.op("vector", (lambda e, o=gyl[h][:, qc], b=gyh[h][:, qc]:
                                         e.tensor_tensor(out=o, in0=pT[:], in1=b,
                                                         op=ALU.subtract)), VE)
                    yfree[h % 2] = VE.v

                for h in range(H):
                    _head(h)
                gy_done = VE.v
                prev_gy[0] = gy_done
                fx_ve = []
                for mc in range(NF):
                    mcol = slice(mc * 128, (mc + 1) * 128)
                    if mc >= 2:
                        ep.w("sync", VE, fx_ve[mc - 2])
                    mslot = slice((mc % 2) * TT, (mc % 2 + 1) * TT)
                    ep.op("sync", _dma(fbuf[:, mslot], d["f0"][mc, :, tok]),
                          LD, 16)
                    ep.op("sync", _dma(xbuf[:, mslot], d["x_in"][mc, :, tok]),
                          LD, 16)
                    ld_fx = LD.v
                    ep.w("tensor", VE, gy_done)
                    ep.w("tensor", SE)
                    if mc == 0:
                        ep.w("vector", ST)
                    for fc in range(NF):
                        _mm3(ep, cx, pW[:], woh[fc][:, mcol], wol[fc][:, mcol],
                             gyh[fc], gyl[fc], start=(fc == 0), stop=(fc == NF - 1),
                             inc=TE)
                    ep.w("vector", TE)
                    ep.w("vector", LD, ld_fx)
                    ep.op("vector", (lambda e, f=fbuf[:, mslot]:
                                     e.tensor_tensor(out=tmp[:], in0=pW[:], in1=f,
                                                     op=ALU.add)))
                    ep.op("vector", (lambda e, x=xbuf[:, mslot],
                                     o=x1s[:, mc * TT:(mc + 1) * TT]:
                                     e.tensor_tensor(out=o, in0=tmp[:], in1=x,
                                                     op=ALU.add)), VE)
                    fx_ve.append(VE.v)
                    ep.w("scalar", VE)
                    ep.op("scalar", (lambda e, i=x1s[:, mc * TT:(mc + 1) * TT]:
                                     e.activation(out=sqt[:], in_=i,
                                                  func=ACT.Square)), SE)
                    ep.w("tensor", VE)
                    ep.w("tensor", SE)
                    ep.op("tensor", (lambda e, r=x1s[:, mc * TT:(mc + 1) * TT],
                                     s0=(mc == 0), p0=(mc == NF - 1): e.matmul(
                        out=psA[0:1, :], lhsT=ones_col[:], rhs=r, start=s0,
                        stop=p0)))
                    ep.op("tensor", (lambda e, s0=(mc == 0), p0=(mc == NF - 1):
                                     e.matmul(
                        out=psB[0:1, :], lhsT=ones_col[:], rhs=sqt[:], start=s0,
                        stop=p0)), TE)
                ep.w("scalar", TE)
                ep.op("scalar", (lambda e: e.activation(out=mrow, in_=psA[0:1, :],
                                                        func=ACT.Copy,
                                                        scale=1.0 / 1024)))
                ep.op("scalar", (lambda e: e.activation(out=m2row, in_=mrow,
                                                        func=ACT.Square)), SE)
                ep.w("vector", SE)
                ep.op("vector", (lambda e: e.tensor_scalar(
                    out=m2row, in0=m2row, scalar1=-1.0, scalar2=None,
                    op0=ALU.mult)))
                ep.op("vector", (lambda e: e.scalar_tensor_tensor(
                    out=m2row, in0=psB[0:1, :], scalar=1.0 / 1024, in1=m2row,
                    op0=ALU.mult, op1=ALU.add)))
                ep.op("vector", (lambda e: e.tensor_scalar(
                    out=m2row, in0=m2row, scalar1=1e-5, scalar2=None,
                    op0=ALU.add)), VE)
                ep.w("scalar", VE)
                ep.op("scalar", (lambda e: e.activation(out=m2row, in_=m2row,
                                                        func=ACT.Sqrt)), SE)
                ep.w("vector", SE)
                ep.op("vector", (lambda e: e.reciprocal(out=m2row, in_=m2row)), VE)
                ep.w("tensor", VE)
                ep.w("tensor", SE)
                ep.op("tensor", (lambda e: e.matmul(out=psA[:], lhsT=ones_row[:],
                                                    rhs=mrow, start=True,
                                                    stop=True)))
                ep.op("tensor", (lambda e: e.matmul(out=psB[:], lhsT=ones_row[:],
                                                    rhs=m2row, start=True,
                                                    stop=True)), TE)
                ep.w("vector", TE)
                ep.w("vector", ST)
                for mc in range(NF):
                    mm = slice(mc * TT, (mc + 1) * TT)
                    ep.op("vector", (lambda e, x=x1s[:, mm]: e.tensor_tensor(
                        out=tmp[:], in0=x, in1=psA[:], op=ALU.subtract)))
                    ep.op("vector", (lambda e, o=sqt[:]: e.tensor_tensor(
                        out=o, in0=tmp[:], in1=psB[:], op=ALU.mult)), VE)
                    ep.w("scalar", VE)
                    ep.op("scalar", (lambda e, o=xn1h_s[:, mm]: e.copy(
                        out=o, in_=sqt[:])), SE)
                    ep.w("vector", SE)
                    ep.op("vector", (lambda e, o=xn1l_s[:, mm], b=xn1h_s[:, mm]:
                                     e.tensor_tensor(out=o, in0=sqt[:], in1=b,
                                                     op=ALU.subtract)), VE)
                gyfree[0] = TE.v
                ep.w("gpsimd", VE)
                for fc in range(NF):
                    mm2 = slice(fc * TT, (fc + 1) * TT)
                    ep.op("gpsimd", _dma(d["x1"][fc, :, tok], x1s[:, mm2]), ST, 16)
                    ep.op("gpsimd", _dma(d["xn1h"][fc, :, tok], xn1h_s[:, mm2]),
                          ST, 16)
                    ep.op("gpsimd", _dma(d["xn1l"][fc, :, tok], xn1l_s[:, mm2]),
                          ST, 16)
            ep.w("gpsimd", ST)
            ep.run("p5")


def _p6_kv1(cx):
    nc = cx.nc
    d = cx.d
    LD, LD2, ST, TE, VE, SE = (cx.sem(n) for n in
                               ("LD", "LD2", "ST", "TE", "VE", "SE"))
    p = cx.p67
    RC, RS, w8, ident, bcols, cmsk = (p["RC"], p["RS"], p["w8"], p["ident"],
                                      p["bcols"], p["cmsk"])
    with ExitStack() as es6:
        wkh_ = es6.enter_context(nc.sbuf_tensor("p6_wkh", [128, NF * D], BF16))
        wkl_ = es6.enter_context(nc.sbuf_tensor("p6_wkl", [128, NF * D], BF16))
        wvh_ = es6.enter_context(nc.sbuf_tensor("p6_wvh", [128, NF * D], BF16))
        wvl_ = es6.enter_context(nc.sbuf_tensor("p6_wvl", [128, NF * D], BF16))
        wkh = [wkh_[:, fc * D:(fc + 1) * D] for fc in range(NF)]
        wkl = [wkl_[:, fc * D:(fc + 1) * D] for fc in range(NF)]
        wvh = [wvh_[:, fc * D:(fc + 1) * D] for fc in range(NF)]
        wvl = [wvl_[:, fc * D:(fc + 1) * D] for fc in range(NF)]
        with ExitStack() as es6b:
            stage_ = es6b.enter_context(
                nc.sbuf_tensor("p6_stage", [128, NF * D], F32))
            stage = [stage_[:, fc * D:(fc + 1) * D] for fc in range(NF)]
            rk = es6b.enter_context(nc.sbuf_tensor("p6_rk", [32, 2 * S], F32))
            rr = es6b.enter_context(nc.sbuf_tensor("p6_rr", [32, 4 * S], F32))
            io32 = es6b.enter_context(nc.sbuf_tensor("p6_io", [32, 2], F32))
            ep = EP(cx)
            ep.op("sync", _dma(rk[:, 0:S], d["tbl_ropek"][0]), LD, 16)
            ep.op("sync", _dma(rk[:, S:2 * S], d["tbl_ropek"][1]), LD, 16)
            ep.op("sync", _dma(rr[:, 0:S], d["tbl_pos32"]), LD, 16)
            ep.op("sync", _dma(io32[:, 0:1], d["tbl_iota32"]), LD, 16)
            ep.op("sync", _dma(ident[:], d["tbl_ident"]), LD, 16)
            for o in range(4):
                ep.op("sync", _dma(cmsk[:, o * TT:(o + 1) * TT],
                                   d["tbl_cmsk"][o]), LD, 16)
            for fc in range(NF):
                ep.op("sync", _dma(bcols[:, fc:fc + 1], d["bias1_col"][fc]),
                      LD, 16)
            ep.w("scalar", LD)
            ep.op("scalar", (lambda e: e.activation(
                out=io32[:, 1:2], in_=io32[:, 0:1], func=ACT.Exp,
                scale=-math.log(10000.0) / 32.0)), SE)
            ep.w("vector", SE)
            ep.w("vector", LD)
            ep.op("vector", (lambda e: e.tensor_scalar(
                out=rr[:, S:2 * S], in0=rr[:, 0:S], scalar1=io32[:, 1:2],
                scalar2=0.0, op0=ALU.mult, op1=ALU.add)), VE)
            for i, (koff, extra) in enumerate(((0, None), (S, PI / 2))):
                ep.w("vector", SE)
                ep.op("vector", (lambda e, ko=koff: e.tensor_scalar(
                    out=rr[:, 2 * S:3 * S], in0=rk[:, ko:ko + S],
                    scalar1=-2 * PI, scalar2=None, op0=ALU.mult)))
                ep.op("vector", (lambda e: e.tensor_tensor(
                    out=rr[:, 2 * S:3 * S], in0=rr[:, 2 * S:3 * S],
                    in1=rr[:, S:2 * S], op=ALU.add)))
                if extra is None:
                    ep.op("vector", (lambda e: e.tensor_scalar(
                        out=rr[:, 2 * S:3 * S], in0=rr[:, 2 * S:3 * S],
                        scalar1=CLP, scalar2=-CLP, op0=ALU.min, op1=ALU.max)), VE)
                else:
                    ep.op("vector", (lambda e, x=extra: e.tensor_scalar(
                        out=rr[:, 2 * S:3 * S], in0=rr[:, 2 * S:3 * S],
                        scalar1=x, scalar2=CLP, op0=ALU.add, op1=ALU.min)))
                    ep.op("vector", (lambda e: e.tensor_scalar(
                        out=rr[:, 2 * S:3 * S], in0=rr[:, 2 * S:3 * S],
                        scalar1=-CLP, scalar2=None, op0=ALU.max)), VE)
                ep.w("scalar", VE)
                ep.op("scalar", (lambda e: e.activation(
                    out=rr[:, 3 * S:4 * S], in_=rr[:, 2 * S:3 * S],
                    func=ACT.Sin)))
                if i == 0:
                    ep.op("scalar", (lambda e: e.activation(
                        out=RS[0:32, :], in_=rr[:, 3 * S:4 * S], func=ACT.Copy,
                        scale=-RINV)))
                    ep.op("scalar", (lambda e: e.activation(
                        out=RS[32:64, :], in_=rr[:, 3 * S:4 * S], func=ACT.Copy,
                        scale=RINV)), SE)
                else:
                    ep.op("scalar", (lambda e: e.activation(
                        out=RC[0:32, :], in_=rr[:, 3 * S:4 * S], func=ACT.Copy,
                        scale=RINV)))
                    ep.op("scalar", (lambda e: e.activation(
                        out=RC[32:64, :], in_=rr[:, 3 * S:4 * S], func=ACT.Copy,
                        scale=RINV)), SE)
            ep.op("vector", (lambda e: e.memset(RC[64:128, :], RINV)), VE)
            if P6_DBG:
                ep.w("gpsimd", SE)
                ep.w("gpsimd", VE)
                ep.op("gpsimd", _dma(d["dbg"][:, 0:S], RC[:]), ST, 16)
                ep.op("gpsimd", _dma(d["dbg"][0:64, S:2 * S], RS[:]), ST, 16)
                ep.w("gpsimd", cx.sem("ST"))
            _emit_weight_pair(ep, cx, d["wk1"], stage, wkh, wkl)
            _emit_weight_pair(ep, cx, d["wv1"], stage, wvh, wvl)
            ep.w("sync", VE)
            for fc in range(NF):
                ep.op("sync", _dma(stage[fc], d["wf1"][fc]), LD, 16)
            ep.w("scalar", LD)
            for fc in range(NF):
                ep.op("scalar", (lambda e, o=w8[fc], i=stage[fc]: e.copy(
                    out=o, in_=i)), SE)
            ep.run("p6w")

        with ExitStack() as es6c:
            xn_ = es6c.enter_context(
                nc.sbuf_tensor("p6_xn", [128, NF * TT * 4], BF16))
            qsb = es6c.enter_context(nc.sbuf_tensor("p6_qsb", [128, 2048], F32))
            qsw = es6c.enter_context(nc.sbuf_tensor("p6_qsw", [64, 2048], F32))
            t1s = es6c.enter_context(nc.sbuf_tensor("p6_t1", [128, TT], F32))
            t2s = es6c.enter_context(nc.sbuf_tensor("p6_t2", [64, TT], F32))
            qf = es6c.enter_context(nc.sbuf_tensor("p6_qf", [128, TT], F32))
            ohi = es6c.enter_context(nc.sbuf_tensor("p6_oh", [128, 2048], BF16))
            olo = es6c.enter_context(nc.sbuf_tensor("p6_ol", [128, 2048], BF16))
            stA = es6c.enter_context(nc.sbuf_tensor("p6_sa", [128, H * 132], BF16))
            stB = es6c.enter_context(nc.sbuf_tensor("p6_sb", [128, H * 128], BF16))
            ps0 = es6c.enter_context(nc.psum_tensor("p6_ps0", [128, TT], F32))
            ps1 = es6c.enter_context(nc.psum_tensor("p6_ps1", [128, TT], F32))
            ps2 = es6c.enter_context(nc.psum_tensor("p6_ps2", [128, TT], F32))
            ps3 = es6c.enter_context(nc.psum_tensor("p6_ps3", [128, TT], F32))
            xnh = [[xn_[:, ((sl * 2 + 0) * NF + fc) * TT:
                        ((sl * 2 + 0) * NF + fc + 1) * TT]
                    for fc in range(NF)] for sl in range(2)]
            xnl = [[xn_[:, ((sl * 2 + 1) * NF + fc) * TT:
                        ((sl * 2 + 1) * NF + fc + 1) * TT]
                    for fc in range(NF)] for sl in range(2)]
            psums = [ps0, ps1]
            ep = EP(cx)
            if P6_DBG:
                ep.op("gpsimd", _dma(d["dbg"][:, 0:S], RC[:]), ST, 16)
                ep.op("gpsimd", _dma(d["dbg"][0:64, S:2 * S], RS[:]), ST, 16)
                ep.w("gpsimd", ST)
            ep.op("vector", lambda e: e.memset(stA[:], 1.0), VE)
            bank_free = [0, 0]
            rel = []
            st_hist = []
            HG = 4
            for s, tt in product(range(BPC), range(NQB)):
                ti = s * NQB + tt
                sl = ti % 2
                tok = slice(s * S + tt * TT, s * S + (tt + 1) * TT)
                ttok = slice(tt * TT, (tt + 1) * TT)
                if ti >= 2:
                    ep.w("sync", SE, rel[ti - 2])
                for fc in range(NF):
                    ep.op("sync", _dma(xnh[sl][fc], d["xn1h"][fc, :, tok]),
                          LD, 16)
                    ep.op("sync", _dma(xnl[sl][fc], d["xn1l"][fc, :, tok]),
                          LD, 16)
                ld_done = LD.v

                def _khg(hg):
                    ep.w("tensor", LD, ld_done)
                    for hh in range(HG):
                        h = hg * HG + hh
                        bk = hh % 2
                        pp = psums[bk]
                        ocol = slice(h * 128, (h + 1) * 128)
                        if bank_free[bk] > 0:
                            ep.w("tensor", SE, bank_free[bk])
                        for fc in range(NF):
                            _mm3(ep, cx, pp[:], wkh[fc][:, ocol],
                                 wkl[fc][:, ocol], xnh[sl][fc], xnl[sl][fc],
                                 start=(fc == 0), stop=(fc == NF - 1), inc=TE)
                        ep.w("scalar", TE)
                        if hh == 0:
                            ep.w("scalar", VE)
                        ep.op("scalar", (lambda e, o=qsb[:, hh * TT:(hh + 1) * TT],
                                         i=pp[:]: e.copy(out=o, in_=i)), SE)
                        bank_free[bk] = SE.v
                    ep.op("scalar", _dma(qsw[0:32, :], qsb[32:64, :]), LD2, 16)
                    ep.op("scalar", _dma(qsw[32:64, :], qsb[0:32, :]), LD2, 16)
                    ep.w("vector", SE)
                    ep.w("vector", LD2)
                    if st_hist:
                        ep.w("vector", ST, st_hist[-1])
                    if P6_NOROT:
                        for hh in range(HG):
                            hc = slice(hh * TT, (hh + 1) * TT)
                            ep.op("vector", (lambda e, o=ohi[:, hc], i=qsb[:, hc]:
                                             e.tensor_copy(out=o, in_=i)))
                            ep.op("vector", (lambda e, o=olo[:, hc], a=qsb[:, hc],
                                             b=ohi[:, hc]: e.tensor_tensor(
                                out=o, in0=a, in1=b, op=ALU.subtract)), VE)
                    elif P6_ROTMODE == 1:
                        for hh in range(HG):
                            hc = slice(hh * TT, (hh + 1) * TT)
                            ep.op("vector", (lambda e, i=qsb[:, hc]:
                                             e.tensor_tensor(out=t1s[:], in0=i,
                                                             in1=RC[:, ttok],
                                                             op=ALU.mult)))
                            ep.op("vector", (lambda e, o=ohi[:, hc]:
                                             e.tensor_copy(out=o, in_=t1s[:])))
                            ep.op("vector", (lambda e, o=olo[:, hc], b=ohi[:, hc]:
                                             e.tensor_tensor(out=o, in0=t1s[:],
                                                             in1=b,
                                                             op=ALU.subtract)), VE)
                    elif P6_ROTMODE == 2:
                        for hh in range(HG):
                            hc = slice(hh * TT, (hh + 1) * TT)
                            ep.op("vector", (lambda e, i=qsw[:, hc]:
                                             e.tensor_tensor(out=t2s[:], in0=i,
                                                             in1=RS[:, ttok],
                                                             op=ALU.mult)))
                            ep.op("vector", (lambda e, o=ohi[:, hc]:
                                             e.memset(o, 0.0)))
                            ep.op("vector", (lambda e, o=ohi[0:64, hc]:
                                             e.tensor_copy(out=o, in_=t2s[:])))
                            ep.op("vector", (lambda e, o=olo[:, hc], b=ohi[:, hc]:
                                             e.tensor_tensor(out=o, in0=b, in1=b,
                                                             op=ALU.subtract)), VE)
                    else:
                      for hh in range(HG):
                        h = hg * HG + hh
                        hc = slice(hh * TT, (hh + 1) * TT)
                        ep.op("vector", (lambda e, i=qsb[:, hc], tb=RC[:, ttok]:
                                         e.tensor_tensor(
                            out=t1s[:], in0=i, in1=tb, op=ALU.mult)))
                        ep.op("vector", (lambda e, i=qsw[:, hc], tb=RS[:, ttok]:
                                         e.tensor_tensor(
                            out=t2s[:], in0=i, in1=tb, op=ALU.mult)))
                        ep.op("vector", (lambda e: e.tensor_tensor(
                            out=qf[0:64, :], in0=t1s[0:64, :], in1=t2s[:],
                            op=ALU.add)))
                        ep.op("vector", (lambda e: e.tensor_copy(
                            out=qf[64:128, :], in_=t1s[64:128, :])))
                        ep.op("vector", (lambda e, o=ohi[:, hc]:
                                         e.tensor_copy(out=o, in_=qf[:])))
                        ep.op("vector", (lambda e, o=olo[:, hc], b=ohi[:, hc]:
                                         e.tensor_tensor(out=o, in0=qf[:], in1=b,
                                                         op=ALU.subtract)), VE)
                    ep.w("gpsimd", VE)
                    for hh in range(HG):
                        h = hg * HG + hh
                        hc = slice(hh * TT, (hh + 1) * TT)
                        ep.op("gpsimd", _dma(d["k1h"][s, h, :, ttok], ohi[:, hc]),
                              ST, 16)
                        ep.op("gpsimd", _dma(d["k1l"][s, h, :, ttok], olo[:, hc]),
                              ST, 16)
                    st_hist.append(ST.v)

                for hg in range(2):
                    _khg(hg)

                def _vch(cc):
                    c = tt * 4 + cc
                    cs = slice(cc * CH, (cc + 1) * CH)
                    ep.w("tensor", LD, ld_done)
                    ep.w("tensor", SE)
                    ep.w("tensor", VE)
                    for ih in range(2):
                        half = (slice(0, TT), slice(TT, D))[ih]
                        pp = (ps2, ps3)[ih]
                        for fc in range(NF):
                            _mm3(ep, cx, pp[:], xnh[sl][fc][:, cs],
                                 xnl[sl][fc][:, cs],
                                 wvh[fc][:, half], wvl[fc][:, half],
                                 start=(fc == 0), stop=(fc == NF - 1), inc=TE)
                    ep.w("scalar", TE)
                    ep.w("scalar", ST)
                    for h in range(H):
                        pp = (ps2, ps3)[h // 4]
                        pcol = slice((h % 4) * 128, (h % 4 + 1) * 128)
                        ep.op("scalar", (lambda e, o=stA[:, h * 132:h * 132 + 128],
                                         i=pp[:, pcol]: e.copy(out=o, in_=i)), SE)
                    ep.w("vector", SE)
                    ep.w("vector", ST)
                    for h in range(H):
                        pp = (ps2, ps3)[h // 4]
                        pcol = slice((h % 4) * 128, (h % 4 + 1) * 128)
                        ep.op("vector", (lambda e, o=stB[:, h * 128:(h + 1) * 128],
                                         i=pp[:, pcol],
                                         b=stA[:, h * 132:h * 132 + 128]:
                                         e.tensor_tensor(out=o, in0=i, in1=b,
                                                         op=ALU.subtract)), VE)
                    ep.w("gpsimd", SE)
                    ep.w("gpsimd", VE)
                    ep.op("gpsimd", _dma(d["v1a"][s, :, :, c, :], stA[:]), ST, 16)
                    ep.op("gpsimd", _dma(d["v1b"][s, :, :, c, :], stB[:]), ST, 16)

                for cc in range(4):
                    _vch(cc)
                rel.append(SE.v)
            ep.w("gpsimd", ST)
            ep.run("p6")


def _p7_attn1(cx):
    nc = cx.nc
    d = cx.d
    LD, LD2, ST, TE, VE, SE = (cx.sem(n) for n in
                               ("LD", "LD2", "ST", "TE", "VE", "SE"))
    p = cx.p67
    RC, RS, w8, ident, bcols, cmsk = (p["RC"], p["RS"], p["w8"], p["ident"],
                                      p["bcols"], p["cmsk"])
    with ExitStack() as es7:
        wqh_ = es7.enter_context(nc.sbuf_tensor("p7_wqh", [128, NF * D], BF16))
        wql_ = es7.enter_context(nc.sbuf_tensor("p7_wql", [128, NF * D], BF16))
        woh_ = es7.enter_context(nc.sbuf_tensor("p7_woh", [128, NF * D], BF16))
        wol_ = es7.enter_context(nc.sbuf_tensor("p7_wol", [128, NF * D], BF16))
        wqh = [wqh_[:, fc * D:(fc + 1) * D] for fc in range(NF)]
        wql = [wql_[:, fc * D:(fc + 1) * D] for fc in range(NF)]
        woh = [woh_[:, fc * D:(fc + 1) * D] for fc in range(NF)]
        wol = [wol_[:, fc * D:(fc + 1) * D] for fc in range(NF)]
        with ExitStack() as es7b:
            stage_ = es7b.enter_context(
                nc.sbuf_tensor("p7_stage", [128, NF * D], F32))
            stage = [stage_[:, fc * D:(fc + 1) * D] for fc in range(NF)]
            ep = EP(cx)
            _emit_weight_pair(ep, cx, d["wq1"], stage, wqh, wql)
            _emit_weight_pair(ep, cx, d["wo1"], stage, woh, wol)
            ep.run("p7w")
        with ExitStack() as es:
            xn_ = es.enter_context(nc.sbuf_tensor("p7_xn", [128, NF * TT * 2], BF16))
            x1b = es.enter_context(nc.sbuf_tensor("p7_x1", [128, 2 * TT], F32))
            qsb = es.enter_context(nc.sbuf_tensor("p7_qsb", [128, 2048], F32))
            qsw = es.enter_context(nc.sbuf_tensor("p7_qsw", [64, 2048], F32))
            t1s = es.enter_context(nc.sbuf_tensor("p7_t1", [128, TT], F32))
            t2s = es.enter_context(nc.sbuf_tensor("p7_t2", [64, TT], F32))
            qf = es.enter_context(nc.sbuf_tensor("p7_qf", [128, TT], F32))
            qph = es.enter_context(nc.sbuf_tensor("p7_qph", [128, H * TT], BF16))
            qpl = es.enter_context(nc.sbuf_tensor("p7_qpl", [128, H * TT], BF16))
            x8_ = es.enter_context(nc.sbuf_tensor("p7_x8", [128, NF * TT], I8))
            xq32 = es.enter_context(nc.sbuf_tensor("p7_xq", [128, TT], F32))
            kp = es.enter_context(nc.sbuf_tensor("p7_kp", [128, S * 2], BF16))
            va = es.enter_context(nc.sbuf_tensor("p7_va", [128, NCH * 132], BF16))
            vb = es.enter_context(nc.sbuf_tensor("p7_vb", [128, NCH * 128], BF16))
            p32_ = es.enter_context(nc.sbuf_tensor("p7_p32", [128, TT * 2], F32))
            phb_ = es.enter_context(nc.sbuf_tensor("p7_ph", [128, TT * 3], BF16))
            plb_ = es.enter_context(nc.sbuf_tensor("p7_pl", [128, TT * 3], BF16))
            rcol = es.enter_context(nc.sbuf_tensor("p7_rc", [128, 8], F32))
            oht = es.enter_context(nc.sbuf_tensor("p7_oht", [128, 128], F32))
            osb = es.enter_context(nc.sbuf_tensor("p7_osb", [128, 132], F32))
            ofh = es.enter_context(nc.sbuf_tensor("p7_ofh", [128, H * TT], BF16))
            ofl = es.enter_context(nc.sbuf_tensor("p7_ofl", [128, H * TT], BF16))
            f16_ = es.enter_context(nc.sbuf_tensor("p7_f16", [128, TT], I16))
            outs = es.enter_context(nc.sbuf_tensor("p7_out", [128, TT], F32))
            pS0 = es.enter_context(nc.psum_tensor("p7_pS0", [128, TT], F32))
            pS1 = es.enter_context(nc.psum_tensor("p7_pS1", [128, TT], F32))
            pS2 = es.enter_context(nc.psum_tensor("p7_pS2", [128, TT], F32))
            pOa = es.enter_context(nc.psum_tensor("p7_pOa", [128, 512], F32))
            pOb = es.enter_context(nc.psum_tensor("p7_pOb", [128, 512], F32))
            pT = es.enter_context(nc.psum_tensor("p7_pT", [128, 128], F32))
            pW = es.enter_context(nc.psum_tensor("p7_pW", [128, TT], F32))
            pF = es.enter_context(nc.psum_tensor("p7_pF", [128, TT], F32))
            xnh = [xn_[:, (2 * fc) * TT:(2 * fc + 1) * TT] for fc in range(NF)]
            xnl = [xn_[:, (2 * fc + 1) * TT:(2 * fc + 2) * TT] for fc in range(NF)]
            x8 = [x8_.bitcast(F8)[:, fc * TT:(fc + 1) * TT] for fc in range(NF)]
            f16 = f16_.bitcast(F16)
            phb = [phb_[:, i * TT:(i + 1) * TT] for i in range(3)]
            plb = [plb_[:, i * TT:(i + 1) * TT] for i in range(3)]
            p32b = [p32_[:, i * TT:(i + 1) * TT] for i in range(2)]
            pSb = [pS0, pS1, pS2]
            ep = EP(cx)
            HG = 4
            te_fin = [0]
            qg_se = [0, 0]
            kvw = []
            prev_xn = [0, 0]
            for s, qb in product(range(BPC), range(NQB)):
                nk = 4 * (qb + 1)
                tok = slice(s * S + qb * TT, s * S + (qb + 1) * TT)
                ttok = slice(qb * TT, (qb + 1) * TT)
                ep.w("sync", TE, prev_xn[0])
                ep.w("sync", VE, prev_xn[1])
                for fc in range(NF):
                    ep.op("sync", _dma(xnh[fc], d["xn1h"][fc, :, tok]), LD, 16)
                    ep.op("sync", _dma(xnl[fc], d["xn1l"][fc, :, tok]), LD, 16)
                ld_done = LD.v

                def _qhg(hg):
                    ep.w("tensor", LD, ld_done)
                    for hh in range(HG):
                        h = hg * HG + hh
                        ocol = slice(h * 128, (h + 1) * 128)
                        pq = (pW, pF)[hh % 2]
                        if hg == 0 and hh < 2:
                            ep.w("tensor", SE)
                            ep.w("tensor", VE)
                        elif qg_se[hh % 2]:
                            ep.w("tensor", SE, qg_se[hh % 2])
                        for fc in range(NF):
                            _mm3(ep, cx, pq[:], wqh[fc][:, ocol],
                                 wql[fc][:, ocol], xnh[fc], xnl[fc],
                                 start=(fc == 0), stop=(fc == NF - 1), inc=TE)
                        ep.w("scalar", TE)
                        if hh == 0:
                            ep.w("scalar", VE)
                        ep.op("scalar", (lambda e, o=qsb[:, hh * TT:(hh + 1) * TT],
                                         i=pq[:]: e.copy(out=o, in_=i)), SE)
                        qg_se[hh % 2] = SE.v
                    ep.op("scalar", _dma(qsw[0:32, :], qsb[32:64, :]), LD2, 16)
                    ep.op("scalar", _dma(qsw[32:64, :], qsb[0:32, :]), LD2, 16)
                    ep.w("vector", SE)
                    ep.w("vector", LD2)
                    ep.w("vector", TE, te_fin[0])
                    for hh in range(HG):
                        h = hg * HG + hh
                        hc = slice(hh * TT, (hh + 1) * TT)
                        qc = slice(h * TT, (h + 1) * TT)
                        ep.op("vector", (lambda e, i=qsb[:, hc], tb=RC[:, ttok]:
                                         e.tensor_tensor(
                            out=t1s[:], in0=i, in1=tb, op=ALU.mult)))
                        ep.op("vector", (lambda e, i=qsw[:, hc], tb=RS[:, ttok]:
                                         e.tensor_tensor(
                            out=t2s[:], in0=i, in1=tb, op=ALU.mult)))
                        ep.op("vector", (lambda e: e.tensor_tensor(
                            out=qf[0:64, :], in0=t1s[0:64, :], in1=t2s[:],
                            op=ALU.add)))
                        ep.op("vector", (lambda e: e.tensor_copy(
                            out=qf[64:128, :], in_=t1s[64:128, :])))
                        ep.op("vector", (lambda e, o=qph[:, qc]:
                                         e.tensor_copy(out=o, in_=qf[:])))
                        ep.op("vector", (lambda e, o=qpl[:, qc], b=qph[:, qc]:
                                         e.tensor_tensor(out=o, in0=qf[:], in1=b,
                                                         op=ALU.subtract)), VE)

                for hg in range(2):
                    _qhg(hg)
                for fc in range(NF):
                    if fc > 0:
                        ep.w("vector", SE)
                    ep.op("vector", (lambda e, a=xnh[fc], b=xnl[fc]:
                                     e.tensor_tensor(out=xq32[:], in0=a, in1=b,
                                                     op=ALU.add)), VE)
                    ep.w("scalar", VE)
                    ep.op("scalar", (lambda e, o=x8[fc]: e.copy(
                        out=o, in_=xq32[:])), SE)
                q_done = VE.v
                x8_done = SE.v
                prev_xn[0] = TE.v
                prev_xn[1] = VE.v

                def _flash(h):
                    qc = slice(h * TT, (h + 1) * TT)
                    ep.w("vector", SE, x8_done)
                    if kvw:
                        ep.w("sync", TE, kvw[-1][0])
                        ep.w("sync", VE, kvw[-1][1])
                    else:
                        ep.w("sync", TE)
                        ep.w("sync", VE)
                    ep.op("sync", _dma(kp[:, 0:nk * CH],
                                       d["k1h"][s, h, :, 0:nk * CH]), LD, 16)
                    ep.op("sync", _dma(kp[:, S:S + nk * CH],
                                       d["k1l"][s, h, :, 0:nk * CH]), LD, 16)
                    ep.op("sync", _dma(va[:, 0:nk * 132],
                                       d["v1a"][s, :, h, 0:nk, :]), LD, 16)
                    ep.op("sync", _dma(vb[:, 0:nk * CH],
                                       d["v1b"][s, :, h, 0:nk, :]), LD, 16)
                    ldh = LD.v
                    ep.w("tensor", LD, ldh)
                    ep.w("tensor", VE)
                    first_a = [True]
                    first_b = [True]
                    pvals = {}
                    tvals = {}

                    def _st(kt):
                        pS = pSb[kt % 3]
                        if kt >= 3:
                            ep.w("tensor", VE, pvals[kt - 3])
                        kc = slice(kt * CH, (kt + 1) * CH)
                        _mm3(ep, cx, pS[:], kp[:, kc],
                             kp[:, S + kt * CH:S + (kt + 1) * CH],
                             qph[:, qc], qpl[:, qc], start=True, stop=True,
                             inc=TE)

                    def _pb(kt):
                        pS = pSb[kt % 3]
                        sl = kt % 3
                        s2 = kt % 2
                        if kt >= 3:
                            ep.w("vector", TE, tvals[kt - 3])
                        if kt >= 2:
                            ep.w("scalar", VE, pvals[kt - 2])
                        o = kt - 4 * qb
                        if o >= 0:
                            ep.w("vector", TE)
                            ep.op("vector", (lambda e, i=pS[:],
                                             m=cmsk[:, o * TT:(o + 1) * TT]:
                                             e.tensor_tensor(out=xq32[:], in0=i,
                                                             in1=m,
                                                             op=ALU.subtract)), VE)
                            ep.w("scalar", VE)
                            ep.op("scalar", (lambda e, o2=p32b[s2]: e.activation(
                                out=o2, in_=xq32[:], func=ACT.Exp)), SE)
                        else:
                            ep.w("scalar", TE)
                            ep.op("scalar", (lambda e, o2=p32b[s2], i=pS[:]:
                                             e.activation(out=o2, in_=i,
                                                          func=ACT.Exp)), SE)
                        ep.w("vector", SE)
                        ep.op("vector", (lambda e, o2=phb[sl], i=p32b[s2]:
                                         e.tensor_copy(out=o2, in_=i)))
                        ep.op("vector", (lambda e, o2=plb[sl], i=p32b[s2],
                                         b=phb[sl]: e.tensor_tensor(
                            out=o2, in0=i, in1=b, op=ALU.subtract)), VE)
                        pvals[kt] = VE.v

                    def _pv(kt, last_k=False):
                        sl = kt % 3
                        ep.w("tensor", VE, pvals[kt])
                        for qs in range(4):
                            qcc = slice(qs * 128, (qs + 1) * 128)
                            po = (pOa, pOb)[qs // 2]
                            base = (qs % 2) * 132
                            oc = slice(base, base + 129)
                            oc3 = slice(base, base + 128)
                            st = first_a[0] if qs < 2 else first_b[0]
                            if qs < 2:
                                first_a[0] = False
                            else:
                                first_b[0] = False
                            ep.op("tensor", (lambda e, o3=po[:, oc],
                                             l=phb[sl][:, qcc],
                                             r=va[:, kt * 132:kt * 132 + 129],
                                             s0=st: e.matmul(
                                out=o3, lhsT=l, rhs=r, start=s0, stop=False)))
                            ep.op("tensor", (lambda e, o3=po[:, oc],
                                             l=plb[sl][:, qcc],
                                             r=va[:, kt * 132:kt * 132 + 129]:
                                             e.matmul(out=o3, lhsT=l, rhs=r,
                                                      start=False, stop=False)))
                            ep.op("tensor", (lambda e, o3=po[:, oc3],
                                             l=phb[sl][:, qcc],
                                             r=vb[:, kt * CH:(kt + 1) * CH],
                                             s1=(last_k and qs == 3):
                                             e.matmul(out=o3, lhsT=l, rhs=r,
                                                      start=False, stop=s1)),
                                TE if qs == 3 else None)
                        tvals[kt] = TE.v

                    _st(0)
                    _pb(0)
                    _st(1)
                    _pb(1)
                    for kt in range(2, nk):
                        _st(kt)
                        _pb(kt)
                        _pv(kt - 2)
                    _pv(nk - 2)
                    _pv(nk - 1, last_k=True)
                    kvw.append((TE.v, VE.v))
                    if P7_DBG and s == 0 and qb == 0 and h == 0:
                        ep.w("vector", TE)
                        ep.op("vector", (lambda e: e.tensor_copy(
                            out=outs[:, 0:264], in_=pOa[:, 0:264])))
                        ep.op("vector", (lambda e: e.tensor_copy(
                            out=outs[:, 264:528], in_=pOb[:, 0:264])), VE)
                        ep.w("gpsimd", VE)
                        ep.op("gpsimd", _dma(d["dbg"][:, 0:528], outs[:, 0:528]),
                              ST, 16)
                        ep.w("gpsimd", ST)
                    for qs in range(4):
                        po = (pOa, pOb)[qs // 2]
                        base = (qs % 2) * 132
                        ep.w("vector", TE)
                        ep.op("vector", (lambda e, i=po[:, base:base + 132]:
                                         e.tensor_copy(out=osb[:], in_=i)), VE)
                        ep.w("scalar", VE)
                        ep.op("scalar", lambda e: e.nop(), SE)
                        ep.w("vector", SE)
                        ep.op("vector", (lambda e:
                                         e.reciprocal(out=rcol[:, 0:1],
                                                      in_=osb[:, 128:129])), VE)
                        ep.w("scalar", VE)
                        ep.op("scalar", (lambda e:
                                         e.activation(out=oht[:],
                                                      in_=osb[:, 0:128],
                                                      func=ACT.Copy,
                                                      scale=rcol[:, 0:1])), SE)
                        ep.w("vector", SE)
                        ep.op("vector", lambda e: e.nop(), VE)
                        if P7_DBG and s == 0 and qb == 0 and h == 0:
                            ep.w("gpsimd", VE)
                            ep.op("gpsimd", _dma(d["dbg"][:, 528 + qs * 128:
                                                          528 + (qs + 1) * 128],
                                                 oht[:]), ST, 16)
                            ep.op("gpsimd", _dma(d["dbg"][:, 1040 + qs * 8:
                                                          1048 + qs * 8],
                                                 rcol[:]), ST, 16)
                            ep.op("gpsimd", _dma(d["dbg"][:, 1072 + qs * 132:
                                                          1204 + qs * 132],
                                                 osb[:]), ST, 16)
                            ep.w("gpsimd", ST)
                            ep.w("vector", ST)
                        ep.w("tensor", VE)
                        ep.w("tensor", SE)
                        ep.op("tensor", (lambda e: e.transpose(
                            out=pT[:], in_=oht[:], identity=ident[:])), TE)
                        ep.w("scalar", TE)
                        if h == 0 and qs == 0:
                            ep.w("scalar", TE, te_fin[0])
                        ep.op("scalar", (lambda e,
                                         o3=ofh[:, h * TT + qs * 128:
                                                h * TT + (qs + 1) * 128]:
                                         e.copy(out=o3, in_=pT[:])), SE)
                        ep.w("vector", SE)
                        ep.op("vector", (lambda e,
                                         o3=ofl[:, h * TT + qs * 128:
                                                h * TT + (qs + 1) * 128],
                                         b=ofh[:, h * TT + qs * 128:
                                               h * TT + (qs + 1) * 128]:
                                         e.tensor_tensor(out=o3, in0=pT[:], in1=b,
                                                         op=ALU.subtract)), VE)

                for h in range(H):
                    _flash(h)
                of_done = VE.v

                fin_ve = []

                def _fin(mc):
                    mcol = slice(mc * 128, (mc + 1) * 128)
                    if mc >= 2:
                        ep.w("sync", VE, fin_ve[mc - 2])
                    ep.op("sync", _dma(x1b[:, (mc % 2) * TT:(mc % 2 + 1) * TT],
                                       d["x1"][mc, :, tok]), LD, 16)
                    ld_fin = LD.v
                    ep.w("tensor", VE, of_done)
                    ep.w("tensor", SE)
                    ep.w("tensor", VE)
                    for fc in range(NF):
                        _mm3(ep, cx, pW[:], woh[fc][:, mcol], wol[fc][:, mcol],
                             ofh[:, fc * TT:(fc + 1) * TT],
                             ofl[:, fc * TT:(fc + 1) * TT],
                             start=(fc == 0), stop=(fc == NF - 1), inc=TE)
                    for fc in range(NF):
                        ep.op("tensor", (lambda e, l=w8[fc][:, mcol], r=x8[fc],
                                         s0=(fc == 0), s1=(fc == NF - 1):
                                         e.matmul(out=pF[:], lhsT=l, rhs=r,
                                                  start=s0, stop=s1)),
                              TE if fc == NF - 1 else None)
                    ep.w("scalar", TE)
                    ep.op("scalar", (lambda e: e.copy(out=f16[:], in_=pF[:])), SE)
                    ep.w("vector", SE)
                    ep.w("vector", ST)
                    ep.w("vector", LD, ld_fin)
                    ob = outs[:, 0:TT]
                    if P7_MODE == 1:
                        ep.op("vector", (lambda e, o3=ob, b=bcols[:, mc:mc + 1]:
                                         e.tensor_scalar(out=o3, in0=f16[:],
                                                         scalar1=b, scalar2=None,
                                                         op0=ALU.add)))
                    elif P7_MODE == 2:
                        ep.op("vector", (lambda e, o3=ob: e.tensor_copy(
                            out=o3, in_=pW[:])))
                    else:
                        ep.op("vector", (lambda e, o3=ob, b=bcols[:, mc:mc + 1]:
                                         e.scalar_tensor_tensor(out=o3, in0=pW[:],
                                                                scalar=b, in1=f16[:],
                                                                op0=ALU.add,
                                                                op1=ALU.add)))
                    ep.op("vector", (lambda e, o3=ob,
                                     x=x1b[:, (mc % 2) * TT:(mc % 2 + 1) * TT]:
                                     e.tensor_tensor(out=o3, in0=o3, in1=x,
                                                     op=ALU.add)), VE)
                    fin_ve.append(VE.v)
                    ep.w("gpsimd", VE)
                    ep.op("gpsimd", _dma(d["y_out"][mc, :, tok], ob), ST, 16)

                for mc in range(NF):
                    _fin(mc)
                te_fin[0] = TE.v
            ep.w("gpsimd", ST)
            ep.run("p7")


def _build(cx):
    nc = cx.nc
    d = cx.d
    _p0_norm0(cx)
    _p12_qk0(cx)
    _p34_vg0(cx)
    _pf12_mix(cx)
    _pf3_ffn0(cx)
    _p5_ret0(cx)
    with ExitStack() as _es67:
        nc2 = cx.nc
        cx.p67 = {
            "RC": _es67.enter_context(nc2.sbuf_tensor("p67_rc", [128, S], F32)),
            "RS": _es67.enter_context(nc2.sbuf_tensor("p67_rs", [64, S], F32)),
            "ident": _es67.enter_context(nc2.sbuf_tensor("p67_id", [128, 128], F32)),
            "bcols": _es67.enter_context(nc2.sbuf_tensor("p67_bc", [128, NF], F32)),
            "cmsk": _es67.enter_context(nc2.sbuf_tensor("p67_cm", [128, 4 * TT], F32)),
        }
        _w8i = _es67.enter_context(nc2.sbuf_tensor("p67_w8", [128, NF * D], I8))
        cx.p67["w8"] = [_w8i.bitcast(F8)[:, fc * D:(fc + 1) * D]
                        for fc in range(NF)]
        _p6_kv1(cx)
        _p7_attn1(cx)


def build_nc(dbg_in=(), dbg_out=(), phases=None):
    nc = bass.Bass("TRN2", target_bir_lowering=False)
    cx = Ctx(nc)
    d = cx.d

    def dram(name, shape, dtype=F32):
        if name in dbg_in:
            kind = "ExternalInput"
        elif name in dbg_out:
            kind = "ExternalOutput"
        else:
            kind = "Internal"
        d[name] = nc.dram_tensor(name, list(shape), dtype, kind=kind).ap()

    for name, shape in [
        ("x_in", (NF, 128, T)),
        ("wq0", (NF, 128, D)), ("wk0", (NF, 128, D)), ("wv0", (NF, 128, D)),
        ("wg0", (NF, 128, D)), ("wo0", (NF, 128, D)),
        ("wr0", (NF, 128, D)), ("wk0f", (NF, 128, D)), ("wv0f", (NF, 128, D)),
        ("gnw_row", (1, D)), ("gnb_row", (1, D)),
        ("mur_col", (NF, 128, 1)), ("muk_col", (NF, 128, 1)),
        ("wq1", (NF, 128, D)), ("wk1", (NF, 128, D)), ("wv1", (NF, 128, D)),
        ("wo1", (NF, 128, D)), ("wf1", (NF, 128, D)),
        ("bias1_col", (NF, 128, 1)),
        ("tbl_xpos", (4, 128, S)), ("tbl_dm", (H, 128, S + 384)),
        ("tbl_cmsk", (4, 128, TT)), ("tbl_ropek", (2, 32, S)),
        ("tbl_pos32", (32, S)), ("tbl_iota32", (32, 1)),
        ("tbl_ident", (128, 128)),
    ]:
        d[name] = nc.dram_tensor(name, list(shape), F32, kind="ExternalInput").ap()

    dram("xn0h", (NF, 128, T), BF16)
    dram("xn0l", (NF, 128, T), BF16)
    dram("q0h", (BPC, H, 128, S), BF16)
    dram("q0l", (BPC, H, 128, S), BF16)
    dram("k0h", (BPC, H, 128, S), BF16)
    dram("k0l", (BPC, H, 128, S), BF16)
    dram("v0h", (BPC, H, 128, S), BF16)
    dram("v0l", (BPC, H, 128, S), BF16)
    dram("g0", (BPC, NCH, 128, D), F32)
    dram("r0", (NF, 128, T), F32)
    dram("kah", (NF, 128, T), BF16)
    dram("kal", (NF, 128, T), BF16)
    dram("f0", (NF, 128, T), F32)
    dram("x1", (NF, 128, T), F32)
    dram("xn1h", (NF, 128, T), BF16)
    dram("xn1l", (NF, 128, T), BF16)
    dram("k1h", (BPC, H, 128, S), BF16)
    dram("k1l", (BPC, H, 128, S), BF16)
    dram("v1a", (BPC, 128, H, NCH, 132), BF16)
    dram("v1b", (BPC, 128, H, NCH, 128), BF16)

    d["y_out"] = nc.dram_tensor("y_out", [NF, 128, T], F32,
                                kind="ExternalOutput").ap()
    d["dbg"] = nc.dram_tensor("dbg", [128, 2 * S + 2048], F32,
                              kind="ExternalOutput" if (P6_DBG or P7_DBG) else "Internal").ap()

    for name in ("LD", "LD2", "ST", "TE", "VE", "SE", "GP",
                 "FR0", "FR1", "FR2", "FR3", "FR4", "FR5"):
        h = nc.semaphore(name)
        cx.sems[name] = Sem(h.__enter__())

    _build(cx)
    return nc, cx


def kernel(**inputs):
    tables = _host_tables()
    nc, _ = build_nc()
    in_maps = [_prep_core_inputs(inputs, c, tables) for c in range(NCORE)]
    res = run_bass_kernel_spmd(nc, in_maps, core_ids=list(range(NCORE)))
    outs = []
    for c in range(NCORE):
        y = res.results[c]["y_out"]
        yt = np.transpose(y, (2, 0, 1)).reshape(BPC, S, D)
        outs.append(yt)
    return np.concatenate(outs, 0).astype(np.float32)


# version 07
# speedup vs baseline: 1.3946x; 1.0162x over previous
import math
from collections import defaultdict
from itertools import product
from contextlib import ExitStack

import numpy as np

import concourse.bass as bass
import concourse.mybir as mybir
from concourse.bass_utils import run_bass_kernel_spmd

F32 = mybir.dt.float32
BF16 = mybir.dt.bfloat16
F8 = mybir.dt.float8e4
F16 = mybir.dt.float16
I8 = mybir.dt.int8
I16 = mybir.dt.int16
ACT = mybir.ActivationFunctionType
ALU = mybir.AluOpType
AX = mybir.AxisListType

PI = math.pi
CLP = 3.1415925

B, S, D = 16, 2048, 1024
H, HD = 8, 128
NCORE = 8
BPC = B // NCORE
T = BPC * S
NF = D // 128
TT = 512
NTT = T // TT
NQB = S // TT
CH = 128
NCH = S // CH
RINV = 128.0 ** (-0.5)

ENGINES = ("sync", "gpsimd", "tensor", "vector", "scalar")
P12_NOROT = False
P6_DBG = False
P6_NOROT = False
P6_ROTMODE = 0
P7_MODE = 0
P7_DBG = False



def _host_tables():
    import jax
    import jax.numpy as jnp
    cpu = jax.devices("cpu")[0]
    out = {}
    with jax.default_device(cpu):
        d = 128
        scale_vec = (jnp.arange(0, d, 2, dtype=jnp.float32) + 0.4 * d) / (1.4 * d)
        pos = jnp.arange(S, dtype=jnp.float32)
        scale = scale_vec[None, :] ** (pos[:, None] / 512.0)
        half = d // 2
        inv_freq = 1.0 / 10000.0 ** (jnp.arange(half, dtype=jnp.float32) / half)
        ang = pos[:, None] * inv_freq[None, :]
        sin, cos = jnp.sin(ang), jnp.cos(ang)
        cq = np.asarray((cos * scale).T)
        sq = np.asarray((sin * scale).T)
        inv_scale = 1.0 / scale
        ck = np.asarray((cos * inv_scale).T)
        sk = np.asarray((sin * inv_scale).T)
        xp = np.empty((4, 128, S), np.float32)
        for i, (c, s_) in enumerate(((cq, sq), (ck, sk))):
            xp[2 * i, 0:64] = c
            xp[2 * i, 64:128] = c
            xp[2 * i + 1, 0:64] = -s_
            xp[2 * i + 1, 64:128] = s_
        out["tbl_xpos"] = xp

        gam = 1.0 - jnp.exp(
            jnp.linspace(math.log(1.0 / 32), math.log(1.0 / 512), H)
        ).astype(jnp.float32)
        kl = jnp.arange(CH)[:, None]
        x = jnp.arange(-384, S)[None, :]
        diff = x - kl
        mask = diff >= 0
        dmat = []
        for h in range(H):
            dh = jnp.where(
                mask, gam[h] ** jnp.where(mask, diff, 0).astype(jnp.float32), 0.0
            )
            dmat.append(np.asarray(dh, np.float32))
        out["tbl_dm"] = np.stack(dmat)

    i = np.arange(128)[:, None]
    j = np.arange(TT)[None, :]
    cm = np.stack(
        [(128 * o + i > j).astype(np.float32) * np.float32(1e30) for o in range(4)]
    )
    out["tbl_cmsk"] = cm

    dims = np.arange(32)
    invf = np.power(10000.0, -dims / 32.0)
    rad = np.arange(S)[None, :] * invf[:, None]
    ks = np.rint(rad / (2 * np.pi))
    kc = np.rint((rad + np.pi / 2) / (2 * np.pi))
    for name, kk, base in (("ks", ks, rad), ("kc", kc, rad + np.pi / 2)):
        r = base - kk * (2 * np.pi)
        assert np.abs(r).max() < np.pi + 1e-9, name
    out["tbl_ropek"] = np.stack([ks, kc]).astype(np.float32)
    out["tbl_pos32"] = np.broadcast_to(
        np.arange(S, dtype=np.float32)[None, :], (32, S)
    ).copy()
    out["tbl_iota32"] = np.arange(32, dtype=np.float32)[:, None].copy()
    out["tbl_ident"] = np.eye(128, dtype=np.float32)
    return out


def _prep_core_inputs(ins, core, tables):
    m = {}
    x = np.asarray(ins["L0_attn_X"][2 * core:2 * core + 2])
    xt = np.transpose(x.reshape(T, D).reshape(T, NF, 128), (1, 2, 0))
    m["x_in"] = np.ascontiguousarray(xt)

    perm = np.concatenate([np.arange(0, 128, 2), np.arange(1, 128, 2)])

    def fm_weight(w):
        return np.ascontiguousarray(w.reshape(NF, 128, -1))

    wq = np.asarray(ins["L0_attn_W_Q"])[:, :, perm]
    wq = np.transpose(wq, (1, 0, 2)).reshape(D, D)
    m["wq0"] = fm_weight(wq)
    wk = np.asarray(ins["L0_attn_W_K"])[:, :, perm]
    wk = np.transpose(wk, (1, 0, 2)).reshape(D, D)
    m["wk0"] = fm_weight(wk)
    wv = np.transpose(np.asarray(ins["L0_attn_W_V"]), (1, 0, 2)).reshape(D, D)
    m["wv0"] = fm_weight(wv)
    m["wg0"] = fm_weight(np.asarray(ins["L0_attn_W_G"]))
    m["wo0"] = fm_weight(np.asarray(ins["L0_attn_W_O"]))
    m["wr0"] = fm_weight(np.ascontiguousarray(np.asarray(ins["L0_ffn_W_r"]).T))
    m["wk0f"] = fm_weight(np.ascontiguousarray(np.asarray(ins["L0_ffn_W_k"]).T))
    m["wv0f"] = fm_weight(np.ascontiguousarray(np.asarray(ins["L0_ffn_W_v"]).T))
    m["gnw_row"] = np.asarray(ins["L0_attn_gn_weight"]).reshape(1, D)
    m["gnb_row"] = np.asarray(ins["L0_attn_gn_bias"]).reshape(1, D)
    m["mur_col"] = np.asarray(ins["L0_ffn_mu_r"]).reshape(NF, 128, 1)
    m["muk_col"] = np.asarray(ins["L0_ffn_mu_k"]).reshape(NF, 128, 1)

    m["wq1"] = fm_weight(np.asarray(ins["L1_attn_wq"]).reshape(D, D))
    m["wk1"] = fm_weight(np.asarray(ins["L1_attn_wk"]).reshape(D, D))
    m["wv1"] = fm_weight(np.asarray(ins["L1_attn_wv"]).reshape(D, D))
    m["wo1"] = fm_weight(np.asarray(ins["L1_attn_wo"]).reshape(D, D))
    m["wf1"] = fm_weight(np.ascontiguousarray(np.asarray(ins["L1_ffn_weight"]).T))
    m["bias1_col"] = np.asarray(ins["L1_ffn_bias"]).reshape(NF, 128, 1)
    m.update(tables)
    return {k: np.ascontiguousarray(np.asarray(v, np.float32)) for k, v in m.items()}



class Sem:
    def __init__(self, handle):
        self.h = handle
        self.v = 0

    def inc(self, n=1):
        self.v += n
        return self.v


class EP:

    def __init__(self, cx):
        self.cx = cx
        self.L = defaultdict(list)
        self._lastw = {}

    def w(self, eng, sem, val=None):
        if val is None:
            val = sem.v
        key = (eng, id(sem))
        if self._lastw.get(key, -1) >= val:
            return
        self._lastw[key] = val
        self.L[eng].append(("w", sem.h, val))

    def op(self, eng, fn, inc=None, n=1):
        if inc is not None:
            inc.inc(n)
        self.L[eng].append(("o", fn, None if inc is None else (inc.h, n)))

    def run(self, name):
        cx = self.cx

        def mk(items):
            def body(eng):
                for it in items:
                    if it[0] == "w":
                        eng.wait_ge(it[1], it[2])
                    else:
                        r = it[1](eng)
                        if it[2] is not None:
                            r.then_inc(it[2][0], it[2][1])
            return body

        with cx.nc.Block(name) as blk:
            for e in ENGINES:
                if self.L[e]:
                    getattr(blk, e)(mk(self.L[e]))


class Ctx:
    def __init__(self, nc):
        self.nc = nc
        self.d = {}
        self.sems = {}
        self._stack = None

    def sem(self, name):
        return self.sems[name]


def _dma(dst, src):
    return lambda e: e.dma_start(out=dst, in_=src)


def _emit_weight_pair(ep, cx, dram, stage, hi, lo, nchunk=NF, width=D):
    LD, SE, VE = cx.sem("LD"), cx.sem("SE"), cx.sem("VE")
    ep.w("sync", VE)
    for fc in range(nchunk):
        ep.op("sync", _dma(stage[fc][:, 0:width], dram[fc, :, 0:width]), LD, 16)
    ep.w("scalar", LD)
    for fc in range(nchunk):
        ep.op("scalar",
              (lambda e, a=hi[fc], b=stage[fc]: e.copy(out=a[:, 0:width], in_=b[:, 0:width])),
              SE)
    ep.w("vector", SE)
    for fc in range(nchunk):
        ep.op("vector",
              (lambda e, a=lo[fc], b=stage[fc], c=hi[fc]: e.tensor_tensor(
                  out=a[:, 0:width], in0=b[:, 0:width], in1=c[:, 0:width],
                  op=ALU.subtract)),
              VE)


def _mm3(ep, cx, ps, wh, wl, ah, al, start, stop, inc=None):
    TE = cx.sem("TE")
    ep.op("tensor", (lambda e, o=ps, l=wh, r=ah, s=start: e.matmul(
        out=o, lhsT=l, rhs=r, start=s, stop=False)))
    ep.op("tensor", (lambda e, o=ps, l=wl, r=ah: e.matmul(
        out=o, lhsT=l, rhs=r, start=False, stop=False)))
    ep.op("tensor", (lambda e, o=ps, l=wh, r=al, s=stop: e.matmul(
        out=o, lhsT=l, rhs=r, start=False, stop=s)), inc if stop else None)



def _p0_norm0(cx):
    nc = cx.nc
    d = cx.d
    LD, ST, TE, VE, SE, FR = (cx.sem(n) for n in
                              ("LD", "ST", "TE", "VE", "SE", "FR0"))
    with (
        nc.sbuf_tensor("p0_xin", [128, NF * TT * 2], F32) as xin_all,
        nc.sbuf_tensor("p0_xsq", [128, TT * 2], F32) as xsq,
        nc.sbuf_tensor("p0_ones", [128, 1], F32) as ones_col,
        nc.sbuf_tensor("p0_onesr", [1, 128], F32) as ones_row,
        nc.sbuf_tensor("p0_rows", [1, TT * 4], F32) as rows,
        nc.sbuf_tensor("p0_tmp", [128, TT * 2], F32) as tmp,
        nc.sbuf_tensor("p0_tf", [128, TT * 2], F32) as tfull,
        nc.sbuf_tensor("p0_oh", [128, TT * 2], BF16) as ohi,
        nc.sbuf_tensor("p0_ol", [128, TT * 2], BF16) as olo,
        nc.psum_tensor("p0_psA", [128, TT], F32) as psA,
        nc.psum_tensor("p0_psB", [128, TT], F32) as psB,
        nc.psum_tensor("p0_psM", [128, TT], F32) as psM,
        nc.psum_tensor("p0_psS", [128, TT], F32) as psS,
    ):
        ep = EP(cx)
        ep.op("vector", lambda e: e.memset(ones_col[:], 1.0), VE)
        ep.op("vector", lambda e: e.memset(ones_row[:], 1.0), VE)
        xin = [
            [xin_all[:, (s * NF + fc) * TT:(s * NF + fc + 1) * TT]
             for fc in range(NF)] for s in range(2)
        ]
        mrow = rows[0:1, 0:TT]
        m2row = rows[0:1, TT:2 * TT]
        erow = rows[0:1, 2 * TT:3 * TT]
        srow = rows[0:1, 3 * TT:4 * TT]
        for tt in range(NTT):
            sl = tt % 2
            tok = slice(tt * TT, (tt + 1) * TT)
            if tt >= 2:
                ep.w("sync", FR, tt - 1)
            for fc in range(NF):
                ep.op("sync", _dma(xin[sl][fc], d["x_in"][fc, :, tok]), LD, 16)
            ep.w("tensor", LD)
            ep.w("tensor", VE)
            ep.w("tensor", SE)
            ep.w("scalar", LD)
            for fc in range(NF):
                ep.op("tensor", (lambda e, o=psA[0:1, :], l=ones_col[:], r=xin[sl][fc],
                                 s=(fc == 0), p=(fc == NF - 1): e.matmul(
                    out=o, lhsT=l, rhs=r, start=s, stop=p)))
                xq = xsq[:, (fc % 2) * TT:(fc % 2 + 1) * TT]
                if fc >= 2:
                    ep.w("scalar", TE, TE.v)
                ep.op("scalar", (lambda e, o=xq, i=xin[sl][fc]: e.activation(
                    out=o, in_=i, func=ACT.Square)), SE)
                ep.w("tensor", SE)
                ep.op("tensor", (lambda e, o=psB[0:1, :], l=ones_col[:], r=xq,
                                 s=(fc == 0), p=(fc == NF - 1): e.matmul(
                    out=o, lhsT=l, rhs=r, start=s, stop=p)), TE)
            ep.w("scalar", TE)
            ep.op("scalar", (lambda e: e.activation(out=mrow, in_=psA[0:1, :],
                                                    func=ACT.Copy, scale=1.0 / 1024)))
            ep.op("scalar", (lambda e: e.activation(out=m2row, in_=mrow,
                                                    func=ACT.Square)))
            ep.op("scalar", (lambda e: e.activation(out=erow, in_=psB[0:1, :],
                                                    func=ACT.Copy, scale=1.0 / 1024)), SE)
            ep.w("vector", SE)
            ep.op("vector", (lambda e: e.tensor_tensor(out=erow, in0=erow,
                                                       in1=m2row, op=ALU.subtract)))
            ep.op("vector", (lambda e: e.tensor_scalar(out=erow, in0=erow,
                                                       scalar1=1e-5, scalar2=None,
                                                       op0=ALU.add)), VE)
            ep.w("scalar", VE)
            ep.op("scalar", (lambda e: e.activation(out=m2row, in_=erow,
                                                    func=ACT.Sqrt)), SE)
            ep.w("vector", SE)
            ep.op("vector", (lambda e: e.reciprocal(out=srow, in_=m2row)), VE)
            ep.w("tensor", VE)
            ep.w("tensor", SE)
            ep.op("tensor", (lambda e: e.matmul(out=psM[:], lhsT=ones_row[:],
                                                rhs=mrow, start=True, stop=True)))
            ep.op("tensor", (lambda e: e.matmul(out=psS[:], lhsT=ones_row[:],
                                                rhs=srow, start=True, stop=True)), TE)
            ep.w("vector", TE)
            for fc in range(NF):
                tp = tmp[:, (fc % 2) * TT:(fc % 2 + 1) * TT]
                tf = tfull[:, (fc % 2) * TT:(fc % 2 + 1) * TT]
                oh = ohi[:, (fc % 2) * TT:(fc % 2 + 1) * TT]
                ol = olo[:, (fc % 2) * TT:(fc % 2 + 1) * TT]
                if fc >= 2:
                    ep.w("vector", ST, ST.v)
                ep.op("vector", (lambda e, o=tp, i=xin[sl][fc]: e.tensor_tensor(
                    out=o, in0=i, in1=psM[:], op=ALU.subtract)),
                    FR if fc == NF - 1 else None)
                ep.op("vector", (lambda e, o=tf, i=tp: e.tensor_tensor(
                    out=o, in0=i, in1=psS[:], op=ALU.mult)), VE)
                ep.w("scalar", VE)
                ep.op("scalar", (lambda e, o=oh, i=tf: e.copy(out=o, in_=i)), SE)
                ep.w("vector", SE)
                ep.op("vector", (lambda e, o=ol, a=tf, b=oh: e.tensor_tensor(
                    out=o, in0=a, in1=b, op=ALU.subtract)), VE)
                ep.w("gpsimd", VE)
                ep.op("gpsimd", _dma(d["xn0h"][fc, :, tok], oh), ST, 16)
                ep.op("gpsimd", _dma(d["xn0l"][fc, :, tok], ol), ST, 16)
        ep.w("gpsimd", ST)
        ep.run("p0")


def _p12_qk0(cx):
    nc = cx.nc
    d = cx.d
    LD, LD2, ST, TE, VE, SE = (cx.sem(n) for n in
                               ("LD", "LD2", "ST", "TE", "VE", "SE"))
    HG = 4
    W = HG * TT
    with (
        nc.sbuf_tensor("p12_wqh", [128, NF * D], BF16) as _wqh,
        nc.sbuf_tensor("p12_wql", [128, NF * D], BF16) as _wql,
        nc.sbuf_tensor("p12_wkh", [128, NF * D], BF16) as _wkh,
        nc.sbuf_tensor("p12_wkl", [128, NF * D], BF16) as _wkl,
    ):
        wqh = [_wqh[:, fc * D:(fc + 1) * D] for fc in range(NF)]
        wql = [_wql[:, fc * D:(fc + 1) * D] for fc in range(NF)]
        wkh = [_wkh[:, fc * D:(fc + 1) * D] for fc in range(NF)]
        wkl = [_wkl[:, fc * D:(fc + 1) * D] for fc in range(NF)]
        with nc.sbuf_tensor("p12_stage", [128, NF * D], F32) as _stage:
            stage = [_stage[:, fc * D:(fc + 1) * D] for fc in range(NF)]
            ep = EP(cx)
            _emit_weight_pair(ep, cx, d["wq0"], stage, wqh, wql)
            _emit_weight_pair(ep, cx, d["wk0"], stage, wkh, wkl)
            ep.run("p12w")
        with (
            nc.sbuf_tensor("p12_tbl", [128, 4 * S], F32) as _tbl,
            nc.sbuf_tensor("p12_xn", [128, NF * TT * 4], BF16) as _xn,
            nc.sbuf_tensor("p12_qsb", [128, W], F32) as qsb,
            nc.sbuf_tensor("p12_qsw", [128, W], F32) as qsw,
            nc.sbuf_tensor("p12_t2", [128, TT], F32) as t2s,
            nc.sbuf_tensor("p12_t1", [128, TT], F32) as t1s,
            nc.sbuf_tensor("p12_qf", [128, TT], F32) as qf,
            nc.sbuf_tensor("p12_oh", [128, W], BF16) as ohi,
            nc.sbuf_tensor("p12_ol", [128, W], BF16) as olo,
            nc.psum_tensor("p12_ps0", [128, TT], F32) as ps0,
            nc.psum_tensor("p12_ps1", [128, TT], F32) as ps1,
        ):
            tbl = [_tbl[:, i * S:(i + 1) * S] for i in range(4)]
            ep = EP(cx)
            for i in range(4):
                ep.op("sync", _dma(tbl[i], d["tbl_xpos"][i]), LD, 16)
            xnh = [[_xn[:, ((sl * 2 + 0) * NF + fc) * TT:((sl * 2 + 0) * NF + fc + 1) * TT]
                    for fc in range(NF)] for sl in range(2)]
            xnl = [[_xn[:, ((sl * 2 + 1) * NF + fc) * TT:((sl * 2 + 1) * NF + fc + 1) * TT]
                    for fc in range(NF)] for sl in range(2)]
            psums = [ps0, ps1]
            bank_free = [0, 0]
            rel = []
            st_hist = []
            for s in range(BPC):
                for tt in range(NQB):
                    ti = s * NQB + tt
                    sl = ti % 2
                    tok = slice(s * S + tt * TT, s * S + (tt + 1) * TT)
                    ttok = slice(tt * TT, (tt + 1) * TT)
                    if ti >= 2:
                        ep.w("sync", SE, rel[ti - 2])
                    for fc in range(NF):
                        ep.op("sync", _dma(xnh[sl][fc], d["xn0h"][fc, :, tok]), LD, 16)
                        ep.op("sync", _dma(xnl[sl][fc], d["xn0l"][fc, :, tok]), LD, 16)
                    ld_done = LD.v
                    for p, (wh, wl, qdh, qdl, ctb, stb) in enumerate((
                            (wqh, wql, d["q0h"], d["q0l"], tbl[0], tbl[1]),
                            (wkh, wkl, d["k0h"], d["k0l"], tbl[2], tbl[3]))):
                        for hg in range(2):
                            ep.w("tensor", LD, ld_done)
                            for hh in range(HG):
                                h = hg * HG + hh
                                bk = hh % 2
                                ps = psums[bk]
                                ocol = slice(h * 128, (h + 1) * 128)
                                if bank_free[bk] > 0:
                                    ep.w("tensor", SE, bank_free[bk])
                                for fc in range(NF):
                                    _mm3(ep, cx, ps[:], wh[fc][:, ocol],
                                         wl[fc][:, ocol], xnh[sl][fc], xnl[sl][fc],
                                         start=(fc == 0), stop=(fc == NF - 1),
                                         inc=TE)
                                ep.w("scalar", TE)
                                if hh == 0:
                                    ep.w("scalar", VE)
                                ep.op("scalar", (lambda e, o=qsb[:, hh * TT:(hh + 1) * TT],
                                                 i=ps[:]: e.copy(out=o, in_=i)), SE)
                                bank_free[bk] = SE.v
                            ep.op("scalar", _dma(qsw[0:64, :], qsb[64:128, :]), LD2, 16)
                            ep.op("scalar", _dma(qsw[64:128, :], qsb[0:64, :]), LD2, 16)
                            ep.w("vector", SE)
                            ep.w("vector", LD2)
                            if st_hist:
                                ep.w("vector", ST, st_hist[-1])
                            if P12_NOROT:
                                for hh in range(HG):
                                    hc = slice(hh * TT, (hh + 1) * TT)
                                    ep.op("vector", (lambda e, o=ohi[:, hc], i=qsb[:, hc]:
                                                     e.tensor_copy(out=o, in_=i)))
                                    ep.op("vector", (lambda e, o=olo[:, hc], a=qsb[:, hc],
                                                     b=ohi[:, hc]: e.tensor_tensor(
                                        out=o, in0=a, in1=b, op=ALU.subtract)), VE)
                            else:
                                for hh in range(HG):
                                  h = hg * HG + hh
                                  hc = slice(hh * TT, (hh + 1) * TT)
                                  ep.op("vector", (lambda e, o=t1s[:], i=qsb[:, hc],
                                                   t=ctb[:, ttok]: e.tensor_tensor(
                                      out=o, in0=i, in1=t, op=ALU.mult)))
                                  ep.op("vector", (lambda e, o=t2s[:], i=qsw[:, hc],
                                                   t=stb[:, ttok]: e.tensor_tensor(
                                      out=o, in0=i, in1=t, op=ALU.mult)))
                                  ep.op("vector", (lambda e, o=ohi[:, hc]: e.tensor_tensor(
                                      out=o, in0=t1s[:], in1=t2s[:], op=ALU.add)))
                                  ep.op("vector", (lambda e, o=qf[:]: e.tensor_tensor(
                                      out=o, in0=t1s[:], in1=t2s[:], op=ALU.add)))
                                  ep.op("vector", (lambda e, o=olo[:, hc], a=qf[:],
                                                   b=ohi[:, hc]: e.tensor_tensor(
                                      out=o, in0=a, in1=b, op=ALU.subtract)), VE)
                            ep.w("gpsimd", VE)
                            for hh in range(HG):
                                h = hg * HG + hh
                                hc = slice(hh * TT, (hh + 1) * TT)
                                ep.op("gpsimd", _dma(qdh[s, h, :, ttok], ohi[:, hc]), ST, 16)
                                ep.op("gpsimd", _dma(qdl[s, h, :, ttok], olo[:, hc]), ST, 16)
                            st_hist.append(ST.v)
                    rel.append(SE.v)
            ep.w("gpsimd", ST)
            ep.run("p12")


def _p34_vg0(cx):
    nc = cx.nc
    d = cx.d
    LD, ST, TE, VE, SE = (cx.sem(n) for n in ("LD", "ST", "TE", "VE", "SE"))
    with (
        nc.sbuf_tensor("p34_wvh", [128, NF * D], BF16) as _wvh,
        nc.sbuf_tensor("p34_wvl", [128, NF * D], BF16) as _wvl,
        nc.sbuf_tensor("p34_wgh", [128, NF * D], BF16) as _wgh,
        nc.sbuf_tensor("p34_wgl", [128, NF * D], BF16) as _wgl,
    ):
        wvh = [_wvh[:, fc * D:(fc + 1) * D] for fc in range(NF)]
        wvl = [_wvl[:, fc * D:(fc + 1) * D] for fc in range(NF)]
        wgh = [_wgh[:, fc * D:(fc + 1) * D] for fc in range(NF)]
        wgl = [_wgl[:, fc * D:(fc + 1) * D] for fc in range(NF)]
        with nc.sbuf_tensor("p34_stage", [128, NF * D], F32) as _stage:
            stage = [_stage[:, fc * D:(fc + 1) * D] for fc in range(NF)]
            ep = EP(cx)
            _emit_weight_pair(ep, cx, d["wv0"], stage, wvh, wvl)
            _emit_weight_pair(ep, cx, d["wg0"], stage, wgh, wgl)
            ep.run("p34w")
        with (
            nc.sbuf_tensor("p34_xn", [128, NF * TT * 4], BF16) as _xn,
            nc.sbuf_tensor("p34_vh", [128, D], BF16) as vh,
            nc.sbuf_tensor("p34_vl", [128, D], BF16) as vl,
            nc.sbuf_tensor("p34_g", [128, D], F32) as gst,
            nc.psum_tensor("p34_ps0", [128, TT], F32) as ps0,
            nc.psum_tensor("p34_ps1", [128, TT], F32) as ps1,
            nc.psum_tensor("p34_ps2", [128, TT], F32) as ps2,
            nc.psum_tensor("p34_ps3", [128, TT], F32) as ps3,
        ):
            xnh = [[_xn[:, ((sl * 2 + 0) * NF + fc) * TT:((sl * 2 + 0) * NF + fc + 1) * TT]
                    for fc in range(NF)] for sl in range(2)]
            xnl = [[_xn[:, ((sl * 2 + 1) * NF + fc) * TT:((sl * 2 + 1) * NF + fc + 1) * TT]
                    for fc in range(NF)] for sl in range(2)]
            ps = [ps0, ps1, ps2, ps3]
            rel = []
            ep = EP(cx)
            for s, tt in product(range(BPC), range(NQB)):
                ti = s * NQB + tt
                sl = ti % 2
                tok = slice(s * S + tt * TT, s * S + (tt + 1) * TT)
                if ti >= 2:
                    ep.w("sync", TE, rel[ti - 2])
                for fc in range(NF):
                    ep.op("sync", _dma(xnh[sl][fc], d["xn0h"][fc, :, tok]), LD, 16)
                    ep.op("sync", _dma(xnl[sl][fc], d["xn0l"][fc, :, tok]), LD, 16)
                ld_done = LD.v
                for cc in range(4):
                    c = tt * 4 + cc
                    cs = slice(cc * CH, (cc + 1) * CH)
                    ep.w("tensor", LD, ld_done)
                    ep.w("tensor", SE)
                    ep.w("tensor", VE)
                    for ih, tgt in product(range(2), range(2)):
                        half = (slice(0, TT), slice(TT, D))[ih]
                        wh, wl = ((wvh, wvl), (wgh, wgl))[tgt]
                        pp = ps[tgt * 2 + ih]
                        for fc in range(NF):
                            _mm3(ep, cx, pp[:], xnh[sl][fc][:, cs],
                                 xnl[sl][fc][:, cs],
                                 wh[fc][:, half], wl[fc][:, half],
                                 start=(fc == 0), stop=(fc == NF - 1),
                                 inc=TE)
                    te_done = TE.v
                    ep.w("scalar", TE, te_done)
                    ep.w("scalar", ST)
                    for ih in range(2):
                        hw = slice(ih * TT, (ih + 1) * TT)
                        ep.op("scalar", (lambda e, o=vh[:, hw], i=ps[ih][:]:
                                         e.copy(out=o, in_=i)), SE)
                        ep.op("scalar", (lambda e, o=gst[:, hw], i=ps[2 + ih][:]:
                                         e.activation(out=o, in_=i,
                                                      func=ACT.Silu)), SE)
                    ep.w("vector", SE)
                    ep.w("vector", ST)
                    for ih in range(2):
                        hw = slice(ih * TT, (ih + 1) * TT)
                        ep.op("vector", (lambda e, o=vl[:, hw], a=ps[ih][:],
                                         b=vh[:, hw]: e.tensor_tensor(
                            out=o, in0=a, in1=b, op=ALU.subtract)), VE)
                    ep.w("gpsimd", SE)
                    ep.w("gpsimd", VE)
                    for h in range(H):
                        hc2 = slice(h * HD, (h + 1) * HD)
                        cc2 = slice(c * CH, (c + 1) * CH)
                        ep.op("gpsimd", _dma(d["v0h"][s, h, :, cc2], vh[:, hc2]),
                              ST, 16)
                        ep.op("gpsimd", _dma(d["v0l"][s, h, :, cc2], vl[:, hc2]),
                              ST, 16)
                    ep.op("gpsimd", _dma(d["g0"][s, c], gst[:]), ST, 16)
                rel.append(TE.v)
            ep.w("gpsimd", ST)
            ep.run("p34")


def _pf12_mix(cx):
    nc = cx.nc
    d = cx.d
    LD, ST, TE, VE, SE = (cx.sem(n) for n in ("LD", "ST", "TE", "VE", "SE"))
    TP = TT + 1
    with (
        nc.sbuf_tensor("pf_wrh", [128, NF * D], BF16) as _wrh,
        nc.sbuf_tensor("pf_wrl", [128, NF * D], BF16) as _wrl,
        nc.sbuf_tensor("pf_wkh", [128, NF * D], BF16) as _wkh,
        nc.sbuf_tensor("pf_wkl", [128, NF * D], BF16) as _wkl,
        nc.sbuf_tensor("pf_mu", [128, 16], F32) as mu,
    ):
        wrh = [_wrh[:, fc * D:(fc + 1) * D] for fc in range(NF)]
        wrl = [_wrl[:, fc * D:(fc + 1) * D] for fc in range(NF)]
        wkh = [_wkh[:, fc * D:(fc + 1) * D] for fc in range(NF)]
        wkl = [_wkl[:, fc * D:(fc + 1) * D] for fc in range(NF)]
        with nc.sbuf_tensor("pf_stage", [128, NF * D], F32) as _stage:
            stage = [_stage[:, fc * D:(fc + 1) * D] for fc in range(NF)]
            ep = EP(cx)
            for fc in range(NF):
                ep.op("sync", _dma(mu[:, fc:fc + 1], d["mur_col"][fc]), LD, 16)
                ep.op("sync", _dma(mu[:, 8 + fc:9 + fc], d["muk_col"][fc]), LD, 16)
            _emit_weight_pair(ep, cx, d["wr0"], stage, wrh, wrl)
            _emit_weight_pair(ep, cx, d["wk0f"], stage, wkh, wkl)
            ep.run("pfw")
        with (
            nc.sbuf_tensor("pf_xn", [128, NF * TP * 4], BF16) as _xn,
            nc.sbuf_tensor("pf_xfz", [128, TP], F32) as xfz,
            nc.sbuf_tensor("pf_xf", [128, TP], F32) as xf,
            nc.sbuf_tensor("pf_dd", [128, TT], F32) as dd,
            nc.sbuf_tensor("pf_m32", [128, TT], F32) as m32,
            nc.sbuf_tensor("pf_xr", [128, NF * TT * 2], BF16) as _xr,
            nc.sbuf_tensor("pf_xk", [128, NF * TT * 2], BF16) as _xk,
            nc.sbuf_tensor("pf_t", [128, TT], F32) as tfu,
            nc.sbuf_tensor("pf_oh", [128, TT], BF16) as koh,
            nc.sbuf_tensor("pf_ol", [128, TT], BF16) as kol,
            nc.sbuf_tensor("pf_r", [128, TT], F32) as rst,
            nc.psum_tensor("pf_ps0", [128, TT], F32) as ps0,
            nc.psum_tensor("pf_ps1", [128, TT], F32) as ps1,
        ):
            xnh = [[_xn[:, ((sl * 2 + 0) * NF + fc) * TP:((sl * 2 + 0) * NF + fc + 1) * TP]
                    for fc in range(NF)] for sl in range(2)]
            xnl = [[_xn[:, ((sl * 2 + 1) * NF + fc) * TP:((sl * 2 + 1) * NF + fc + 1) * TP]
                    for fc in range(NF)] for sl in range(2)]
            xrh = [_xr[:, (2 * fc) * TT:(2 * fc + 1) * TT] for fc in range(NF)]
            xrl = [_xr[:, (2 * fc + 1) * TT:(2 * fc + 2) * TT] for fc in range(NF)]
            xkh = [_xk[:, (2 * fc) * TT:(2 * fc + 1) * TT] for fc in range(NF)]
            xkl = [_xk[:, (2 * fc + 1) * TT:(2 * fc + 2) * TT] for fc in range(NF)]
            psums = [ps0, ps1]
            rel = []
            bank_se = [0, 0]
            ep = EP(cx)
            ep.op("vector", lambda e: e.memset(xfz[:], 0.0), VE)
            for s, tt in product(range(BPC), range(NQB)):
                    ti = s * NQB + tt
                    sl = ti % 2
                    t0 = s * S + tt * TT
                    tok = slice(t0, t0 + TT)
                    if ti >= 2:
                        ep.w("sync", TE, rel[ti - 2])
                    for fc, (dr, buf) in product(range(NF),
                                                 ((d["xn0h"], xnh), (d["xn0l"], xnl))):
                        if tt == 0:
                            ep.op("sync", _dma(buf[sl][fc][:, 1:TP],
                                               dr[fc, :, tok]), LD, 16)
                        else:
                            ep.op("sync", _dma(buf[sl][fc],
                                               dr[fc, :, t0 - 1:t0 + TT]), LD, 16)
                    ld_done = LD.v
                    ep.w("vector", LD, ld_done)
                    if ti >= 1:
                        ep.w("vector", TE, rel[ti - 1])
                    for fc in range(NF):
                        xfb = xfz if tt == 0 else xf
                        if tt == 0:
                            ep.op("vector", (lambda e, a=xnh[sl][fc], b=xnl[sl][fc]:
                                             e.tensor_tensor(out=xfz[:, 1:TP],
                                                             in0=a[:, 1:TP],
                                                             in1=b[:, 1:TP],
                                                             op=ALU.add)))
                        else:
                            ep.op("vector", (lambda e, a=xnh[sl][fc], b=xnl[sl][fc]:
                                             e.tensor_tensor(out=xf[:], in0=a, in1=b,
                                                             op=ALU.add)))
                        ep.op("vector", (lambda e, xx=xfb: e.tensor_tensor(
                            out=dd[:], in0=xx[:, 1:TP], in1=xx[:, 0:TT],
                            op=ALU.subtract)))
                        for (mc, hh, ll) in ((mu[:, fc:fc + 1], xrh, xrl),
                                             (mu[:, 8 + fc:9 + fc], xkh, xkl)):
                            ep.op("vector", (lambda e, m=mc, xx=xfb: e.scalar_tensor_tensor(
                                out=m32[:], in0=dd[:], scalar=m, in1=xx[:, 0:TT],
                                op0=ALU.mult, op1=ALU.add)), VE)
                            ep.w("scalar", VE)
                            ep.op("scalar", (lambda e, o=hh[fc]: e.copy(
                                out=o, in_=m32[:])), SE)
                            ep.w("vector", SE)
                            ep.op("vector", (lambda e, o=ll[fc], b=hh[fc]:
                                             e.tensor_tensor(out=o, in0=m32[:],
                                                             in1=b,
                                                             op=ALU.subtract)), VE)
                    mix_done = VE.v
                    for tgt, oc in product(range(2), range(NF)):
                        pp = psums[oc % 2]
                        ocol = slice(oc * 128, (oc + 1) * 128)
                        ep.w("tensor", VE, mix_done)
                        if bank_se[oc % 2]:
                            ep.w("tensor", SE, bank_se[oc % 2])
                        ww = ((wrh, wrl), (wkh, wkl))[tgt]
                        aa = ((xrh, xrl), (xkh, xkl))[tgt]
                        for fc in range(NF):
                            _mm3(ep, cx, pp[:], ww[0][fc][:, ocol],
                                 ww[1][fc][:, ocol], aa[0][fc], aa[1][fc],
                                 start=(fc == 0), stop=(fc == NF - 1),
                                 inc=TE)
                        ep.w("scalar", TE)
                        ep.w("scalar", ST)
                        if tgt == 0:
                            ep.op("scalar", (lambda e, i=pp[:]: e.activation(
                                out=rst[:], in_=i, func=ACT.Sigmoid)), SE)
                            bank_se[oc % 2] = SE.v
                            ep.w("gpsimd", SE)
                            ep.op("gpsimd", _dma(d["r0"][oc, :, tok], rst[:]),
                                  ST, 16)
                        else:
                            ep.op("scalar", (lambda e, i=pp[:]: e.activation(
                                out=tfu[:], in_=i, func=ACT.Relu)), SE)
                            bank_se[oc % 2] = SE.v
                            ep.w("vector", SE)
                            ep.op("vector", (lambda e: e.tensor_tensor(
                                out=m32[:], in0=tfu[:], in1=tfu[:],
                                op=ALU.mult)), VE)
                            ep.w("scalar", VE)
                            ep.op("scalar", (lambda e: e.copy(
                                out=koh[:], in_=m32[:])), SE)
                            ep.w("vector", SE)
                            ep.op("vector", (lambda e: e.tensor_tensor(
                                out=kol[:], in0=m32[:], in1=koh[:],
                                op=ALU.subtract)), VE)
                            ep.w("gpsimd", VE)
                            ep.op("gpsimd", _dma(d["kah"][oc, :, tok], koh[:]),
                                  ST, 16)
                            ep.op("gpsimd", _dma(d["kal"][oc, :, tok], kol[:]),
                                  ST, 16)
                    rel.append(TE.v)
            ep.w("gpsimd", ST)
            ep.run("pf12")


def _pf3_ffn0(cx):
    nc = cx.nc
    d = cx.d
    LD, ST, TE, VE, SE = (cx.sem(n) for n in ("LD", "ST", "TE", "VE", "SE"))
    with (
        nc.sbuf_tensor("p3_wh", [128, NF * D], BF16) as _wh,
        nc.sbuf_tensor("p3_wl", [128, NF * D], BF16) as _wl,
    ):
        wh = [_wh[:, fc * D:(fc + 1) * D] for fc in range(NF)]
        wl = [_wl[:, fc * D:(fc + 1) * D] for fc in range(NF)]
        with nc.sbuf_tensor("p3_stage", [128, NF * D], F32) as _stage:
            stage = [_stage[:, fc * D:(fc + 1) * D] for fc in range(NF)]
            ep = EP(cx)
            _emit_weight_pair(ep, cx, d["wv0f"], stage, wh, wl)
            ep.run("p3w")
        with (
            nc.sbuf_tensor("p3_ka", [128, NF * TT * 4], BF16) as _ka,
            nc.sbuf_tensor("p3_r", [128, NF * TT * 2], F32) as _r,
            nc.sbuf_tensor("p3_o", [128, TT * 2], F32) as ost,
            nc.psum_tensor("p3_ps0", [128, TT], F32) as ps0,
            nc.psum_tensor("p3_ps1", [128, TT], F32) as ps1,
        ):
            kah = [[_ka[:, ((sl * 2 + 0) * NF + fc) * TT:((sl * 2 + 0) * NF + fc + 1) * TT]
                    for fc in range(NF)] for sl in range(2)]
            kal = [[_ka[:, ((sl * 2 + 1) * NF + fc) * TT:((sl * 2 + 1) * NF + fc + 1) * TT]
                    for fc in range(NF)] for sl in range(2)]
            rti = [[_r[:, (sl * NF + oc) * TT:(sl * NF + oc + 1) * TT]
                    for oc in range(NF)] for sl in range(2)]
            psums = [ps0, ps1]
            rel = []
            bank_ve = [0, 0]
            ep = EP(cx)
            for s in range(BPC):
                for tt in range(NQB):
                    ti = s * NQB + tt
                    sl = ti % 2
                    tok = slice(s * S + tt * TT, s * S + (tt + 1) * TT)
                    if ti >= 2:
                        ep.w("sync", VE, rel[ti - 2])
                    for fc in range(NF):
                        ep.op("sync", _dma(kah[sl][fc], d["kah"][fc, :, tok]), LD, 16)
                        ep.op("sync", _dma(kal[sl][fc], d["kal"][fc, :, tok]), LD, 16)
                        ep.op("sync", _dma(rti[sl][fc], d["r0"][fc, :, tok]), LD, 16)
                    ld_done = LD.v
                    for oc in range(NF):
                        pp = psums[oc % 2]
                        ocol = slice(oc * 128, (oc + 1) * 128)
                        ep.w("tensor", LD, ld_done)
                        if bank_ve[oc % 2]:
                            ep.w("tensor", VE, bank_ve[oc % 2])
                        for fc in range(NF):
                            _mm3(ep, cx, pp[:], wh[fc][:, ocol], wl[fc][:, ocol],
                                 kah[sl][fc], kal[sl][fc],
                                 start=(fc == 0), stop=(fc == NF - 1), inc=TE)
                        ep.w("vector", TE)
                        ep.w("vector", ST)
                        ob = ost[:, (oc % 2) * TT:(oc % 2 + 1) * TT]
                        ep.op("vector", (lambda e, o=ob, a=pp[:], r=rti[sl][oc]:
                                         e.tensor_tensor(out=o, in0=a, in1=r,
                                                         op=ALU.mult)), VE)
                        bank_ve[oc % 2] = VE.v
                        ep.w("gpsimd", VE)
                        ep.op("gpsimd", _dma(d["f0"][oc, :, tok], ob), ST, 16)
                    rel.append(VE.v)
            ep.w("gpsimd", ST)
            ep.run("pf3")


def _p5_ret0(cx):
    nc = cx.nc
    d = cx.d
    LD, LD2, ST, TE, VE, SE, GP = (cx.sem(n) for n in
                                   ("LD", "LD2", "ST", "TE", "VE", "SE", "GP"))
    DMW = S + 384
    with (
        nc.sbuf_tensor("p5_woh", [128, NF * D], BF16) as _woh,
        nc.sbuf_tensor("p5_wol", [128, NF * D], BF16) as _wol,
        nc.sbuf_tensor("p5_gnw", [128, D], F32) as gnwB,
        nc.sbuf_tensor("p5_gnb", [128, D], F32) as gnbB,
        nc.sbuf_tensor("p5_ident", [128, 128], F32) as ident,
        nc.sbuf_tensor("p5_ones", [128, 1], F32) as ones_col,
        nc.sbuf_tensor("p5_onesr", [1, 128], F32) as ones_row,
        nc.sbuf_tensor("p5_rows", [1, 6 * TT], F32) as rows,
    ):
        woh = [_woh[:, fc * D:(fc + 1) * D] for fc in range(NF)]
        wol = [_wol[:, fc * D:(fc + 1) * D] for fc in range(NF)]
        with (nc.sbuf_tensor("p5_stage", [128, NF * D], F32) as _stage,
              nc.psum_tensor("p5_bc", [128, TT], F32) as psbc):
            stage = [_stage[:, fc * D:(fc + 1) * D] for fc in range(NF)]
            ep = EP(cx)
            _emit_weight_pair(ep, cx, d["wo0"], stage, woh, wol)
            ep.op("sync", _dma(ident[:], d["tbl_ident"]), LD, 16)
            ep.op("sync", _dma(rows[0:1, 0:D], d["gnw_row"]), LD, 16)
            ep.op("sync", _dma(rows[0:1, D:2 * D], d["gnb_row"]), LD, 16)
            ep.op("vector", lambda e: e.memset(ones_col[:], 1.0), VE)
            ep.op("vector", lambda e: e.memset(ones_row[:], 1.0), VE)
            ep.w("tensor", LD)
            ep.w("tensor", VE)
            for i, dst in enumerate((gnwB, gnbB)):
                for hw in range(2):
                    ep.op("tensor", (lambda e, r=rows[0:1, i * D + hw * TT:
                                                      i * D + (hw + 1) * TT]:
                                     e.matmul(out=psbc[:], lhsT=ones_row[:],
                                              rhs=r, start=True, stop=True)), TE)
                    ep.w("vector", TE)
                    ep.op("vector", (lambda e, o=dst[:, hw * TT:(hw + 1) * TT]:
                                     e.tensor_copy(out=o, in_=psbc[:])), VE)
                    ep.w("tensor", VE)
            ep.run("p5w")
        with ExitStack() as _es:
            _qb = _es.enter_context(nc.sbuf_tensor("p5_q", [128, TT * 4], BF16))
            _kb = _es.enter_context(nc.sbuf_tensor("p5_k", [128, S * 4], BF16))
            _vb = _es.enter_context(nc.sbuf_tensor("p5_v", [128, S * 4], BF16))
            _dmb = _es.enter_context(nc.sbuf_tensor("p5_dm", [128, DMW * 2], F32))
            gbuf = _es.enter_context(nc.sbuf_tensor("p5_g", [128, 4 * D], F32))
            fbuf = _es.enter_context(nc.sbuf_tensor("p5_f0", [128, 2 * TT], F32))
            xbuf = _es.enter_context(nc.sbuf_tensor("p5_x", [128, 2 * TT], F32))
            _ph = _es.enter_context(nc.sbuf_tensor("p5_ph", [128, TT * 2], BF16))
            _pl = _es.enter_context(nc.sbuf_tensor("p5_pl", [128, TT * 2], BF16))
            _p32 = _es.enter_context(nc.sbuf_tensor("p5_p32", [128, TT * 2], F32))
            sqt = _es.enter_context(nc.sbuf_tensor("p5_sq", [128, TT], F32))
            cols = _es.enter_context(nc.sbuf_tensor("p5_cols", [128, 32], F32))
            ynt = _es.enter_context(nc.sbuf_tensor("p5_yn", [128, 128], F32))
            gyt = _es.enter_context(nc.sbuf_tensor("p5_gy", [128, 128], F32))
            _gyh = _es.enter_context(nc.sbuf_tensor("p5_gyh", [128, NF * TT], BF16))
            _gyl = _es.enter_context(nc.sbuf_tensor("p5_gyl", [128, NF * TT], BF16))
            x1s = _es.enter_context(nc.sbuf_tensor("p5_x1", [128, NF * TT], F32))
            xn1h_s = _es.enter_context(nc.sbuf_tensor("p5_xnh", [128, NF * TT], BF16))
            xn1l_s = _es.enter_context(nc.sbuf_tensor("p5_xnl", [128, NF * TT], BF16))
            tmp = _es.enter_context(nc.sbuf_tensor("p5_tmp", [128, TT], F32))
            pS0 = _es.enter_context(nc.psum_tensor("p5_pS0", [128, TT], F32))
            pS1 = _es.enter_context(nc.psum_tensor("p5_pS1", [128, TT], F32))
            pY0 = _es.enter_context(nc.psum_tensor("p5_pY0", [128, TT], F32))
            pY1 = _es.enter_context(nc.psum_tensor("p5_pY1", [128, TT], F32))
            pT = _es.enter_context(nc.psum_tensor("p5_pT", [128, 128], F32))
            pW = _es.enter_context(nc.psum_tensor("p5_pW", [128, TT], F32))
            psA = _es.enter_context(nc.psum_tensor("p5_pA", [128, TT], F32))
            psB = _es.enter_context(nc.psum_tensor("p5_pB", [128, TT], F32))
            qh = [_qb[:, i * TT:(i + 1) * TT] for i in range(4)]
            kbs = [_kb[:, i * S:(i + 1) * S] for i in range(4)]
            vbs = [_vb[:, i * S:(i + 1) * S] for i in range(4)]
            dmb = [_dmb[:, i * DMW:(i + 1) * DMW] for i in range(2)]
            phb = [_ph[:, i * TT:(i + 1) * TT] for i in range(2)]
            plb = [_pl[:, i * TT:(i + 1) * TT] for i in range(2)]
            p32b = [_p32[:, i * TT:(i + 1) * TT] for i in range(2)]
            gyh = [_gyh[:, fc * TT:(fc + 1) * TT] for fc in range(NF)]
            gyl = [_gyl[:, fc * TT:(fc + 1) * TT] for fc in range(NF)]
            mrow = rows[0:1, 2 * D:2 * D + TT]
            m2row = rows[0:1, 2 * D + TT:2 * D + 2 * TT]
            pSb = [pS0, pS1]
            pYb = [pY0, pY1]
            ep = EP(cx)
            yfree = [0, 0]
            gyfree = [0]
            hfree = []
            hseq = [0]
            prev_gy = [0]
            for s, qb in product(range(BPC), range(NQB)):
                nk = 4 * (qb + 1)
                tok = slice(s * S + qb * TT, s * S + (qb + 1) * TT)
                stok = slice(qb * TT, (qb + 1) * TT)
                ep.w("sync", VE, prev_gy[0])
                for cc in range(4):
                    ep.op("sync", _dma(gbuf[:, cc * D:(cc + 1) * D],
                                       d["g0"][s, qb * 4 + cc]), LD, 16)
                def _head(h):
                    hs = hseq[0] % 2
                    if len(hfree) >= 2:
                        ep.w("sync", TE, hfree[-2][0])
                        ep.w("sync", VE, hfree[-2][1])
                    qhh, qhl = qh[2 * hs], qh[2 * hs + 1]
                    kbh, kbl = kbs[2 * hs], kbs[2 * hs + 1]
                    vbh, vbl = vbs[2 * hs], vbs[2 * hs + 1]
                    dmt = dmb[hs]
                    ep.op("sync", _dma(qhh, d["q0h"][s, h, :, stok]), LD, 16)
                    ep.op("sync", _dma(qhl, d["q0l"][s, h, :, stok]), LD, 16)
                    ep.op("sync", _dma(kbh[:, 0:nk * CH],
                                       d["k0h"][s, h, :, 0:nk * CH]), LD, 16)
                    ep.op("sync", _dma(kbl[:, 0:nk * CH],
                                       d["k0l"][s, h, :, 0:nk * CH]), LD, 16)
                    ep.op("sync", _dma(vbh[:, 0:nk * CH],
                                       d["v0h"][s, h, :, 0:nk * CH]), LD, 16)
                    ep.op("sync", _dma(vbl[:, 0:nk * CH],
                                       d["v0l"][s, h, :, 0:nk * CH]), LD, 16)
                    ep.op("sync", _dma(dmt[:, 0:384 + (qb + 1) * TT],
                                       d["tbl_dm"][h, :, 0:384 + (qb + 1) * TT]),
                          LD, 16)
                    ld_done = LD.v
                    pY = pYb[h % 2]
                    ep.w("tensor", LD, ld_done)
                    ep.w("tensor", VE, yfree[h % 2])
                    first_y = [True]
                    pvals = {}
                    tvals = {}

                    def _st(kt):
                        pS = pSb[kt % 2]
                        if kt >= 2:
                            ep.w("tensor", VE, pvals[kt - 2])
                        kc = slice(kt * CH, (kt + 1) * CH)
                        _mm3(ep, cx, pS[:], kbh[:, kc], kbl[:, kc],
                             qhh, qhl, start=True, stop=True, inc=TE)

                    def _pb(kt):
                        pS = pSb[kt % 2]
                        sl = kt % 2
                        x0 = 384 + qb * TT - kt * CH
                        if kt >= 2:
                            ep.w("vector", TE, tvals[kt - 2])
                        ep.w("vector", TE)
                        ep.op("vector", (lambda e, o=phb[sl], a=pS[:],
                                         m=dmt[:, x0:x0 + TT]: e.tensor_tensor(
                            out=o, in0=a, in1=m, op=ALU.mult)))
                        ep.op("vector", (lambda e, o=p32b[sl], a=pS[:],
                                         m=dmt[:, x0:x0 + TT]: e.tensor_tensor(
                            out=o, in0=a, in1=m, op=ALU.mult)))
                        ep.op("vector", (lambda e, o=plb[sl], a=p32b[sl],
                                         b=phb[sl]: e.tensor_tensor(
                            out=o, in0=a, in1=b, op=ALU.subtract)), VE)
                        pvals[kt] = VE.v

                    def _pv(kt, last_k=False):
                        sl = kt % 2
                        kc = slice(kt * CH, (kt + 1) * CH)
                        ep.w("tensor", VE, pvals[kt])
                        for qs in range(4):
                            qc = slice(qs * 128, (qs + 1) * 128)
                            st = first_y[0]
                            first_y[0] = False
                            ep.op("tensor", (lambda e, o=pY[:, qc], l=phb[sl][:, qc],
                                             r=vbh[:, kc], s0=st: e.matmul(
                                out=o, lhsT=l, rhs=r, start=s0, stop=False)))
                            ep.op("tensor", (lambda e, o=pY[:, qc], l=plb[sl][:, qc],
                                             r=vbh[:, kc]: e.matmul(
                                out=o, lhsT=l, rhs=r, start=False, stop=False)))
                            ep.op("tensor", (lambda e, o=pY[:, qc], l=phb[sl][:, qc],
                                             r=vbl[:, kc],
                                             s1=(last_k and qs == 3): e.matmul(
                                out=o, lhsT=l, rhs=r, start=False, stop=s1)),
                                TE if qs == 3 else None)
                        tvals[kt] = TE.v

                    _st(0)
                    _pb(0)
                    for kt in range(1, nk):
                        _st(kt)
                        _pb(kt)
                        _pv(kt - 1)
                    _pv(nk - 1, last_k=True)
                    hfree.append((TE.v, VE.v))
                    hseq[0] += 1
                    ep.w("scalar", TE)
                    ep.w("vector", TE)
                    for qs in range(4):
                        qc = slice(qs * 128, (qs + 1) * 128)
                        c0, c1, c2, c3, c4 = (cols[:, i:i + 1] for i in range(5))
                        ep.op("scalar", (lambda e, i=pY[:, qc]: e.activation(
                            out=sqt[:, 0:128], in_=i, func=ACT.Square)), SE)
                        ep.w("vector", SE)
                        ep.op("vector", (lambda e, i=pY[:, qc]: e.reduce_sum(
                            out=c0, in_=i, axis=AX.X)))
                        ep.op("vector", (lambda e: e.reduce_sum(
                            out=c1, in_=sqt[:, 0:128], axis=AX.X)), VE)
                        ep.w("scalar", VE)
                        ep.op("scalar", (lambda e: e.activation(
                            out=c2, in_=c0, func=ACT.Copy, scale=1.0 / 128)))
                        ep.op("scalar", (lambda e: e.activation(
                            out=c3, in_=c2, func=ACT.Square)))
                        ep.op("scalar", (lambda e: e.activation(
                            out=c4, in_=c1, func=ACT.Copy, scale=1.0 / 128)), SE)
                        ep.w("vector", SE)
                        ep.op("vector", (lambda e: e.tensor_tensor(
                            out=c4, in0=c4, in1=c3, op=ALU.subtract)))
                        ep.op("vector", (lambda e: e.tensor_scalar(
                            out=c4, in0=c4, scalar1=1e-5, scalar2=None,
                            op0=ALU.add)), VE)
                        ep.w("scalar", VE)
                        ep.op("scalar", (lambda e: e.activation(
                            out=c3, in_=c4, func=ACT.Sqrt)), SE)
                        ep.w("vector", SE)
                        ep.op("vector", (lambda e: e.reciprocal(out=c4, in_=c3)))
                        ep.op("vector", (lambda e, i=pY[:, qc]: e.tensor_scalar(
                            out=ynt[:], in0=i, scalar1=c2, scalar2=c4,
                            op0=ALU.subtract, op1=ALU.mult)))
                        ep.op("vector", (lambda e, w=gnwB[:, h * HD:(h + 1) * HD]:
                                         e.tensor_tensor(out=ynt[:], in0=ynt[:],
                                                         in1=w, op=ALU.mult)))
                        ep.op("vector", (lambda e, w=gnbB[:, h * HD:(h + 1) * HD]:
                                         e.tensor_tensor(out=ynt[:], in0=ynt[:],
                                                         in1=w, op=ALU.add)))
                        ep.op("vector", (lambda e, g=gbuf[:, qs * D + h * HD:
                                                          qs * D + (h + 1) * HD]:
                                         e.tensor_tensor(out=gyt[:], in0=ynt[:],
                                                         in1=g, op=ALU.mult)), VE)
                        ep.w("tensor", VE)
                        ep.w("tensor", SE)
                        ep.op("tensor", (lambda e: e.transpose(
                            out=pT[:], in_=gyt[:], identity=ident[:])), TE)
                        ep.w("scalar", TE)
                        if h == 0 and qs == 0:
                            ep.w("scalar", TE, gyfree[0])
                        ep.op("scalar", (lambda e, o=gyh[h][:, qc]: e.copy(
                            out=o, in_=pT[:])), SE)
                        ep.w("vector", SE)
                        ep.op("vector", (lambda e, o=gyl[h][:, qc], b=gyh[h][:, qc]:
                                         e.tensor_tensor(out=o, in0=pT[:], in1=b,
                                                         op=ALU.subtract)), VE)
                    yfree[h % 2] = VE.v

                for h in range(H):
                    _head(h)
                gy_done = VE.v
                prev_gy[0] = gy_done
                fx_ve = []
                for mc in range(NF):
                    mcol = slice(mc * 128, (mc + 1) * 128)
                    if mc >= 2:
                        ep.w("sync", VE, fx_ve[mc - 2])
                    mslot = slice((mc % 2) * TT, (mc % 2 + 1) * TT)
                    ep.op("sync", _dma(fbuf[:, mslot], d["f0"][mc, :, tok]),
                          LD, 16)
                    ep.op("sync", _dma(xbuf[:, mslot], d["x_in"][mc, :, tok]),
                          LD, 16)
                    ld_fx = LD.v
                    ep.w("tensor", VE, gy_done)
                    ep.w("tensor", SE)
                    if mc == 0:
                        ep.w("vector", ST)
                    for fc in range(NF):
                        _mm3(ep, cx, pW[:], woh[fc][:, mcol], wol[fc][:, mcol],
                             gyh[fc], gyl[fc], start=(fc == 0), stop=(fc == NF - 1),
                             inc=TE)
                    ep.w("vector", TE)
                    ep.w("vector", LD, ld_fx)
                    ep.op("vector", (lambda e, f=fbuf[:, mslot]:
                                     e.tensor_tensor(out=tmp[:], in0=pW[:], in1=f,
                                                     op=ALU.add)))
                    ep.op("vector", (lambda e, x=xbuf[:, mslot],
                                     o=x1s[:, mc * TT:(mc + 1) * TT]:
                                     e.tensor_tensor(out=o, in0=tmp[:], in1=x,
                                                     op=ALU.add)), VE)
                    fx_ve.append(VE.v)
                    ep.w("scalar", VE)
                    ep.op("scalar", (lambda e, i=x1s[:, mc * TT:(mc + 1) * TT]:
                                     e.activation(out=sqt[:], in_=i,
                                                  func=ACT.Square)), SE)
                    ep.w("tensor", VE)
                    ep.w("tensor", SE)
                    ep.op("tensor", (lambda e, r=x1s[:, mc * TT:(mc + 1) * TT],
                                     s0=(mc == 0), p0=(mc == NF - 1): e.matmul(
                        out=psA[0:1, :], lhsT=ones_col[:], rhs=r, start=s0,
                        stop=p0)))
                    ep.op("tensor", (lambda e, s0=(mc == 0), p0=(mc == NF - 1):
                                     e.matmul(
                        out=psB[0:1, :], lhsT=ones_col[:], rhs=sqt[:], start=s0,
                        stop=p0)), TE)
                ep.w("scalar", TE)
                ep.op("scalar", (lambda e: e.activation(out=mrow, in_=psA[0:1, :],
                                                        func=ACT.Copy,
                                                        scale=1.0 / 1024)))
                ep.op("scalar", (lambda e: e.activation(out=m2row, in_=mrow,
                                                        func=ACT.Square)), SE)
                ep.w("vector", SE)
                ep.op("vector", (lambda e: e.tensor_scalar(
                    out=m2row, in0=m2row, scalar1=-1.0, scalar2=None,
                    op0=ALU.mult)))
                ep.op("vector", (lambda e: e.scalar_tensor_tensor(
                    out=m2row, in0=psB[0:1, :], scalar=1.0 / 1024, in1=m2row,
                    op0=ALU.mult, op1=ALU.add)))
                ep.op("vector", (lambda e: e.tensor_scalar(
                    out=m2row, in0=m2row, scalar1=1e-5, scalar2=None,
                    op0=ALU.add)), VE)
                ep.w("scalar", VE)
                ep.op("scalar", (lambda e: e.activation(out=m2row, in_=m2row,
                                                        func=ACT.Sqrt)), SE)
                ep.w("vector", SE)
                ep.op("vector", (lambda e: e.reciprocal(out=m2row, in_=m2row)), VE)
                ep.w("tensor", VE)
                ep.w("tensor", SE)
                ep.op("tensor", (lambda e: e.matmul(out=psA[:], lhsT=ones_row[:],
                                                    rhs=mrow, start=True,
                                                    stop=True)))
                ep.op("tensor", (lambda e: e.matmul(out=psB[:], lhsT=ones_row[:],
                                                    rhs=m2row, start=True,
                                                    stop=True)), TE)
                ep.w("vector", TE)
                ep.w("vector", ST)
                for mc in range(NF):
                    mm = slice(mc * TT, (mc + 1) * TT)
                    ep.op("vector", (lambda e, x=x1s[:, mm]: e.tensor_tensor(
                        out=tmp[:], in0=x, in1=psA[:], op=ALU.subtract)))
                    ep.op("vector", (lambda e, o=sqt[:]: e.tensor_tensor(
                        out=o, in0=tmp[:], in1=psB[:], op=ALU.mult)), VE)
                    ep.w("scalar", VE)
                    ep.op("scalar", (lambda e, o=xn1h_s[:, mm]: e.copy(
                        out=o, in_=sqt[:])), SE)
                    ep.w("vector", SE)
                    ep.op("vector", (lambda e, o=xn1l_s[:, mm], b=xn1h_s[:, mm]:
                                     e.tensor_tensor(out=o, in0=sqt[:], in1=b,
                                                     op=ALU.subtract)), VE)
                gyfree[0] = TE.v
                ep.w("gpsimd", VE)
                for fc in range(NF):
                    mm2 = slice(fc * TT, (fc + 1) * TT)
                    ep.op("gpsimd", _dma(d["x1"][fc, :, tok], x1s[:, mm2]), ST, 16)
                    ep.op("gpsimd", _dma(d["xn1h"][fc, :, tok], xn1h_s[:, mm2]),
                          ST, 16)
                    ep.op("gpsimd", _dma(d["xn1l"][fc, :, tok], xn1l_s[:, mm2]),
                          ST, 16)
            ep.w("gpsimd", ST)
            ep.run("p5")


def _p6_kv1(cx):
    nc = cx.nc
    d = cx.d
    LD, LD2, ST, TE, VE, SE = (cx.sem(n) for n in
                               ("LD", "LD2", "ST", "TE", "VE", "SE"))
    p = cx.p67
    RC, RS, w8, ident, bcols, cmsk = (p["RC"], p["RS"], p["w8"], p["ident"],
                                      p["bcols"], p["cmsk"])
    with ExitStack() as es6:
        wkh_ = es6.enter_context(nc.sbuf_tensor("p6_wkh", [128, NF * D], BF16))
        wkl_ = es6.enter_context(nc.sbuf_tensor("p6_wkl", [128, NF * D], BF16))
        wvh_ = es6.enter_context(nc.sbuf_tensor("p6_wvh", [128, NF * D], BF16))
        wvl_ = es6.enter_context(nc.sbuf_tensor("p6_wvl", [128, NF * D], BF16))
        wkh = [wkh_[:, fc * D:(fc + 1) * D] for fc in range(NF)]
        wkl = [wkl_[:, fc * D:(fc + 1) * D] for fc in range(NF)]
        wvh = [wvh_[:, fc * D:(fc + 1) * D] for fc in range(NF)]
        wvl = [wvl_[:, fc * D:(fc + 1) * D] for fc in range(NF)]
        with ExitStack() as es6b:
            stage_ = es6b.enter_context(
                nc.sbuf_tensor("p6_stage", [128, NF * D], F32))
            stage = [stage_[:, fc * D:(fc + 1) * D] for fc in range(NF)]
            rk = es6b.enter_context(nc.sbuf_tensor("p6_rk", [32, 2 * S], F32))
            rr = es6b.enter_context(nc.sbuf_tensor("p6_rr", [32, 4 * S], F32))
            io32 = es6b.enter_context(nc.sbuf_tensor("p6_io", [32, 2], F32))
            ep = EP(cx)
            ep.op("sync", _dma(rk[:, 0:S], d["tbl_ropek"][0]), LD, 16)
            ep.op("sync", _dma(rk[:, S:2 * S], d["tbl_ropek"][1]), LD, 16)
            ep.op("sync", _dma(rr[:, 0:S], d["tbl_pos32"]), LD, 16)
            ep.op("sync", _dma(io32[:, 0:1], d["tbl_iota32"]), LD, 16)
            ep.op("sync", _dma(ident[:], d["tbl_ident"]), LD, 16)
            for o in range(4):
                ep.op("sync", _dma(cmsk[:, o * TT:(o + 1) * TT],
                                   d["tbl_cmsk"][o]), LD, 16)
            for fc in range(NF):
                ep.op("sync", _dma(bcols[:, fc:fc + 1], d["bias1_col"][fc]),
                      LD, 16)
            ep.w("scalar", LD)
            ep.op("scalar", (lambda e: e.activation(
                out=io32[:, 1:2], in_=io32[:, 0:1], func=ACT.Exp,
                scale=-math.log(10000.0) / 32.0)), SE)
            ep.w("vector", SE)
            ep.w("vector", LD)
            ep.op("vector", (lambda e: e.tensor_scalar(
                out=rr[:, S:2 * S], in0=rr[:, 0:S], scalar1=io32[:, 1:2],
                scalar2=0.0, op0=ALU.mult, op1=ALU.add)), VE)
            for i, (koff, extra) in enumerate(((0, None), (S, PI / 2))):
                ep.w("vector", SE)
                ep.op("vector", (lambda e, ko=koff: e.tensor_scalar(
                    out=rr[:, 2 * S:3 * S], in0=rk[:, ko:ko + S],
                    scalar1=-2 * PI, scalar2=None, op0=ALU.mult)))
                ep.op("vector", (lambda e: e.tensor_tensor(
                    out=rr[:, 2 * S:3 * S], in0=rr[:, 2 * S:3 * S],
                    in1=rr[:, S:2 * S], op=ALU.add)))
                if extra is None:
                    ep.op("vector", (lambda e: e.tensor_scalar(
                        out=rr[:, 2 * S:3 * S], in0=rr[:, 2 * S:3 * S],
                        scalar1=CLP, scalar2=-CLP, op0=ALU.min, op1=ALU.max)), VE)
                else:
                    ep.op("vector", (lambda e, x=extra: e.tensor_scalar(
                        out=rr[:, 2 * S:3 * S], in0=rr[:, 2 * S:3 * S],
                        scalar1=x, scalar2=CLP, op0=ALU.add, op1=ALU.min)))
                    ep.op("vector", (lambda e: e.tensor_scalar(
                        out=rr[:, 2 * S:3 * S], in0=rr[:, 2 * S:3 * S],
                        scalar1=-CLP, scalar2=None, op0=ALU.max)), VE)
                ep.w("scalar", VE)
                ep.op("scalar", (lambda e: e.activation(
                    out=rr[:, 3 * S:4 * S], in_=rr[:, 2 * S:3 * S],
                    func=ACT.Sin)))
                if i == 0:
                    ep.op("scalar", (lambda e: e.activation(
                        out=RS[0:32, :], in_=rr[:, 3 * S:4 * S], func=ACT.Copy,
                        scale=-RINV)))
                    ep.op("scalar", (lambda e: e.activation(
                        out=RS[32:64, :], in_=rr[:, 3 * S:4 * S], func=ACT.Copy,
                        scale=RINV)), SE)
                else:
                    ep.op("scalar", (lambda e: e.activation(
                        out=RC[0:32, :], in_=rr[:, 3 * S:4 * S], func=ACT.Copy,
                        scale=RINV)))
                    ep.op("scalar", (lambda e: e.activation(
                        out=RC[32:64, :], in_=rr[:, 3 * S:4 * S], func=ACT.Copy,
                        scale=RINV)), SE)
            ep.op("vector", (lambda e: e.memset(RC[64:128, :], RINV)), VE)
            if P6_DBG:
                ep.w("gpsimd", SE)
                ep.w("gpsimd", VE)
                ep.op("gpsimd", _dma(d["dbg"][:, 0:S], RC[:]), ST, 16)
                ep.op("gpsimd", _dma(d["dbg"][0:64, S:2 * S], RS[:]), ST, 16)
                ep.w("gpsimd", cx.sem("ST"))
            _emit_weight_pair(ep, cx, d["wk1"], stage, wkh, wkl)
            _emit_weight_pair(ep, cx, d["wv1"], stage, wvh, wvl)
            ep.w("sync", VE)
            for fc in range(NF):
                ep.op("sync", _dma(stage[fc], d["wf1"][fc]), LD, 16)
            ep.w("scalar", LD)
            for fc in range(NF):
                ep.op("scalar", (lambda e, o=w8[fc], i=stage[fc]: e.copy(
                    out=o, in_=i)), SE)
            ep.run("p6w")

        with ExitStack() as es6c:
            xn_ = es6c.enter_context(
                nc.sbuf_tensor("p6_xn", [128, NF * TT * 4], BF16))
            qsb = es6c.enter_context(nc.sbuf_tensor("p6_qsb", [128, 2048], F32))
            qsw = es6c.enter_context(nc.sbuf_tensor("p6_qsw", [64, 2048], F32))
            t1s = es6c.enter_context(nc.sbuf_tensor("p6_t1", [128, TT], F32))
            t2s = es6c.enter_context(nc.sbuf_tensor("p6_t2", [64, TT], F32))
            qf = es6c.enter_context(nc.sbuf_tensor("p6_qf", [128, TT], F32))
            ohi = es6c.enter_context(nc.sbuf_tensor("p6_oh", [128, 2048], BF16))
            olo = es6c.enter_context(nc.sbuf_tensor("p6_ol", [128, 2048], BF16))
            stA = es6c.enter_context(nc.sbuf_tensor("p6_sa", [128, H * 132], BF16))
            stB = es6c.enter_context(nc.sbuf_tensor("p6_sb", [128, H * 128], BF16))
            ps0 = es6c.enter_context(nc.psum_tensor("p6_ps0", [128, TT], F32))
            ps1 = es6c.enter_context(nc.psum_tensor("p6_ps1", [128, TT], F32))
            ps2 = es6c.enter_context(nc.psum_tensor("p6_ps2", [128, TT], F32))
            ps3 = es6c.enter_context(nc.psum_tensor("p6_ps3", [128, TT], F32))
            xnh = [[xn_[:, ((sl * 2 + 0) * NF + fc) * TT:
                        ((sl * 2 + 0) * NF + fc + 1) * TT]
                    for fc in range(NF)] for sl in range(2)]
            xnl = [[xn_[:, ((sl * 2 + 1) * NF + fc) * TT:
                        ((sl * 2 + 1) * NF + fc + 1) * TT]
                    for fc in range(NF)] for sl in range(2)]
            psums = [ps0, ps1]
            ep = EP(cx)
            if P6_DBG:
                ep.op("gpsimd", _dma(d["dbg"][:, 0:S], RC[:]), ST, 16)
                ep.op("gpsimd", _dma(d["dbg"][0:64, S:2 * S], RS[:]), ST, 16)
                ep.w("gpsimd", ST)
            ep.op("vector", lambda e: e.memset(stA[:], 1.0), VE)
            bank_free = [0, 0]
            rel = []
            st_hist = []
            HG = 4
            for s, tt in product(range(BPC), range(NQB)):
                ti = s * NQB + tt
                sl = ti % 2
                tok = slice(s * S + tt * TT, s * S + (tt + 1) * TT)
                ttok = slice(tt * TT, (tt + 1) * TT)
                if ti >= 2:
                    ep.w("sync", SE, rel[ti - 2])
                for fc in range(NF):
                    ep.op("sync", _dma(xnh[sl][fc], d["xn1h"][fc, :, tok]),
                          LD, 16)
                    ep.op("sync", _dma(xnl[sl][fc], d["xn1l"][fc, :, tok]),
                          LD, 16)
                ld_done = LD.v

                def _khg(hg):
                    ep.w("tensor", LD, ld_done)
                    for hh in range(HG):
                        h = hg * HG + hh
                        bk = hh % 2
                        pp = psums[bk]
                        ocol = slice(h * 128, (h + 1) * 128)
                        if bank_free[bk] > 0:
                            ep.w("tensor", SE, bank_free[bk])
                        for fc in range(NF):
                            _mm3(ep, cx, pp[:], wkh[fc][:, ocol],
                                 wkl[fc][:, ocol], xnh[sl][fc], xnl[sl][fc],
                                 start=(fc == 0), stop=(fc == NF - 1), inc=TE)
                        ep.w("scalar", TE)
                        if hh == 0:
                            ep.w("scalar", VE)
                        ep.op("scalar", (lambda e, o=qsb[:, hh * TT:(hh + 1) * TT],
                                         i=pp[:]: e.copy(out=o, in_=i)), SE)
                        bank_free[bk] = SE.v
                    ep.op("scalar", _dma(qsw[0:32, :], qsb[32:64, :]), LD2, 16)
                    ep.op("scalar", _dma(qsw[32:64, :], qsb[0:32, :]), LD2, 16)
                    ep.w("vector", SE)
                    ep.w("vector", LD2)
                    if st_hist:
                        ep.w("vector", ST, st_hist[-1])
                    if P6_NOROT:
                        for hh in range(HG):
                            hc = slice(hh * TT, (hh + 1) * TT)
                            ep.op("vector", (lambda e, o=ohi[:, hc], i=qsb[:, hc]:
                                             e.tensor_copy(out=o, in_=i)))
                            ep.op("vector", (lambda e, o=olo[:, hc], a=qsb[:, hc],
                                             b=ohi[:, hc]: e.tensor_tensor(
                                out=o, in0=a, in1=b, op=ALU.subtract)), VE)
                    elif P6_ROTMODE == 1:
                        for hh in range(HG):
                            hc = slice(hh * TT, (hh + 1) * TT)
                            ep.op("vector", (lambda e, i=qsb[:, hc]:
                                             e.tensor_tensor(out=t1s[:], in0=i,
                                                             in1=RC[:, ttok],
                                                             op=ALU.mult)))
                            ep.op("vector", (lambda e, o=ohi[:, hc]:
                                             e.tensor_copy(out=o, in_=t1s[:])))
                            ep.op("vector", (lambda e, o=olo[:, hc], b=ohi[:, hc]:
                                             e.tensor_tensor(out=o, in0=t1s[:],
                                                             in1=b,
                                                             op=ALU.subtract)), VE)
                    elif P6_ROTMODE == 2:
                        for hh in range(HG):
                            hc = slice(hh * TT, (hh + 1) * TT)
                            ep.op("vector", (lambda e, i=qsw[:, hc]:
                                             e.tensor_tensor(out=t2s[:], in0=i,
                                                             in1=RS[:, ttok],
                                                             op=ALU.mult)))
                            ep.op("vector", (lambda e, o=ohi[:, hc]:
                                             e.memset(o, 0.0)))
                            ep.op("vector", (lambda e, o=ohi[0:64, hc]:
                                             e.tensor_copy(out=o, in_=t2s[:])))
                            ep.op("vector", (lambda e, o=olo[:, hc], b=ohi[:, hc]:
                                             e.tensor_tensor(out=o, in0=b, in1=b,
                                                             op=ALU.subtract)), VE)
                    else:
                      for hh in range(HG):
                        h = hg * HG + hh
                        hc = slice(hh * TT, (hh + 1) * TT)
                        ep.op("vector", (lambda e, i=qsb[:, hc], tb=RC[:, ttok]:
                                         e.tensor_tensor(
                            out=t1s[:], in0=i, in1=tb, op=ALU.mult)))
                        ep.op("vector", (lambda e, i=qsw[:, hc], tb=RS[:, ttok]:
                                         e.tensor_tensor(
                            out=t2s[:], in0=i, in1=tb, op=ALU.mult)))
                        ep.op("vector", (lambda e: e.tensor_tensor(
                            out=qf[0:64, :], in0=t1s[0:64, :], in1=t2s[:],
                            op=ALU.add)))
                        ep.op("vector", (lambda e: e.tensor_copy(
                            out=qf[64:128, :], in_=t1s[64:128, :])))
                        ep.op("vector", (lambda e, o=ohi[:, hc]:
                                         e.tensor_copy(out=o, in_=qf[:])))
                        ep.op("vector", (lambda e, o=olo[:, hc], b=ohi[:, hc]:
                                         e.tensor_tensor(out=o, in0=qf[:], in1=b,
                                                         op=ALU.subtract)), VE)
                    ep.w("gpsimd", VE)
                    for hh in range(HG):
                        h = hg * HG + hh
                        hc = slice(hh * TT, (hh + 1) * TT)
                        ep.op("gpsimd", _dma(d["k1h"][s, h, :, ttok], ohi[:, hc]),
                              ST, 16)
                        ep.op("gpsimd", _dma(d["k1l"][s, h, :, ttok], olo[:, hc]),
                              ST, 16)
                    st_hist.append(ST.v)

                for hg in range(2):
                    _khg(hg)

                def _vch(cc):
                    c = tt * 4 + cc
                    cs = slice(cc * CH, (cc + 1) * CH)
                    ep.w("tensor", LD, ld_done)
                    ep.w("tensor", SE)
                    ep.w("tensor", VE)
                    for ih in range(2):
                        half = (slice(0, TT), slice(TT, D))[ih]
                        pp = (ps2, ps3)[ih]
                        for fc in range(NF):
                            _mm3(ep, cx, pp[:], xnh[sl][fc][:, cs],
                                 xnl[sl][fc][:, cs],
                                 wvh[fc][:, half], wvl[fc][:, half],
                                 start=(fc == 0), stop=(fc == NF - 1), inc=TE)
                    ep.w("scalar", TE)
                    ep.w("scalar", ST)
                    for h in range(H):
                        pp = (ps2, ps3)[h // 4]
                        pcol = slice((h % 4) * 128, (h % 4 + 1) * 128)
                        ep.op("scalar", (lambda e, o=stA[:, h * 132:h * 132 + 128],
                                         i=pp[:, pcol]: e.copy(out=o, in_=i)), SE)
                    ep.w("vector", SE)
                    ep.w("vector", ST)
                    for h in range(H):
                        pp = (ps2, ps3)[h // 4]
                        pcol = slice((h % 4) * 128, (h % 4 + 1) * 128)
                        ep.op("vector", (lambda e, o=stB[:, h * 128:(h + 1) * 128],
                                         i=pp[:, pcol],
                                         b=stA[:, h * 132:h * 132 + 128]:
                                         e.tensor_tensor(out=o, in0=i, in1=b,
                                                         op=ALU.subtract)), VE)
                    ep.w("gpsimd", SE)
                    ep.w("gpsimd", VE)
                    ep.op("gpsimd", _dma(d["v1a"][s, :, :, c, :], stA[:]), ST, 16)
                    ep.op("gpsimd", _dma(d["v1b"][s, :, :, c, :], stB[:]), ST, 16)

                for cc in range(4):
                    _vch(cc)
                rel.append(SE.v)
            ep.w("gpsimd", ST)
            ep.run("p6")


def _p7_attn1(cx):
    nc = cx.nc
    d = cx.d
    LD, LD2, ST, TE, VE, SE = (cx.sem(n) for n in
                               ("LD", "LD2", "ST", "TE", "VE", "SE"))
    p = cx.p67
    RC, RS, w8, ident, bcols, cmsk = (p["RC"], p["RS"], p["w8"], p["ident"],
                                      p["bcols"], p["cmsk"])
    with ExitStack() as es7:
        wqh_ = es7.enter_context(nc.sbuf_tensor("p7_wqh", [128, NF * D], BF16))
        wql_ = es7.enter_context(nc.sbuf_tensor("p7_wql", [128, NF * D], BF16))
        woh_ = es7.enter_context(nc.sbuf_tensor("p7_woh", [128, NF * D], BF16))
        wol_ = es7.enter_context(nc.sbuf_tensor("p7_wol", [128, NF * D], BF16))
        wqh = [wqh_[:, fc * D:(fc + 1) * D] for fc in range(NF)]
        wql = [wql_[:, fc * D:(fc + 1) * D] for fc in range(NF)]
        woh = [woh_[:, fc * D:(fc + 1) * D] for fc in range(NF)]
        wol = [wol_[:, fc * D:(fc + 1) * D] for fc in range(NF)]
        with ExitStack() as es7b:
            stage_ = es7b.enter_context(
                nc.sbuf_tensor("p7_stage", [128, NF * D], F32))
            stage = [stage_[:, fc * D:(fc + 1) * D] for fc in range(NF)]
            ep = EP(cx)
            _emit_weight_pair(ep, cx, d["wq1"], stage, wqh, wql)
            _emit_weight_pair(ep, cx, d["wo1"], stage, woh, wol)
            ep.run("p7w")
        with ExitStack() as es:
            xn_ = es.enter_context(nc.sbuf_tensor("p7_xn", [128, NF * TT * 2], BF16))
            x1b = es.enter_context(nc.sbuf_tensor("p7_x1", [128, 2 * TT], F32))
            qsb = es.enter_context(nc.sbuf_tensor("p7_qsb", [128, 2048], F32))
            qsw = es.enter_context(nc.sbuf_tensor("p7_qsw", [64, 2048], F32))
            t1s = es.enter_context(nc.sbuf_tensor("p7_t1", [128, TT], F32))
            t2s = es.enter_context(nc.sbuf_tensor("p7_t2", [64, TT], F32))
            qf = es.enter_context(nc.sbuf_tensor("p7_qf", [128, TT], F32))
            qph = es.enter_context(nc.sbuf_tensor("p7_qph", [128, H * TT], BF16))
            qpl = es.enter_context(nc.sbuf_tensor("p7_qpl", [128, H * TT], BF16))
            x8_ = es.enter_context(nc.sbuf_tensor("p7_x8", [128, NF * TT], I8))
            xq32 = es.enter_context(nc.sbuf_tensor("p7_xq", [128, TT], F32))
            kp = es.enter_context(nc.sbuf_tensor("p7_kp", [128, S * 2], BF16))
            va = es.enter_context(nc.sbuf_tensor("p7_va", [128, NCH * 132], BF16))
            vb = es.enter_context(nc.sbuf_tensor("p7_vb", [128, NCH * 128], BF16))
            p32_ = es.enter_context(nc.sbuf_tensor("p7_p32", [128, TT * 2], F32))
            phb_ = es.enter_context(nc.sbuf_tensor("p7_ph", [128, TT * 3], BF16))
            plb_ = es.enter_context(nc.sbuf_tensor("p7_pl", [128, TT * 3], BF16))
            rcol = es.enter_context(nc.sbuf_tensor("p7_rc", [128, 8], F32))
            oht = es.enter_context(nc.sbuf_tensor("p7_oht", [128, 128], F32))
            osb = es.enter_context(nc.sbuf_tensor("p7_osb", [128, 132], F32))
            ofh = es.enter_context(nc.sbuf_tensor("p7_ofh", [128, H * TT], BF16))
            ofl = es.enter_context(nc.sbuf_tensor("p7_ofl", [128, H * TT], BF16))
            f16_ = es.enter_context(nc.sbuf_tensor("p7_f16", [128, TT], I16))
            outs = es.enter_context(nc.sbuf_tensor("p7_out", [128, TT], F32))
            pS0 = es.enter_context(nc.psum_tensor("p7_pS0", [128, TT], F32))
            pS1 = es.enter_context(nc.psum_tensor("p7_pS1", [128, TT], F32))
            pS2 = es.enter_context(nc.psum_tensor("p7_pS2", [128, TT], F32))
            pOa = es.enter_context(nc.psum_tensor("p7_pOa", [128, 512], F32))
            pOb = es.enter_context(nc.psum_tensor("p7_pOb", [128, 512], F32))
            pT = es.enter_context(nc.psum_tensor("p7_pT", [128, 128], F32))
            pW = es.enter_context(nc.psum_tensor("p7_pW", [128, TT], F32))
            pF = es.enter_context(nc.psum_tensor("p7_pF", [128, TT], F32))
            xnh = [xn_[:, (2 * fc) * TT:(2 * fc + 1) * TT] for fc in range(NF)]
            xnl = [xn_[:, (2 * fc + 1) * TT:(2 * fc + 2) * TT] for fc in range(NF)]
            x8 = [x8_.bitcast(F8)[:, fc * TT:(fc + 1) * TT] for fc in range(NF)]
            f16 = f16_.bitcast(F16)
            phb = [phb_[:, i * TT:(i + 1) * TT] for i in range(3)]
            plb = [plb_[:, i * TT:(i + 1) * TT] for i in range(3)]
            p32b = [p32_[:, i * TT:(i + 1) * TT] for i in range(2)]
            pSb = [pS0, pS1, pS2]
            ep = EP(cx)
            HG = 4
            te_fin = [0]
            qg_se = [0, 0]
            kvw = []
            prev_xn = [0, 0]
            for s, qb in product(range(BPC), range(NQB)):
                nk = 4 * (qb + 1)
                tok = slice(s * S + qb * TT, s * S + (qb + 1) * TT)
                ttok = slice(qb * TT, (qb + 1) * TT)
                ep.w("sync", TE, prev_xn[0])
                ep.w("sync", VE, prev_xn[1])
                for fc in range(NF):
                    ep.op("sync", _dma(xnh[fc], d["xn1h"][fc, :, tok]), LD, 16)
                    ep.op("sync", _dma(xnl[fc], d["xn1l"][fc, :, tok]), LD, 16)
                ld_done = LD.v

                def _qhg(hg):
                    ep.w("tensor", LD, ld_done)
                    for hh in range(HG):
                        h = hg * HG + hh
                        ocol = slice(h * 128, (h + 1) * 128)
                        pq = (pW, pF)[hh % 2]
                        if hg == 0 and hh < 2:
                            ep.w("tensor", SE)
                            ep.w("tensor", VE)
                        elif qg_se[hh % 2]:
                            ep.w("tensor", SE, qg_se[hh % 2])
                        for fc in range(NF):
                            _mm3(ep, cx, pq[:], wqh[fc][:, ocol],
                                 wql[fc][:, ocol], xnh[fc], xnl[fc],
                                 start=(fc == 0), stop=(fc == NF - 1), inc=TE)
                        ep.w("scalar", TE)
                        if hh == 0:
                            ep.w("scalar", VE)
                        ep.op("scalar", (lambda e, o=qsb[:, hh * TT:(hh + 1) * TT],
                                         i=pq[:]: e.copy(out=o, in_=i)), SE)
                        qg_se[hh % 2] = SE.v
                    ep.op("scalar", _dma(qsw[0:32, :], qsb[32:64, :]), LD2, 16)
                    ep.op("scalar", _dma(qsw[32:64, :], qsb[0:32, :]), LD2, 16)
                    ep.w("vector", SE)
                    ep.w("vector", LD2)
                    ep.w("vector", TE, te_fin[0])
                    for hh in range(HG):
                        h = hg * HG + hh
                        hc = slice(hh * TT, (hh + 1) * TT)
                        qc = slice(h * TT, (h + 1) * TT)
                        ep.op("vector", (lambda e, i=qsb[:, hc], tb=RC[:, ttok]:
                                         e.tensor_tensor(
                            out=t1s[:], in0=i, in1=tb, op=ALU.mult)))
                        ep.op("vector", (lambda e, i=qsw[:, hc], tb=RS[:, ttok]:
                                         e.tensor_tensor(
                            out=t2s[:], in0=i, in1=tb, op=ALU.mult)))
                        ep.op("vector", (lambda e: e.tensor_tensor(
                            out=qf[0:64, :], in0=t1s[0:64, :], in1=t2s[:],
                            op=ALU.add)))
                        ep.op("vector", (lambda e: e.tensor_copy(
                            out=qf[64:128, :], in_=t1s[64:128, :])))
                        ep.op("vector", (lambda e, o=qph[:, qc]:
                                         e.tensor_copy(out=o, in_=qf[:])))
                        ep.op("vector", (lambda e, o=qpl[:, qc], b=qph[:, qc]:
                                         e.tensor_tensor(out=o, in0=qf[:], in1=b,
                                                         op=ALU.subtract)), VE)

                for hg in range(2):
                    _qhg(hg)
                for fc in range(NF):
                    if fc > 0:
                        ep.w("vector", SE)
                    ep.op("vector", (lambda e, a=xnh[fc], b=xnl[fc]:
                                     e.tensor_tensor(out=xq32[:], in0=a, in1=b,
                                                     op=ALU.add)), VE)
                    ep.w("scalar", VE)
                    ep.op("scalar", (lambda e, o=x8[fc]: e.copy(
                        out=o, in_=xq32[:])), SE)
                q_done = VE.v
                x8_done = SE.v
                prev_xn[0] = TE.v
                prev_xn[1] = VE.v

                def _flash(h):
                    qc = slice(h * TT, (h + 1) * TT)
                    ep.w("vector", SE, x8_done)
                    if kvw:
                        ep.w("sync", TE, kvw[-1][0])
                        ep.w("sync", VE, kvw[-1][1])
                    else:
                        ep.w("sync", TE)
                        ep.w("sync", VE)
                    ep.op("sync", _dma(kp[:, 0:nk * CH],
                                       d["k1h"][s, h, :, 0:nk * CH]), LD, 16)
                    ep.op("sync", _dma(kp[:, S:S + nk * CH],
                                       d["k1l"][s, h, :, 0:nk * CH]), LD, 16)
                    ep.op("sync", _dma(va[:, 0:nk * 132],
                                       d["v1a"][s, :, h, 0:nk, :]), LD, 16)
                    ep.op("sync", _dma(vb[:, 0:nk * CH],
                                       d["v1b"][s, :, h, 0:nk, :]), LD, 16)
                    ldh = LD.v
                    ep.w("tensor", LD, ldh)
                    ep.w("tensor", VE)
                    first_a = [True]
                    first_b = [True]
                    pvals = {}
                    tvals = {}

                    def _st(kt):
                        pS = pSb[kt % 3]
                        if kt >= 3:
                            ep.w("tensor", VE, pvals[kt - 3])
                        kc = slice(kt * CH, (kt + 1) * CH)
                        _mm3(ep, cx, pS[:], kp[:, kc],
                             kp[:, S + kt * CH:S + (kt + 1) * CH],
                             qph[:, qc], qpl[:, qc], start=True, stop=True,
                             inc=TE)

                    def _pb(kt):
                        pS = pSb[kt % 3]
                        sl = kt % 3
                        s2 = kt % 2
                        if kt >= 3:
                            ep.w("vector", TE, tvals[kt - 3])
                        if kt >= 2:
                            ep.w("scalar", VE, pvals[kt - 2])
                        o = kt - 4 * qb
                        if o >= 0:
                            ep.w("vector", TE)
                            ep.op("vector", (lambda e, i=pS[:],
                                             m=cmsk[:, o * TT:(o + 1) * TT]:
                                             e.tensor_tensor(out=xq32[:], in0=i,
                                                             in1=m,
                                                             op=ALU.subtract)), VE)
                            ep.w("scalar", VE)
                            ep.op("scalar", (lambda e, o2=p32b[s2]: e.activation(
                                out=o2, in_=xq32[:], func=ACT.Exp)), SE)
                        else:
                            ep.w("scalar", TE)
                            ep.op("scalar", (lambda e, o2=p32b[s2], i=pS[:]:
                                             e.activation(out=o2, in_=i,
                                                          func=ACT.Exp)), SE)
                        ep.w("vector", SE)
                        ep.op("vector", (lambda e, o2=phb[sl], i=p32b[s2]:
                                         e.tensor_copy(out=o2, in_=i)))
                        ep.op("vector", (lambda e, o2=plb[sl], i=p32b[s2],
                                         b=phb[sl]: e.tensor_tensor(
                            out=o2, in0=i, in1=b, op=ALU.subtract)), VE)
                        pvals[kt] = VE.v

                    def _pv(kt, last_k=False):
                        sl = kt % 3
                        ep.w("tensor", VE, pvals[kt])
                        for qs in range(4):
                            qcc = slice(qs * 128, (qs + 1) * 128)
                            po = (pOa, pOb)[qs // 2]
                            base = (qs % 2) * 132
                            oc = slice(base, base + 129)
                            oc3 = slice(base, base + 128)
                            st = first_a[0] if qs < 2 else first_b[0]
                            if qs < 2:
                                first_a[0] = False
                            else:
                                first_b[0] = False
                            ep.op("tensor", (lambda e, o3=po[:, oc],
                                             l=phb[sl][:, qcc],
                                             r=va[:, kt * 132:kt * 132 + 129],
                                             s0=st: e.matmul(
                                out=o3, lhsT=l, rhs=r, start=s0, stop=False)))
                            ep.op("tensor", (lambda e, o3=po[:, oc],
                                             l=plb[sl][:, qcc],
                                             r=va[:, kt * 132:kt * 132 + 129]:
                                             e.matmul(out=o3, lhsT=l, rhs=r,
                                                      start=False, stop=False)))
                            ep.op("tensor", (lambda e, o3=po[:, oc3],
                                             l=phb[sl][:, qcc],
                                             r=vb[:, kt * CH:(kt + 1) * CH],
                                             s1=(last_k and qs == 3):
                                             e.matmul(out=o3, lhsT=l, rhs=r,
                                                      start=False, stop=s1)),
                                TE if qs == 3 else None)
                        tvals[kt] = TE.v

                    _st(0)
                    _pb(0)
                    _st(1)
                    _pb(1)
                    for kt in range(2, nk):
                        _st(kt)
                        _pb(kt)
                        _pv(kt - 2)
                    _pv(nk - 2)
                    _pv(nk - 1, last_k=True)
                    kvw.append((TE.v, VE.v))
                    if P7_DBG and s == 0 and qb == 0 and h == 0:
                        ep.w("vector", TE)
                        ep.op("vector", (lambda e: e.tensor_copy(
                            out=outs[:, 0:264], in_=pOa[:, 0:264])))
                        ep.op("vector", (lambda e: e.tensor_copy(
                            out=outs[:, 264:528], in_=pOb[:, 0:264])), VE)
                        ep.w("gpsimd", VE)
                        ep.op("gpsimd", _dma(d["dbg"][:, 0:528], outs[:, 0:528]),
                              ST, 16)
                        ep.w("gpsimd", ST)
                    for qs in range(4):
                        po = (pOa, pOb)[qs // 2]
                        base = (qs % 2) * 132
                        ep.w("vector", TE)
                        ep.op("vector", (lambda e, i=po[:, base:base + 132]:
                                         e.tensor_copy(out=osb[:], in_=i)), VE)
                        ep.w("scalar", VE)
                        ep.op("scalar", lambda e: e.nop(), SE)
                        ep.w("vector", SE)
                        ep.op("vector", (lambda e:
                                         e.reciprocal(out=rcol[:, 0:1],
                                                      in_=osb[:, 128:129])), VE)
                        ep.w("scalar", VE)
                        ep.op("scalar", (lambda e:
                                         e.activation(out=oht[:],
                                                      in_=osb[:, 0:128],
                                                      func=ACT.Copy,
                                                      scale=rcol[:, 0:1])), SE)
                        ep.w("vector", SE)
                        ep.op("vector", lambda e: e.nop(), VE)
                        if P7_DBG and s == 0 and qb == 0 and h == 0:
                            ep.w("gpsimd", VE)
                            ep.op("gpsimd", _dma(d["dbg"][:, 528 + qs * 128:
                                                          528 + (qs + 1) * 128],
                                                 oht[:]), ST, 16)
                            ep.op("gpsimd", _dma(d["dbg"][:, 1040 + qs * 8:
                                                          1048 + qs * 8],
                                                 rcol[:]), ST, 16)
                            ep.op("gpsimd", _dma(d["dbg"][:, 1072 + qs * 132:
                                                          1204 + qs * 132],
                                                 osb[:]), ST, 16)
                            ep.w("gpsimd", ST)
                            ep.w("vector", ST)
                        ep.w("tensor", VE)
                        ep.w("tensor", SE)
                        ep.op("tensor", (lambda e: e.transpose(
                            out=pT[:], in_=oht[:], identity=ident[:])), TE)
                        ep.w("scalar", TE)
                        if h == 0 and qs == 0:
                            ep.w("scalar", TE, te_fin[0])
                        ep.op("scalar", (lambda e,
                                         o3=ofh[:, h * TT + qs * 128:
                                                h * TT + (qs + 1) * 128]:
                                         e.copy(out=o3, in_=pT[:])), SE)
                        ep.w("vector", SE)
                        ep.op("vector", (lambda e,
                                         o3=ofl[:, h * TT + qs * 128:
                                                h * TT + (qs + 1) * 128],
                                         b=ofh[:, h * TT + qs * 128:
                                               h * TT + (qs + 1) * 128]:
                                         e.tensor_tensor(out=o3, in0=pT[:], in1=b,
                                                         op=ALU.subtract)), VE)

                for h in range(H):
                    _flash(h)
                of_done = VE.v

                fin_ve = []
                fin_guard = [0, 0]

                def _fin(mc):
                    mcol = slice(mc * 128, (mc + 1) * 128)
                    pa = (pW, pS0)[mc % 2]
                    pb = (pF, pS1)[mc % 2]
                    if mc >= 2:
                        ep.w("sync", VE, fin_ve[mc - 2])
                    ep.op("sync", _dma(x1b[:, (mc % 2) * TT:(mc % 2 + 1) * TT],
                                       d["x1"][mc, :, tok]), LD, 16)
                    ld_fin = LD.v
                    ep.w("tensor", VE, of_done)
                    if mc < 2:
                        ep.w("tensor", SE)
                        ep.w("tensor", VE)
                    else:
                        ep.w("tensor", VE, fin_guard[mc % 2])
                    for fc in range(NF):
                        _mm3(ep, cx, pa[:], woh[fc][:, mcol], wol[fc][:, mcol],
                             ofh[:, fc * TT:(fc + 1) * TT],
                             ofl[:, fc * TT:(fc + 1) * TT],
                             start=(fc == 0), stop=(fc == NF - 1), inc=TE)
                    for fc in range(NF):
                        ep.op("tensor", (lambda e, l=w8[fc][:, mcol], r=x8[fc],
                                         s0=(fc == 0), s1=(fc == NF - 1),
                                         o3=pb[:]:
                                         e.matmul(out=o3, lhsT=l, rhs=r,
                                                  start=s0, stop=s1)),
                              TE if fc == NF - 1 else None)
                    ep.w("scalar", TE)
                    ep.op("scalar", (lambda e, i=pb[:]: e.copy(
                        out=f16[:], in_=i)), SE)
                    ep.w("vector", SE)
                    ep.w("vector", ST)
                    ep.w("vector", LD, ld_fin)
                    ob = outs[:, 0:TT]
                    if P7_MODE == 1:
                        ep.op("vector", (lambda e, o3=ob, b=bcols[:, mc:mc + 1]:
                                         e.tensor_scalar(out=o3, in0=f16[:],
                                                         scalar1=b, scalar2=None,
                                                         op0=ALU.add)))
                    elif P7_MODE == 2:
                        ep.op("vector", (lambda e, o3=ob, i=pa[:]: e.tensor_copy(
                            out=o3, in_=i)))
                    else:
                        ep.op("vector", (lambda e, o3=ob, b=bcols[:, mc:mc + 1],
                                         i=pa[:]:
                                         e.scalar_tensor_tensor(out=o3, in0=i,
                                                                scalar=b, in1=f16[:],
                                                                op0=ALU.add,
                                                                op1=ALU.add)))
                    ep.op("vector", (lambda e, o3=ob,
                                     x=x1b[:, (mc % 2) * TT:(mc % 2 + 1) * TT]:
                                     e.tensor_tensor(out=o3, in0=o3, in1=x,
                                                     op=ALU.add)), VE)
                    fin_ve.append(VE.v)
                    fin_guard[mc % 2] = VE.v
                    ep.w("gpsimd", VE)
                    ep.op("gpsimd", _dma(d["y_out"][mc, :, tok], ob), ST, 16)

                for mc in range(NF):
                    _fin(mc)
                te_fin[0] = TE.v
            ep.w("gpsimd", ST)
            ep.run("p7")


def _build(cx):
    nc = cx.nc
    d = cx.d
    _p0_norm0(cx)
    _p12_qk0(cx)
    _p34_vg0(cx)
    _pf12_mix(cx)
    _pf3_ffn0(cx)
    _p5_ret0(cx)
    with ExitStack() as _es67:
        nc2 = cx.nc
        cx.p67 = {
            "RC": _es67.enter_context(nc2.sbuf_tensor("p67_rc", [128, S], F32)),
            "RS": _es67.enter_context(nc2.sbuf_tensor("p67_rs", [64, S], F32)),
            "ident": _es67.enter_context(nc2.sbuf_tensor("p67_id", [128, 128], F32)),
            "bcols": _es67.enter_context(nc2.sbuf_tensor("p67_bc", [128, NF], F32)),
            "cmsk": _es67.enter_context(nc2.sbuf_tensor("p67_cm", [128, 4 * TT], F32)),
        }
        _w8i = _es67.enter_context(nc2.sbuf_tensor("p67_w8", [128, NF * D], I8))
        cx.p67["w8"] = [_w8i.bitcast(F8)[:, fc * D:(fc + 1) * D]
                        for fc in range(NF)]
        _p6_kv1(cx)
        _p7_attn1(cx)


def build_nc(dbg_in=(), dbg_out=(), phases=None):
    nc = bass.Bass("TRN2", target_bir_lowering=False)
    cx = Ctx(nc)
    d = cx.d

    def dram(name, shape, dtype=F32):
        if name in dbg_in:
            kind = "ExternalInput"
        elif name in dbg_out:
            kind = "ExternalOutput"
        else:
            kind = "Internal"
        d[name] = nc.dram_tensor(name, list(shape), dtype, kind=kind).ap()

    for name, shape in [
        ("x_in", (NF, 128, T)),
        ("wq0", (NF, 128, D)), ("wk0", (NF, 128, D)), ("wv0", (NF, 128, D)),
        ("wg0", (NF, 128, D)), ("wo0", (NF, 128, D)),
        ("wr0", (NF, 128, D)), ("wk0f", (NF, 128, D)), ("wv0f", (NF, 128, D)),
        ("gnw_row", (1, D)), ("gnb_row", (1, D)),
        ("mur_col", (NF, 128, 1)), ("muk_col", (NF, 128, 1)),
        ("wq1", (NF, 128, D)), ("wk1", (NF, 128, D)), ("wv1", (NF, 128, D)),
        ("wo1", (NF, 128, D)), ("wf1", (NF, 128, D)),
        ("bias1_col", (NF, 128, 1)),
        ("tbl_xpos", (4, 128, S)), ("tbl_dm", (H, 128, S + 384)),
        ("tbl_cmsk", (4, 128, TT)), ("tbl_ropek", (2, 32, S)),
        ("tbl_pos32", (32, S)), ("tbl_iota32", (32, 1)),
        ("tbl_ident", (128, 128)),
    ]:
        d[name] = nc.dram_tensor(name, list(shape), F32, kind="ExternalInput").ap()

    dram("xn0h", (NF, 128, T), BF16)
    dram("xn0l", (NF, 128, T), BF16)
    dram("q0h", (BPC, H, 128, S), BF16)
    dram("q0l", (BPC, H, 128, S), BF16)
    dram("k0h", (BPC, H, 128, S), BF16)
    dram("k0l", (BPC, H, 128, S), BF16)
    dram("v0h", (BPC, H, 128, S), BF16)
    dram("v0l", (BPC, H, 128, S), BF16)
    dram("g0", (BPC, NCH, 128, D), F32)
    dram("r0", (NF, 128, T), F32)
    dram("kah", (NF, 128, T), BF16)
    dram("kal", (NF, 128, T), BF16)
    dram("f0", (NF, 128, T), F32)
    dram("x1", (NF, 128, T), F32)
    dram("xn1h", (NF, 128, T), BF16)
    dram("xn1l", (NF, 128, T), BF16)
    dram("k1h", (BPC, H, 128, S), BF16)
    dram("k1l", (BPC, H, 128, S), BF16)
    dram("v1a", (BPC, 128, H, NCH, 132), BF16)
    dram("v1b", (BPC, 128, H, NCH, 128), BF16)

    d["y_out"] = nc.dram_tensor("y_out", [NF, 128, T], F32,
                                kind="ExternalOutput").ap()
    d["dbg"] = nc.dram_tensor("dbg", [128, 2 * S + 2048], F32,
                              kind="ExternalOutput" if (P6_DBG or P7_DBG) else "Internal").ap()

    for name in ("LD", "LD2", "ST", "TE", "VE", "SE", "GP",
                 "FR0", "FR1", "FR2", "FR3", "FR4", "FR5"):
        h = nc.semaphore(name)
        cx.sems[name] = Sem(h.__enter__())

    _build(cx)
    return nc, cx


def kernel(**inputs):
    tables = _host_tables()
    nc, _ = build_nc()
    in_maps = [_prep_core_inputs(inputs, c, tables) for c in range(NCORE)]
    res = run_bass_kernel_spmd(nc, in_maps, core_ids=list(range(NCORE)))
    outs = []
    for c in range(NCORE):
        y = res.results[c]["y_out"]
        yt = np.transpose(y, (2, 0, 1)).reshape(BPC, S, D)
        outs.append(yt)
    return np.concatenate(outs, 0).astype(np.float32)
